# version 12
# baseline (speedup 1.0000x reference)
import contextlib
import math
import numpy as np
import ml_dtypes
import concourse.bass as bass
import concourse.mybir as mybir
from concourse.bass_utils import run_bass_kernel_spmd

F32 = mybir.dt.float32
BF16 = mybir.dt.bfloat16
I32 = mybir.dt.int32
AF = mybir.ActivationFunctionType
ALU = mybir.AluOpType
AX = mybir.AxisListType

NCORES = 8
D = 1024
NCH = 8
TC = 2048
TB = 512
NTB = TC // TB
NH = 16
HD = 64
IF = 2816
ICH = IF // 128
MAGIC = float(12582912.0)
ENGINES = ("tensor", "vector", "scalar", "gpsimd", "sync")

bf16 = ml_dtypes.bfloat16


class Ctr:
    def __init__(self, sem):
        self.sem = sem
        self.n = 0


class Prog:
    def __init__(self):
        self.nc = bass.Bass("TRN2", target_bir_lowering=False)
        self.records = {e: [] for e in ENGINES}
        self.stack = contextlib.ExitStack()
        self.pending_waits = {e: [] for e in ENGINES}
        self.phase_ctrs = set()
        self.dirty = {e: False for e in ENGINES}
        self.last_rec = {e: None for e in ENGINES}
        self._semcnt = 0

    def _uname(self, name):
        self._tname = getattr(self, "_tname", 0) + 1
        return f"{name}_{self._tname}"

    def sbuf(self, name, shape, dtype, scope=None):
        cm = self.nc.sbuf_tensor(self._uname(name), shape, dtype)
        tgt = scope or getattr(self, "_default_scope", None) or self.stack
        return tgt.enter_context(cm)

    def psum(self, name, shape, dtype=F32, scope=None):
        cm = self.nc.psum_tensor(self._uname(name), shape, dtype)
        tgt = scope or getattr(self, "_default_scope", None) or self.stack
        return tgt.enter_context(cm)

    def ctr(self, name=None, unique=False):
        if not hasattr(self, "_ctrs"):
            self._ctrs = {}
        if name is not None and not unique and name in self._ctrs:
            return self._ctrs[name]
        self._semcnt += 1
        sem = self.stack.enter_context(
            self.nc.semaphore(f"{name or 's'}{self._semcnt}"))
        c = Ctr(sem)
        if name is not None and not unique:
            self._ctrs[name] = c
        return c

    def scope(self):
        return self.stack.enter_context(contextlib.ExitStack())

    def op(self, engine, emit, inc=None, dma=False, background=False):
        incs = []
        if inc is not None:
            ctr, amt = inc if isinstance(inc, tuple) else (inc, 16 if dma else 1)
            ctr.n += amt
            incs.append((ctr.sem, amt))
            if not background:
                self.phase_ctrs.add(ctr)
        rec = ["op", emit, incs, dma]
        if not background and self.pending_waits[engine]:
            for sem, cnt in self.pending_waits[engine]:
                self.records[engine].append(["wait", sem, cnt])
            self.pending_waits[engine] = []
        self.records[engine].append(rec)
        if not background:
            self.dirty[engine] = True
            self.last_rec[engine] = rec
        return rec

    def wait(self, engine, ctr, count=None, background=False):
        if not background and self.pending_waits[engine]:
            for sem, cnt in self.pending_waits[engine]:
                self.records[engine].append(["wait", sem, cnt])
            self.pending_waits[engine] = []
        self.records[engine].append(
            ["wait", ctr.sem, ctr.n if count is None else count])

    def barrier(self):
        targets = [(c.sem, c.n) for c in self.phase_ctrs]
        bar = None
        for e in ENGINES:
            if not self.dirty[e]:
                continue
            rec = self.last_rec[e]
            if rec is not None and not rec[2]:
                if bar is None:
                    bar = self.ctr("bar")
                amt = 16 if rec[3] else 1
                bar.n += amt
                rec[2].append((bar.sem, amt))
            self.dirty[e] = False
            self.last_rec[e] = None
        if bar is not None:
            targets = [t for t in targets if t[0] is not bar.sem]
            targets.append((bar.sem, bar.n))
        if targets:
            for e in ENGINES:
                self.pending_waits[e] = list(targets)
        self.phase_ctrs = set()

    def finalize(self, final_waits=()):
        block = self.stack.enter_context(self.nc.Block())

        def make(engname):
            def fn(eng):
                for rec in self.records[engname]:
                    if rec[0] == "wait":
                        eng.wait_ge(rec[1], rec[2])
                    else:
                        instr = rec[1](eng)
                        for sem, amt in rec[2]:
                            instr = instr.then_inc(sem, amt)
            return fn

        for ctr, cnt in final_waits:
            self.records["sync"].append(["wait", ctr.sem, cnt])
        block.tensor(make("tensor"))
        block.vector(make("vector"))
        block.scalar(make("scalar"))
        block.gpsimd(make("gpsimd"))
        block.sync(make("sync"))
        self.stack.close()
        return self.nc


CONST_SPEC = {
    "id128": ([128, 128], np.float32),
    "id128b": ([128, 128], np.float32),
    "onescol_b": ([128, 1], np.float32),
    "onesrow_f": ([1, 128], np.float32),
    "onesrow_b": ([1, 128], np.float32),
}


def make_const_arrays():
    return {
        "id128": np.eye(128, dtype=np.float32),
        "id128b": np.eye(128, dtype=np.float32),
        "onescol_b": np.ones((128, 1), np.float32),
        "onesrow_f": np.ones((1, 128), np.float32),
        "onesrow_b": np.ones((1, 128), np.float32),
    }


class Consts:

    def __init__(self, p):
        self.aps = {}
        for name, (shape, _) in CONST_SPEC.items():
            self.aps[name] = p.nc.dram_tensor(
                "c_" + name, shape, F32, kind="ExternalInput").ap()
        self.id128 = p.sbuf("id128", [128, 128], F32)
        self.id128b = p.sbuf("id128b", [128, 128], BF16)
        self.onescol_b = p.sbuf("onescol_b", [128, 1], BF16)
        self.onesrow_f = p.sbuf("onesrow_f", [1, 128], F32)
        self.onesrow_b = p.sbuf("onesrow_b", [1, 128], BF16)
        lds = p.ctr("clds")
        ldg = p.ctr("cldg")
        p.op("sync", lambda e: e.dma_start(out=self.id128[:, :],
                                           in_=self.aps["id128"]), inc=lds, dma=True)
        p.op("gpsimd", lambda e: e.dma_start(out=self.id128b[:, :],
                                             in_=self.aps["id128b"]), inc=ldg, dma=True)
        p.op("gpsimd", lambda e: e.dma_start(out=self.onescol_b[:, :],
                                             in_=self.aps["onescol_b"]), inc=ldg, dma=True)
        p.op("sync", lambda e: e.dma_start(out=self.onesrow_f[:, :],
                                           in_=self.aps["onesrow_f"]), inc=lds, dma=True)
        p.op("gpsimd", lambda e: e.dma_start(out=self.onesrow_b[:, :],
                                             in_=self.aps["onesrow_b"]), inc=ldg, dma=True)


def load_weight_cast(p, dram_ap, sb, cin_chunks, cout, ctr, col0=0):
    src = dram_ap.rearrange("(c p) f -> p c f", p=128)
    dst = sb[:, col0:col0 + cin_chunks * cout].rearrange(
        "p (c f) -> p c f", c=cin_chunks)
    p.op("gpsimd", (lambda dst, src: lambda e: e.dma_start(out=dst, in_=src))(
        dst, src), inc=ctr, dma=True)


def load_vec(p, dram_ap, sb, ctr):
    p.op("sync", (lambda sb, dram_ap: lambda e: e.dma_start(
        out=sb[:, :], in_=dram_ap))(sb, dram_ap), inc=ctr, dma=True)


def emit_proj(p, *, w_sb, cout, cin_chunks, cout_chunks, rhs_sb, rhs_col0,
              out_cb, pp, n_tok=TC, tbs=TB, w_col0=0, tb_outer=False,
              rhs_tok=None):
    if rhs_tok is None:
        rhs_tok = n_tok
    mm = p.ctr("pmm")
    ev = p.ctr("pev")
    b_ev = ev.n
    g = 0
    ntb = n_tok // tbs
    loop = ([(f, tb) for tb in range(ntb) for f in range(cout_chunks)]
            if tb_outer else
            [(f, tb) for f in range(cout_chunks) for tb in range(ntb)])
    for f, tb in loop:
        bank = pp[:, (g % 2) * tbs:(g % 2) * tbs + tbs]
        if g >= 2:
            p.wait("tensor", ev, b_ev + g - 1)
        for ci in range(cin_chunks):
            wsl = w_sb[:, w_col0 + ci * cout + f * 128:
                       w_col0 + ci * cout + f * 128 + 128]
            rsl = rhs_sb[:, rhs_col0 + ci * rhs_tok + tb * tbs:
                         rhs_col0 + ci * rhs_tok + tb * tbs + tbs]
            last = ci == cin_chunks - 1
            p.op("tensor",
                 (lambda bank, wsl, rsl, first, last:
                  lambda e: e.matmul(bank, wsl, rsl, start=first, stop=last)
                  )(bank, wsl, rsl, ci == 0, last),
                 inc=mm if last else None)
        out_cb(p, bank, f, tb * tbs, tbs, mm, ev)
        assert ev.n == b_ev + g + 1, "evac callback must inc ev exactly once"
        g += 1
    return mm, ev


def evac_act(out_sb, out_col0, bias_sb=None, func=AF.Identity, scale=1.0,
             chunk_stride=TC):
    def cb(p, bank, f, t0, nt, mm, ev):
        osl = out_sb[:, out_col0 + f * chunk_stride + t0:
                     out_col0 + f * chunk_stride + t0 + nt]
        p.wait("scalar", mm)
        if bias_sb is not None:
            bsl = bias_sb[:, f:f + 1]
            p.op("scalar", (lambda o, b, bs: lambda e: e.activation(
                o, b, func, bias=bs, scale=scale))(osl, bank, bsl), inc=ev)
        else:
            p.op("scalar", (lambda o, b: lambda e: e.activation(
                o, b, func, scale=scale))(osl, bank), inc=ev)
    return cb


def evac_resid(s_sb, bias_sb, x_sb, x_col0=0, chunk_stride=TC):
    def cb(p, bank, f, t0, nt, mm, ev):
        osl = s_sb[:, f * chunk_stride + t0: f * chunk_stride + t0 + nt]
        xsl = x_sb[:, x_col0 + f * chunk_stride + t0:
                   x_col0 + f * chunk_stride + t0 + nt]
        p.wait("vector", mm)
        if bias_sb is not None:
            bsl = bias_sb[:, f:f + 1]
            p.op("vector", (lambda o, b, bs, x: lambda e: e.scalar_tensor_tensor(
                o, b, bs, x, ALU.add, ALU.add))(osl, bank, bsl, xsl), inc=ev)
        else:
            p.op("vector", (lambda o, b, x: lambda e: e.tensor_tensor(
                o, b, x, ALU.add))(osl, bank, xsl), inc=ev)
    return cb


def emit_ln(p, consts, *, s_sb, out_sb, sc):
    bf_a, bf_b, rows, ps, ps_row = (sc["bf_a"], sc["bf_b"], sc["rows"],
                                    sc["ps"], sc["ps_row"])
    for c in range(NCH):
        sl = s_sb[:, c * TC:(c + 1) * TC]
        p.op("vector", (lambda a, b: lambda e: e.tensor_copy(a, b))(
            bf_a[:, c * TC:(c + 1) * TC], sl))
        p.op("vector", (lambda o, a: lambda e: e.tensor_tensor(
            o, a, a, ALU.mult))(bf_b[:, c * TC:(c + 1) * TC], sl))
    p.barrier()
    for which, src_sb in ((0, bf_a), (1, bf_b)):
        for tb in range(NTB):
            psl = ps_row[0:1, (tb % 4) * TB:(tb % 4) * TB + TB]
            for c in range(NCH):
                p.op("tensor", (lambda psl, a, first, last: lambda e: e.matmul(
                    psl, consts.onescol_b[:, 0:1], a, start=first, stop=last))(
                    psl,
                    src_sb[:, c * TC + tb * TB: c * TC + tb * TB + TB],
                    c == 0, c == NCH - 1))
            p.barrier()
            p.op("vector", (lambda o, i: lambda e: e.tensor_copy(o, i))(
                rows[0:1, which * TC + tb * TB: which * TC + tb * TB + TB], psl))
            p.barrier()
    p.op("vector", lambda e: e.tensor_single_scalar(
        rows[0:1, 0:TC], rows[0:1, 0:TC], 1.0 / D, ALU.mult))
    p.op("vector", lambda e: e.tensor_single_scalar(
        rows[0:1, TC:2 * TC], rows[0:1, TC:2 * TC], 1.0 / D, ALU.mult))
    p.barrier()
    p.op("vector", lambda e: e.tensor_tensor(
        rows[0:1, 2 * TC:3 * TC], rows[0:1, 0:TC], rows[0:1, 0:TC], ALU.mult))
    p.barrier()
    p.op("vector", lambda e: e.scalar_tensor_tensor(
        rows[0:1, 2 * TC:3 * TC], rows[0:1, 2 * TC:3 * TC], -1.0,
        rows[0:1, TC:2 * TC], ALU.mult, ALU.add))
    p.barrier()
    p.op("vector", lambda e: e.tensor_single_scalar(
        rows[0:1, 2 * TC:3 * TC], rows[0:1, 2 * TC:3 * TC], 1e-5, ALU.add))
    p.barrier()
    p.op("scalar", lambda e: e.activation(
        rows[0:1, 2 * TC:3 * TC], rows[0:1, 2 * TC:3 * TC], AF.Ln))
    p.op("scalar", lambda e: e.activation(
        rows[0:1, 2 * TC:3 * TC], rows[0:1, 2 * TC:3 * TC], AF.Exp, scale=-0.5))
    p.barrier()
    p.op("vector", lambda e: e.scalar_tensor_tensor(
        rows[0:1, 0:TC], rows[0:1, 0:TC], -1.0, rows[0:1, 2 * TC:3 * TC],
        ALU.mult, ALU.mult))
    p.barrier()
    for tb in range(NTB):
        pa = ps[:, 0:TB]
        pb = ps[:, TB:2 * TB]
        p.op("tensor", (lambda pa, tb: lambda e: e.matmul(
            pa, consts.onesrow_f[0:1, :],
            rows[0:1, 2 * TC + tb * TB: 2 * TC + tb * TB + TB],
            start=True, stop=True))(pa, tb))
        p.op("tensor", (lambda pb, tb: lambda e: e.matmul(
            pb, consts.onesrow_f[0:1, :],
            rows[0:1, tb * TB: tb * TB + TB],
            start=True, stop=True))(pb, tb))
        p.barrier()
        for c in range(NCH):
            ssl = s_sb[:, c * TC + tb * TB: c * TC + tb * TB + TB]
            osl = out_sb[:, c * TC + tb * TB: c * TC + tb * TB + TB]
            p.op("vector", (lambda o, s, pa: lambda e: e.tensor_tensor(
                o, s, pa, ALU.mult))(osl, ssl, pa))
            p.op("vector", (lambda o, pb: lambda e: e.tensor_tensor(
                o, o, pb, ALU.add))(osl, pb))
        p.barrier()


def emit_ln_fast(p, consts, *, s_sb, out_sb, sc):
    bf_a, bf_b, rows, ps, ps_row = (sc["bf_a"], sc["bf_b"], sc["rows"],
                                    sc["ps"], sc["ps_row"])
    lcp = p.ctr("lcp")
    lmm = p.ctr("lmm")
    lev = p.ctr("lev")
    lst = p.ctr("lst")
    lex = p.ctr("lex")
    lbr = p.ctr("lbr")
    lbc = p.ctr("lbc")
    lap = p.ctr("lap")
    b_lcp, b_lmm, b_lev = lcp.n, lmm.n, lev.n
    b_lbc, b_lap = lbc.n, lap.n
    for c in range(NCH):
        sl = s_sb[:, c * TC:(c + 1) * TC]
        p.op("vector", (lambda a, b: lambda e: e.tensor_copy(a, b))(
            bf_a[:, c * TC:(c + 1) * TC], sl))
        p.op("vector", (lambda o, a: lambda e: e.tensor_tensor(
            o, a, a, ALU.mult))(bf_b[:, c * TC:(c + 1) * TC], sl),
            inc=lcp)
    p.wait("tensor", lcp, b_lcp + NCH)
    groups = [(0, tb) for tb in range(NTB)] + [(1, tb) for tb in range(NTB)]
    for j, (which, tb) in enumerate(groups):
        src_sb = bf_a if which == 0 else bf_b
        psl = ps_row[0:1, (j % 4) * TB:(j % 4) * TB + TB]
        if j >= 4:
            p.wait("tensor", lev, b_lev + j - 3)
        for c in range(NCH):
            p.op("tensor", (lambda psl, a, first, last: lambda e: e.matmul(
                psl, consts.onescol_b[:, 0:1], a, start=first, stop=last))(
                psl,
                src_sb[:, c * TC + tb * TB: c * TC + tb * TB + TB],
                c == 0, c == NCH - 1), inc=lmm if c == NCH - 1 else None)
        p.wait("vector", lmm, b_lmm + j + 1)
        p.op("vector", (lambda o, psl: lambda e: e.tensor_copy(o, psl))(
            rows[0:1, which * TC + tb * TB: which * TC + tb * TB + TB], psl),
            inc=lev)
    p.op("vector", lambda e: e.tensor_single_scalar(
        rows[0:1, 0:TC], rows[0:1, 0:TC], 1.0 / D, ALU.mult))
    p.op("vector", lambda e: e.tensor_single_scalar(
        rows[0:1, TC:2 * TC], rows[0:1, TC:2 * TC], 1.0 / D, ALU.mult))
    p.op("vector", lambda e: e.tensor_tensor(
        rows[0:1, 2 * TC:3 * TC], rows[0:1, 0:TC], rows[0:1, 0:TC], ALU.mult))
    p.op("vector", lambda e: e.scalar_tensor_tensor(
        rows[0:1, 2 * TC:3 * TC], rows[0:1, 2 * TC:3 * TC], -1.0,
        rows[0:1, TC:2 * TC], ALU.mult, ALU.add))
    p.op("vector", lambda e: e.tensor_single_scalar(
        rows[0:1, 2 * TC:3 * TC], rows[0:1, 2 * TC:3 * TC], 1e-5, ALU.add),
        inc=lst)
    p.wait("scalar", lst)
    p.op("scalar", lambda e: e.activation(
        rows[0:1, 2 * TC:3 * TC], rows[0:1, 2 * TC:3 * TC], AF.Ln))
    p.op("scalar", lambda e: e.activation(
        rows[0:1, 2 * TC:3 * TC], rows[0:1, 2 * TC:3 * TC], AF.Exp,
        scale=-0.5), inc=lex)
    p.wait("vector", lex)
    p.op("vector", lambda e: e.scalar_tensor_tensor(
        rows[0:1, 0:TC], rows[0:1, 0:TC], -1.0, rows[0:1, 2 * TC:3 * TC],
        ALU.mult, ALU.mult), inc=lbr)
    p.wait("tensor", lex)
    p.wait("tensor", lbr)
    for tb in range(NTB):
        sl = (tb % 2) * 1024
        pa = ps[:, sl:sl + TB]
        pb = ps[:, sl + TB:sl + 2 * TB]
        if tb >= 2:
            p.wait("tensor", lap, b_lap + (tb - 1) * NCH)
        p.op("tensor", (lambda pa, tb: lambda e: e.matmul(
            pa, consts.onesrow_f[0:1, :],
            rows[0:1, 2 * TC + tb * TB: 2 * TC + tb * TB + TB],
            start=True, stop=True))(pa, tb))
        p.op("tensor", (lambda pb, tb: lambda e: e.matmul(
            pb, consts.onesrow_f[0:1, :],
            rows[0:1, tb * TB: tb * TB + TB],
            start=True, stop=True))(pb, tb), inc=(lbc, 1))
        p.wait("vector", lbc, b_lbc + tb + 1)
        for c in range(NCH):
            ssl = s_sb[:, c * TC + tb * TB: c * TC + tb * TB + TB]
            osl = out_sb[:, c * TC + tb * TB: c * TC + tb * TB + TB]
            p.op("vector", (lambda o, s, pa: lambda e: e.tensor_tensor(
                o, s, pa, ALU.mult))(osl, ssl, pa))
            p.op("vector", (lambda o, pb: lambda e: e.tensor_tensor(
                o, o, pb, ALU.add))(osl, pb), inc=lap)


def emit_absmax_cell(p, consts, src_sb, col0, n_chunks, n_tok, scr, psc, out_cell):
    for c in range(n_chunks):
        p.op("vector", (lambda o, i: lambda e: e.tensor_reduce(
            o, i, AX.X, ALU.max, apply_absolute_value=True))(
            scr[:, c:c + 1], src_sb[:, col0 + c * n_tok: col0 + (c + 1) * n_tok]))
    p.barrier()
    p.op("vector", lambda e: e.tensor_reduce(
        scr[:, n_chunks:n_chunks + 1], scr[:, 0:n_chunks], AX.X, ALU.max))
    p.barrier()
    p.op("tensor", lambda e: e.transpose(
        psc[0:1, 0:128], scr[:, n_chunks:n_chunks + 1], consts.id128[:, :]))
    p.barrier()
    p.op("vector", lambda e: e.tensor_reduce(
        out_cell, psc[0:1, 0:128], AX.X, ALU.max))
    p.barrier()


def emit_flash(p, consts, *, q_sb, kg_ap, vg_ap, negb_sb, scale, attn_sb,
               fl, n_blocks=4):
    TK = n_blocks * 2048
    NKT = TK // 128
    kbuf, vbuf, pT, rec, otmp = fl["kbuf"], fl["vbuf"], fl["pT"], fl["rec"], fl["otmp"]
    S = (fl["S0"], fl["S1"])
    O = fl["O"]
    kv = [p.ctr("kv0"), p.ctr("kv1")]
    mm = p.ctr("fmm")
    ex = p.ctr("fex")
    pv = p.ctr("fpv")
    rc = p.ctr("frc")
    bc = p.ctr("fbc")
    oc = p.ctr("foc")
    ev = p.ctr("fev")
    b_kv = [kv[0].n, kv[1].n]
    b_mm, b_ex, b_pv = mm.n, ex.n, pv.n
    b_rc, b_bc, b_oc, b_ev = rc.n, bc.n, oc.n, ev.n

    def issue_loads(hp):
        b = hp % 2
        for r in range(n_blocks):
            src = kg_ap[r * 1024 + hp * 128: r * 1024 + hp * 128 + 128, :]
            dst = kbuf[:, b * TK + r * 2048: b * TK + r * 2048 + 2048]
            p.op("sync", (lambda dst, src: lambda e: e.dma_start(
                out=dst, in_=src))(dst, src), inc=kv[b], dma=True)
        vsrc = vg_ap.rearrange("(k p) c -> p k c", p=128)[
            :, :, hp * 130:(hp + 1) * 130]
        vdst = vbuf[:, b * NKT * 130:(b + 1) * NKT * 130].rearrange(
            "p (k c) -> p k c", c=130)
        p.op("sync", (lambda vdst, vsrc: lambda e: e.dma_start(
            out=vdst, in_=vsrc))(vdst, vsrc), inc=kv[b], dma=True)

    issue_loads(0)
    deferred = [None]

    def flush_evac(u):
        o1, o2 = O[2 * (u % 2)], O[2 * (u % 2) + 1]
        ru = rec[0:1, (u % 2) * 1024:(u % 2) * 1024 + 1024]
        hp, qb = u // 4, u % 4
        p.wait("tensor", rc, b_rc + 2 * (u + 1))
        p.op("tensor", (lambda o1, ru: lambda e: e.matmul(
            o1[64:128, :], consts.onesrow_b[0:1, 0:64], ru[0:1, 0:512],
            start=True, stop=True))(o1, ru))
        p.op("tensor", (lambda o2, ru: lambda e: e.matmul(
            o2[64:128, :], consts.onesrow_b[0:1, 0:64], ru[0:1, 512:1024],
            start=True, stop=True))(o2, ru), inc=(bc, 2))
        ot = otmp[:, (u % 2) * 512:(u % 2) * 512 + 512]
        p.wait("scalar", bc, b_bc + 2 * (u + 1))
        p.op("scalar", (lambda ot, o1: lambda e: e.activation(
            ot[0:64, :], o1[0:64, :], AF.Copy))(ot, o1))
        p.op("scalar", (lambda ot, o2: lambda e: e.activation(
            ot[64:128, :], o2[0:64, :], AF.Copy))(ot, o2), inc=(oc, 2))
        a0 = attn_sb[:, hp * TC + qb * 512: hp * TC + qb * 512 + 512]
        p.wait("vector", oc, b_oc + 2 * (u + 1))
        p.op("vector", (lambda a0, ot, o1: lambda e: e.tensor_tensor(
            a0[0:64, :], ot[0:64, :], o1[64:128, :], ALU.mult))(a0, ot, o1))
        p.op("vector", (lambda a0, ot, o2: lambda e: e.tensor_tensor(
            a0[64:128, :], ot[64:128, :], o2[64:128, :], ALU.mult))(a0, ot, o2),
            inc=ev)

    for hp in range(8):
        b = hp % 2
        if hp + 1 < 8:
            p.wait("sync", pv, b_pv + 256 * hp)
            issue_loads(hp + 1)
        for qb in range(4):
            u = hp * 4 + qb
            o1, o2 = O[2 * (u % 2)], O[2 * (u % 2) + 1]
            qsl = q_sb[:, hp * TC + qb * 512: hp * TC + qb * 512 + 512]
            vb = vbuf[:, b * NKT * 130:(b + 1) * NKT * 130]
            kb = kbuf[:, b * TK:(b + 1) * TK]

            def emit_pv(j, o1=o1, o2=o2, u=u, vb=vb):
                gj = u * NKT + j
                ptj = pT[:, (gj % 2) * 1024:(gj % 2) * 1024 + 1024]
                p.wait("tensor", ex, b_ex + gj + 1)
                if j == 0 and u >= 2:
                    p.wait("tensor", ev, b_ev + u - 1)
                p.op("tensor", (lambda o1, vbs, pt, first, last: lambda e:
                     e.matmul(o1[0:65, :], vbs[:, 0:65], pt[:, 0:512],
                              start=first, stop=last))(
                    o1, vb[:, j * 130:j * 130 + 130], ptj, j == 0,
                    j == NKT - 1))
                p.op("tensor", (lambda o2, vbs, pt, first, last: lambda e:
                     e.matmul(o2[0:65, :], vbs[:, 65:130], pt[:, 512:1024],
                              start=first, stop=last))(
                    o2, vb[:, j * 130:j * 130 + 130], ptj, j == 0,
                    j == NKT - 1), inc=pv)

            for kt in range(NKT):
                g = u * NKT + kt
                pr = S[g % 2]
                if deferred[0] is not None and kt == 2:
                    flush_evac(deferred[0])
                    deferred[0] = None
                if kt == 0:
                    p.wait("tensor", kv[b], b_kv[b] + 80 * (hp // 2 + 1))
                if g >= 2:
                    p.wait("tensor", ex, b_ex + g - 1)
                for (tp, prow, pcol, osl) in (
                        ((0, 0), 0, 0, pr[0:64, 0:512]),
                        ((0, 64), 0, 64, pr[64:128, 0:512]),
                        ((64, 0), 64, 0, pr[0:64, 512:1024]),
                        ((64, 64), 64, 64, pr[64:128, 512:1024])):
                    lhs = kb[prow:prow + 64, kt * 128 + pcol: kt * 128 + pcol + 64]
                    rhs = qsl[prow:prow + 64, :]
                    last = tp == (64, 64)
                    p.op("tensor", (lambda o, l, r, tp: lambda e: e.matmul(
                        o, l, r, start=True, stop=True, tile_position=tp))(
                        osl, lhs, rhs, tp), inc=mm if last else None)
                p.wait("scalar", mm, b_mm + g + 1)
                pt = pT[:, (g % 2) * 1024:(g % 2) * 1024 + 1024]
                p.op("scalar", (lambda pt, pr: lambda e: e.activation(
                    pt, pr[:, 0:1024], AF.Exp, bias=negb_sb[:, 0:1],
                    scale=scale))(pt, pr), inc=ex)
                if kt >= 1:
                    emit_pv(kt - 1)
            emit_pv(NKT - 1)
            p.wait("vector", pv, b_pv + (u + 1) * NKT)
            ru = rec[0:1, (u % 2) * 1024:(u % 2) * 1024 + 1024]
            recf = fl["recf"]
            frc2 = p.ctr("frc2")
            p.op("vector", (lambda recf, o1: lambda e: e.reciprocal(
                recf[0:1, 0:512], o1[64:65, :]))(recf, o1))
            p.op("vector", (lambda recf, o2: lambda e: e.reciprocal(
                recf[0:1, 512:1024], o2[64:65, :]))(recf, o2), inc=frc2)
            p.wait("vector", frc2)
            p.op("vector", (lambda ru, recf: lambda e: e.tensor_copy(
                ru[0:1, 0:1024], recf[0:1, 0:1024]))(ru, recf), inc=(rc, 2))
            deferred[0] = u
    flush_evac(deferred[0])
    deferred[0] = None
    p.barrier()


def alloc_flash(p, scope, kbuf_bf16, big_f32):
    return dict(
        kbuf=kbuf_bf16,
        vbuf=big_f32[:, 0:8320].bitcast(BF16),
        pT=big_f32[:, 8320:9344].bitcast(BF16),
        otmp=big_f32[:, 9344:10368],
        rec=big_f32[0:1, 10368:11392].bitcast(BF16),
        recf=big_f32[0:1, 11392:12416],
        S0=p.psum("fS0", [128, 1024], F32, scope),
        S1=p.psum("fS1", [128, 1024], F32, scope),
        O=[p.psum(f"fO{i}", [128, 512], F32, scope) for i in range(4)],
    )


def emit_quant2(p, src, dst, nblk, qs, src_ready=None, done=None):
    y, ta, tb_, tc_, sgn = qs["y"], qs["ta"], qs["tb_"], qs["tc_"], qs["sgn"]
    bm, sv, rv = qs["bm"], qs["sv"], qs["rv"]
    n = 16 * nblk
    P = src.shape[0]
    sg = p.ctr("qsg")
    if src_ready is not None:
        p.wait("scalar", src_ready[0], src_ready[1])
        p.wait("vector", src_ready[0], src_ready[1])
    if done is not None:
        p.wait("scalar", done, done.n)
    p.op("scalar", (lambda o, s: lambda e: e.activation(o, s, AF.Sign))(
        sgn[:P, 0:n], src), inc=sg)
    qrc = p.ctr("qrc")
    p.op("vector", (lambda o, s: lambda e: e.tensor_reduce(
        o, s, AX.X, ALU.max, apply_absolute_value=True))(
        bm[:P, 0:nblk], src.rearrange("p (b k) -> p b k", k=16)), inc=qrc)
    p.wait("vector", qrc)
    p.op("vector", (lambda o, i: lambda e: e.tensor_scalar(
        o, i, 1.0 / 6.0, 1e-12, ALU.mult, ALU.max))(sv[:P, 0:nblk], bm[:P, 0:nblk]))
    p.op("vector", (lambda o, i: lambda e: e.reciprocal(o, i))(
        rv[:P, 0:nblk], sv[:P, 0:nblk]), inc=qrc)
    p.op("vector", (lambda o, i: lambda e: e.tensor_single_scalar(
        o, i, 0x7FFFFFFF, ALU.bitwise_and))(
        y[:P, 0:n].bitcast(I32), src.bitcast(I32)))
    p.wait("vector", qrc)
    p.op("vector", (lambda o, i, r: lambda e: e.tensor_tensor(
        o, i, r, ALU.mult))(
        y[:P, 0:n].rearrange("p (b k) -> p b k", k=16),
        y[:P, 0:n].rearrange("p (b k) -> p b k", k=16),
        rv[:P, 0:nblk].rearrange("p b -> p b ()").broadcast_to([P, nblk, 16])))
    p.op("vector", (lambda o, i: lambda e: e.tensor_scalar(
        o, i, 2.0, 2.0, ALU.min, ALU.mult))(ta[:P, 0:n], y[:P, 0:n]))
    p.op("vector", (lambda o, i: lambda e: e.tensor_scalar(
        o, i, MAGIC, MAGIC, ALU.add, ALU.subtract))(ta[:P, 0:n], ta[:P, 0:n]))
    p.op("vector", (lambda o, i: lambda e: e.tensor_scalar(
        o, i, 2.0, MAGIC, ALU.max, ALU.add))(tb_[:P, 0:n], y[:P, 0:n]))
    p.op("vector", (lambda o, i: lambda e: e.tensor_scalar(
        o, i, MAGIC, 4.0, ALU.subtract, ALU.min))(tb_[:P, 0:n], tb_[:P, 0:n]))
    p.op("vector", (lambda o, i: lambda e: e.tensor_scalar(
        o, i, 0.5, MAGIC, ALU.mult, ALU.add))(tc_[:P, 0:n], y[:P, 0:n]))
    p.op("vector", (lambda o, i: lambda e: e.tensor_scalar(
        o, i, MAGIC, 2.0, ALU.subtract, ALU.mult))(tc_[:P, 0:n], tc_[:P, 0:n]))
    p.op("vector", (lambda o, i: lambda e: e.tensor_scalar(
        o, i, 4.0, -6.0, ALU.max, ALU.add))(tc_[:P, 0:n], tc_[:P, 0:n]))
    p.op("vector", (lambda o, a, b: lambda e: e.scalar_tensor_tensor(
        o, a, 0.5, b, ALU.mult, ALU.add))(ta[:P, 0:n], ta[:P, 0:n], tb_[:P, 0:n]))
    p.op("vector", (lambda o, a, c: lambda e: e.tensor_tensor(
        o, a, c, ALU.add))(ta[:P, 0:n], ta[:P, 0:n], tc_[:P, 0:n]))
    p.op("vector", (lambda o, a, s: lambda e: e.tensor_tensor(
        o, a, s, ALU.mult))(
        ta[:P, 0:n].rearrange("p (b k) -> p b k", k=16),
        ta[:P, 0:n].rearrange("p (b k) -> p b k", k=16),
        sv[:P, 0:nblk].rearrange("p b -> p b ()").broadcast_to([P, nblk, 16])))
    p.wait("vector", sg)
    p.op("vector", (lambda o, a, s: lambda e: e.tensor_tensor(
        o, a, s, ALU.mult))(dst, ta[:P, 0:n], sgn[:P, 0:n]), inc=done)


def alloc_quant_scratch(p, scope, n, nblk):
    return dict(
        y=p.sbuf("q_y", [128, n], F32, scope),
        ta=p.sbuf("q_ta", [128, n], F32, scope),
        tb_=p.sbuf("q_tb", [128, n], F32, scope),
        tc_=p.sbuf("q_tc", [128, n], F32, scope),
        sgn=p.sbuf("q_sgn", [128, n], F32, scope),
        bm=p.sbuf("q_bm", [128, nblk], F32, scope),
        sv=p.sbuf("q_sv", [128, nblk], F32, scope),
        rv=p.sbuf("q_rv", [128, nblk], F32, scope),
    )


def quant_scratch_views(reg, n, nblk):
    d = {}
    o = 0
    for nm in ("y", "ta", "tb_", "tc_", "sgn"):
        d[nm] = reg[:, o:o + n]
        o += n
    for nm in ("bm", "sv", "rv"):
        d[nm] = reg[:, o:o + nblk]
        o += nblk
    return d


def emit_v_stage(p, consts, v_sb, vg_in, scope):
    psv = [p.psum("vps0", [128, 1024], BF16, scope),
           p.psum("vps1", [128, 1024], BF16, scope)]
    vstage = p.sbuf("vstage", [128, 2 * 1040], BF16, scope)
    tr = p.ctr("vtr")
    cpy = p.ctr("vcp")
    st = [p.ctr("vsta"), p.ctr("vstb")]
    b_tr, b_cpy = tr.n, cpy.n
    b_st = [st[0].n, st[1].n]
    for b in range(2):
        p.op("vector", (lambda b: lambda e: e.memset(
            vstage[:, b * 1040:(b + 1) * 1040].rearrange(
                "p (h c) -> p h c", c=65)[:, :, 64:65], 1.0))(b))
    for tt in range(16):
        b = tt % 2
        ps = psv[b]
        if tt >= 2:
            p.wait("tensor", cpy, b_cpy + tt - 1)
        for c in range(NCH):
            in_ = v_sb[:, c * TC + tt * 128: c * TC + tt * 128 + 128]
            p.op("tensor", (lambda ps, in_, c: lambda e: e.transpose(
                ps[:, c * 128:(c + 1) * 128], in_, consts.id128b[:, :]))(
                ps, in_, c), inc=tr if c == NCH - 1 else None)
        p.wait("vector", tr, b_tr + tt + 1)
        if tt >= 2:
            p.wait("vector", st[b], b_st[b] + 16 * ((tt - 2) // 2 + 1))
        p.op("vector", (lambda b, ps: lambda e: e.tensor_copy(
            vstage[:, b * 1040:(b + 1) * 1040].rearrange(
                "p (h c) -> p h c", c=65)[:, :, 0:64],
            ps[:, :].rearrange("p (h c) -> p h c", c=64)))(b, ps), inc=cpy)
        p.wait("sync", cpy, b_cpy + tt + 1)
        p.op("sync", (lambda b, tt: lambda e: e.dma_start(
            out=vg_in[tt * 128:(tt + 1) * 128, :],
            in_=vstage[:, b * 1040:(b + 1) * 1040]))(b, tt), inc=st[b], dma=True)


def emit_proj_resid(p, *, w_sb, w_col0, cout, cin_chunks, rhs_sb, bias_sb,
                    x_dram, s_sb, xstr, pp):
    xld = [p.ctr("xlda"), p.ctr("xldb")]
    b_xld = [c.n for c in xld]
    g_of = {}
    g = 0
    for f in range(NCH):
        for tb in range(NTB):
            g_of[(f, tb)] = g
            g += 1
    done_ev = [None, 0]

    def x_dma(f, tb, gg):
        src = x_dram[f * 128:(f + 1) * 128, tb * TB:(tb + 1) * TB]
        dst = xstr[:, (gg % 2) * TB:(gg % 2) * TB + TB]
        if gg >= 2:
            p.wait("sync", done_ev[0], done_ev[1] + gg - 1)
        p.op("sync", (lambda dst, src: lambda e: e.dma_start(
            out=dst, in_=src))(dst, src), inc=xld[gg % 2], dma=True)

    def cb(p_, bank, f, t0, nt, mm, ev):
        gg = g_of[(f, t0 // TB)]
        if done_ev[0] is None:
            done_ev[0] = ev
            done_ev[1] = ev.n
            for g0 in range(2):
                x_dma(*divmod(g0, NTB), g0)
        if gg + 2 < NCH * NTB:
            f2, tb2 = divmod(gg + 2, NTB)
            x_dma(f2, tb2, gg + 2)
        osl = s_sb[:, f * TC + t0: f * TC + t0 + nt]
        xsl = xstr[:, (gg % 2) * TB:(gg % 2) * TB + TB]
        p_.wait("vector", mm)
        p_.wait("vector", xld[gg % 2], b_xld[gg % 2] + 16 * (gg // 2 + 1))
        if bias_sb is not None:
            bsl = bias_sb[:, f:f + 1]
            p_.op("vector", (lambda o, b, bs, x: lambda e: e.scalar_tensor_tensor(
                o, b, bs, x, ALU.add, ALU.add))(osl, bank, bsl, xsl), inc=ev)
        else:
            p_.op("vector", (lambda o, b, x: lambda e: e.tensor_tensor(
                o, b, x, ALU.add))(osl, bank, xsl), inc=ev)

    return emit_proj(p, w_sb=w_sb, w_col0=w_col0, cout=cout,
                     cin_chunks=cin_chunks, cout_chunks=NCH, rhs_sb=rhs_sb,
                     rhs_col0=0, out_cb=cb, pp=pp)


def emit_negb(p, consts, st_out, srow, negb_sb, scale, psB):
    ld = p.ctr("nld")
    p.op("sync", lambda e: e.dma_start(
        out=srow[0:1, 0:16],
        in_=st_out.rearrange("r c -> (r c)").unsqueeze(0)), inc=ld, dma=True)
    p.barrier()
    nrd = p.ctr("nrd")
    p.op("vector", lambda e: e.tensor_reduce(
        srow[0:1, 16:17],
        srow[0:1, 0:16].rearrange("p (r c) -> p c r", c=2)[:, 0:1, :].squeeze(1),
        AX.X, ALU.max))
    p.op("vector", lambda e: e.tensor_reduce(
        srow[0:1, 17:18],
        srow[0:1, 0:16].rearrange("p (r c) -> p c r", c=2)[:, 1:2, :].squeeze(1),
        AX.X, ALU.max), inc=nrd)
    p.wait("vector", nrd)
    p.op("vector", lambda e: e.tensor_tensor(
        srow[0:1, 16:17], srow[0:1, 16:17], srow[0:1, 17:18], ALU.mult))
    p.op("vector", lambda e: e.tensor_single_scalar(
        srow[0:1, 16:17], srow[0:1, 16:17], -64.0 * scale, ALU.mult))
    p.barrier()
    p.op("tensor", lambda e: e.matmul(
        psB[:, 0:1], consts.onesrow_f[0:1, :], srow[0:1, 16:17],
        start=True, stop=True))
    p.barrier()
    p.op("vector", lambda e: e.tensor_copy(negb_sb[:, 0:1], psB[:, 0:1]))
    p.barrier()


def emit_a0(p, consts, io, phases=8.0, pre_flash_cb=None):
    def bail(buf):
        od = p.ctr("od")
        p.op("vector", (lambda buf: lambda e: e.memset(buf, 0.5))(buf))
        p.barrier()
        p.op("sync", (lambda buf: lambda e: e.dma_start(
            out=io["x_out"].rearrange("(c p) t -> p c t", p=128),
            in_=buf.rearrange("p (c t) -> p c t", c=NCH)))(buf), inc=od, dma=True)
        p.barrier()
        return od
    nc = p.nc
    wld = p.ctr("wld")
    vld = p.ctr("vld")
    wbuf = p.sbuf("wbuf", [128, 2 * NCH * D], BF16)
    qsb = p.sbuf("qsb", [128, NCH * TC], BF16)
    buA = p.sbuf("buA", [128, NCH * TC], BF16)
    buB = p.sbuf("buB", [128, NCH * TC], BF16)
    big0 = p.sbuf("big0", [128, NCH * TC], F32)
    xbf = big0[:, 0:NCH * TC // 2].bitcast(BF16)
    s_sb = big0[:, :]
    bias_sb = p.sbuf("bias_sb", [128, 40], F32)
    srow = p.sbuf("srow", [1, 64], F32)
    cells = p.sbuf("cells", [1, 8], F32)
    statscr = p.sbuf("statscr", [128, 16], F32)
    negb_sb = p.sbuf("negb", [128, 1], F32)

    p.op("gpsimd", lambda e: e.dma_start(
        out=xbf.rearrange("p (c t) -> p c t", c=NCH),
        in_=io["x_in"].rearrange("(c p) t -> p c t", p=128)), inc=wld, dma=True)
    load_weight_cast(p, io["wq"], wbuf, NCH, D, wld, col0=0)
    load_weight_cast(p, io["wk"], wbuf, NCH, D, wld, col0=NCH * D)
    load_vec(p, io["bias_all"], bias_sb, vld)
    p.barrier()
    if phases <= 1:
        return bail(s_sb)

    sc_pp = p.scope()
    pp = p.psum("pp0", [128, 1024], F32, sc_pp)
    mm_q, _ = emit_proj(
        p, w_sb=wbuf, w_col0=0, cout=D, cin_chunks=NCH, cout_chunks=NCH,
        rhs_sb=xbf, rhs_col0=0, pp=pp,
        out_cb=evac_act(qsb, 0, bias_sb=bias_sb[:, 0:8]))
    n_mm_q = mm_q.n
    mm_kk, _ = emit_proj(
        p, w_sb=wbuf, w_col0=NCH * D, cout=D, cin_chunks=NCH,
        cout_chunks=NCH, rhs_sb=xbf, rhs_col0=0, pp=pp,
        out_cb=evac_act(buA, 0, bias_sb=bias_sb[:, 8:16]))
    if phases <= 2:
        p.barrier()
        return bail(s_sb)

    p.wait("gpsimd", mm_q, n_mm_q)
    load_weight_cast(p, io["wl"], wbuf, NCH, D, wld, col0=0)
    p.wait("gpsimd", mm_kk)
    load_weight_cast(p, io["wv"], wbuf, NCH, D, wld, col0=NCH * D)
    p.wait("tensor", wld)
    emit_proj(p, w_sb=wbuf, w_col0=0, cout=D, cin_chunks=NCH, cout_chunks=NCH,
              rhs_sb=buA, rhs_col0=0, pp=pp,
              out_cb=evac_act(buB, 0, bias_sb=bias_sb[:, 24:32]))
    p.barrier()
    if phases <= 2.3:
        return bail(s_sb)
    emit_absmax_cell(p, consts, qsb, 0, NCH, TC, statscr, pp, cells[0:1, 0:1])
    emit_absmax_cell(p, consts, buB, 0, NCH, TC, statscr, pp, cells[0:1, 1:2])
    kd = p.ctr("kd")
    p.op("sync", lambda e: e.dma_start(out=io["kg_in"][:, :].rearrange(
        "(c p) t -> p c t", p=128), in_=buB[:, :].rearrange(
        "p (c t) -> p c t", c=NCH)), inc=kd, dma=True)
    cck = p.ctr("cck", unique=True)
    p.wait("gpsimd", kd)
    p.op("gpsimd", lambda e: e.collective_compute(
        "AllGather", ALU.bypass, replica_groups=[[0, 1, 2, 3], [4, 5, 6, 7]],
        ins=[io["kg_in"].opt()], outs=[io["kg_out"].opt()]), inc=(cck, 1))
    if phases <= 2.6:
        p.barrier()
        return bail(s_sb)
    emit_proj(p, w_sb=wbuf, w_col0=NCH * D, cout=D, cin_chunks=NCH,
              cout_chunks=NCH, rhs_sb=xbf, rhs_col0=0, pp=pp,
              out_cb=evac_act(buA, 0, bias_sb=bias_sb[:, 16:24]))
    p.barrier()
    emit_proj(p, w_sb=wbuf, w_col0=0, cout=D, cin_chunks=NCH, cout_chunks=NCH,
              rhs_sb=buA, rhs_col0=0, pp=pp,
              out_cb=evac_act(buB, 0, bias_sb=bias_sb[:, 24:32]))
    p.barrier()
    if phases <= 2.8:
        return bail(s_sb)
    scv = p.scope()
    emit_v_stage(p, consts, buB, io["vg_in"], scv)
    p.op("sync", lambda e: e.dma_start(out=io["st_in"], in_=cells[0:1, 0:2]),
         inc=kd, dma=True)
    p.barrier()
    scv.close()
    if phases <= 3:
        return bail(s_sb)

    ccv = p.ctr("ccv", unique=True)
    ccs = p.ctr("ccs", unique=True)
    grp4 = [[0, 1, 2, 3], [4, 5, 6, 7]]
    p.op("gpsimd", lambda e: e.collective_compute(
        "AllGather", ALU.bypass, replica_groups=grp4,
        ins=[io["vg_in"].opt()], outs=[io["vg_out"].opt()]), inc=(ccv, 1))
    p.op("gpsimd", lambda e: e.collective_compute(
        "AllGather", ALU.bypass, replica_groups=[[0, 1, 2, 3, 4, 5, 6, 7]],
        ins=[io["st_in"].opt()], outs=[io["st_out"].opt()]), inc=(ccs, 1))
    for e in ENGINES:
        p.wait(e, cck, 1)
        p.wait(e, ccv, 1)
        p.wait(e, ccs, 1)
    if phases <= 4:
        return bail(s_sb)

    emit_negb(p, consts, io["st_out"], srow, negb_sb, 0.125, pp)
    load_weight_cast(p, io["wo"], wbuf, NCH, D, wld, col0=0)
    sc_pp.close()
    p.barrier()
    if phases <= 5:
        return bail(s_sb)

    if pre_flash_cb is not None:
        pre_flash_cb()
    scf = p.scope()
    fl = alloc_flash(p, scf, buB[:, :], big0)
    emit_flash(p, consts, q_sb=qsb, kg_ap=io["kg_out"], vg_ap=io["vg_out"],
               negb_sb=negb_sb, scale=0.125, attn_sb=buA, fl=fl)
    scf.close()
    if phases <= 6:
        return bail(s_sb)

    scw = p.scope()
    xstr = p.sbuf("xstr", [128, 4 * TB], F32, scw)
    ppw = p.psum("ppw", [128, 1024], F32, scw)
    emit_proj_resid(p, w_sb=wbuf, w_col0=0, cout=D, cin_chunks=NCH,
                    rhs_sb=buA, bias_sb=bias_sb[:, 32:40], x_dram=io["x_in"],
                    s_sb=s_sb, xstr=xstr, pp=ppw)
    p.barrier()
    scw.close()
    if phases <= 7:
        return bail(s_sb)

    scl = p.scope()
    lnsc = dict(
        bf_a=buB[:, :], bf_b=qsb[:, :],
        rows=wbuf[0:1, 0:12288].bitcast(F32),
        ps=p.psum("lnps", [128, 2048], F32, scl),
        ps_row=p.psum("lnpr", [1, 2048], F32, scl),
    )
    emit_ln(p, consts, s_sb=s_sb, out_sb=s_sb, sc=lnsc)
    od = p.ctr("od")
    p.op("sync", lambda e: e.dma_start(
        out=io["x_out"].rearrange("(c p) t -> p c t", p=128),
        in_=s_sb.rearrange("p (c t) -> p c t", c=NCH)), inc=od, dma=True)
    p.barrier()
    scl.close()
    return od


def _attn_gather_tensors(nc, sfx=""):
    return dict(
        kg_in=nc.dram_tensor("kg_in" + sfx, [1024, 2048], BF16).ap(),
        kg_out=nc.dram_tensor("kg_out" + sfx, [4096, 2048], BF16).ap(),
        vg_in=nc.dram_tensor("vg_in" + sfx, [2048, 1040], BF16).ap(),
        vg_out=nc.dram_tensor("vg_out" + sfx, [8192, 1040], BF16).ap(),
        st_in=nc.dram_tensor("st_in" + sfx, [1, 2], F32).ap(),
        st_out=nc.dram_tensor("st_out" + sfx, [8, 2], F32).ap(),
    )


def build_a0_launch(phases=8):
    p = Prog()
    nc = p.nc
    io = {}
    io["x_in"] = nc.dram_tensor("x", [1024, 2048], F32, kind="ExternalInput").ap()
    io["x_out"] = nc.dram_tensor("xo", [1024, 2048], F32,
                                 kind="ExternalOutput").ap()
    for w in ("wq", "wk", "wv", "wl", "wo"):
        io[w] = nc.dram_tensor(w, [1024, 1024], F32, kind="ExternalInput").ap()
    io["bias_all"] = nc.dram_tensor("bias_all", [128, 40], F32,
                                    kind="ExternalInput").ap()
    io.update(_attn_gather_tensors(nc))
    consts = Consts(p)
    od = emit_a0(p, consts, io, phases=phases)
    return p.finalize(final_waits=[(od, od.n)])


def pack_vec(v, n_chunks=NCH):
    v = np.asarray(v, np.float32).reshape(n_chunks, 128)
    return np.ascontiguousarray(v.T)


def shard_x(x):
    out = []
    for c in range(NCORES):
        sl = np.asarray(x[c // 4, (c % 4) * TC:(c % 4 + 1) * TC, :], np.float32)
        out.append(np.ascontiguousarray(sl.T))
    return out


def unshard_x(parts):
    x = np.zeros((2, 4 * TC, D), np.float32)
    for c in range(NCORES):
        x[c // 4, (c % 4) * TC:(c % 4 + 1) * TC, :] = np.asarray(
            parts[c], np.float32).T
    return x


def stage_a0(inputs, xs):
    consts = {("c_" + k): v for k, v in make_const_arrays().items()}
    shared = dict(
        wq=np.ascontiguousarray(inputs["L0_attn_Wq"], np.float32),
        wk=np.ascontiguousarray(inputs["L0_attn_Wk"], np.float32),
        wv=np.ascontiguousarray(inputs["L0_attn_Wv"], np.float32),
        wl=np.ascontiguousarray(inputs["L0_attn_Wl"], np.float32),
        wo=np.ascontiguousarray(inputs["L0_attn_Wo"], np.float32),
        bias_all=np.concatenate(
            [pack_vec(inputs["L0_attn_b" + t]) for t in "qkvlo"], axis=1),
        **consts,
    )
    return [dict(shared, x=xs[c]) for c in range(NCORES)]


def run_spmd(nc, in_maps):
    res = run_bass_kernel_spmd(nc, in_maps, core_ids=list(range(NCORES)))
    return res.results


def kernel(**inputs):
    xs = shard_x(inputs["L0_attn_x"])
    res = run_spmd(build_fused_launch(), stage_fused(inputs, xs))
    return unshard_x([np.asarray(r["xo"], np.float32) for r in res])


NSL = 11
SLC = 2


def emit_f0_staging(p, io, wtmp, *, background=False):
    bg = background
    tl = [p.ctr("wptla"), p.ctr("wptlb")]
    ts_ = [p.ctr("wptsa"), p.ctr("wptsb")]
    b_tl = [tl[0].n, tl[1].n]
    b_ts = [ts_[0].n, ts_[1].n]
    jobs = []
    for fc in range(8):
        for rh in range(4):
            jobs.append(("wgq_g", "wgq_fm", fc, rh * 704, 704))
            jobs.append(("wuq_g", "wuq_fm", fc, rh * 704, 704))
    for ic in range(22):
        jobs.append(("wdq_g", "wdq_im", ic, 0, 704))
        jobs.append(("wdq_g", "wdq_im", ic, 704, 320))
    for j, (a, b2, blk, r0, ncols) in enumerate(jobs):
        par = j % 2
        if j >= 2:
            p.wait("sync", ts_[par], b_ts[par] + 16 * (j // 2), background=bg)
        p.op("sync", (lambda a, blk, r0, ncols, par: lambda e:
             e.dma_start_transpose(
            wtmp[:, par * 704: par * 704 + ncols],
            io[a][r0:r0 + ncols, blk * 128:(blk + 1) * 128]))(
            a, blk, r0, ncols, par), inc=tl[par], dma=True, background=bg)
        p.wait("gpsimd", tl[par], b_tl[par] + 16 * (j // 2 + 1), background=bg)
        p.op("gpsimd", (lambda b2, blk, r0, ncols, par: lambda e: e.dma_start(
            out=io[b2][blk * 128:(blk + 1) * 128, r0:r0 + ncols],
            in_=wtmp[:, par * 704: par * 704 + ncols]))(
            b2, blk, r0, ncols, par), inc=ts_[par], dma=True, background=bg)
    return dict(ts0=(ts_[0], ts_[0].n), ts1=(ts_[1], ts_[1].n))


def emit_f0(p, consts, io, phases=8.0, wprep=None):
    nc = p.nc
    hq_sb = p.sbuf("hq_sb", [128, ICH * TC], BF16)
    wdq_sb = p.sbuf("wdq_sb", [128, ICH * D], BF16)
    bigf = p.sbuf("bigf", [128, 16384], F32)
    xq = bigf[:, 0:8192].bitcast(BF16)
    s_sb = bigf[:, :]

    def bail(buf):
        od = p.ctr("od")
        p.op("vector", (lambda buf: lambda e: e.memset(buf, 0.5))(buf))
        p.barrier()
        p.op("sync", (lambda buf: lambda e: e.dma_start(
            out=io["x_out"].rearrange("(c p) t -> p c t", p=128),
            in_=buf.rearrange("p (c t) -> p c t", c=NCH)))(buf), inc=od, dma=True)
        p.barrier()
        return od

    if wprep is None:
        scw = p.scope()
        wprep = emit_f0_wprep(p, io, background=False, scope=scw,
                              bufs=hq_sb[:, 0:3968].bitcast(F32))
        p.barrier()
        scw.close()
    for e in ENGINES:
        for k in ("cc0", "cc1", "cc2"):
            p.wait(e, wprep[k][0], wprep[k][1])
    p.barrier()
    stg = emit_f0_staging(p, io, hq_sb[:, 20480:21888], background=True)
    if phases <= 1:
        return bail(s_sb)

    scx = p.scope()
    xtb = hq_sb[:, 0:8192].bitcast(F32)
    xt = hq_sb[:, 8192:16384].bitcast(F32)
    xqt = hq_sb[:, 16384:20480]
    qs2 = quant_scratch_views(wdq_sb[:, 0:22528].bitcast(F32), 1024, 64)
    pst = p.psum("pst", [128, 1024], F32, scx)
    pstb = p.psum("pstb", [128, 1024], BF16, scx)
    xld = p.ctr("xld")
    xtr = p.ctr("xtr")
    xcp = p.ctr("xcp")
    xqd = p.ctr("xqd")
    xbt = p.ctr("xbt")
    xhc = p.ctr("xhc")
    for tbb in range(NTB):
        if tbb > 0:
            p.wait("gpsimd", xtr, xtr.n)
        p.op("gpsimd", (lambda tbb: lambda e: e.dma_start(
            out=xtb[:, :].rearrange("p (c t) -> p c t", c=NCH),
            in_=io["x_in"][:, tbb * TB:(tbb + 1) * TB].rearrange(
                "(c p) t -> p c t", p=128)))(tbb), inc=xld, dma=True)
        for tt in range(4):
            p.wait("tensor", xld, xld.n)
            if not (tbb == 0 and tt == 0):
                p.wait("tensor", xcp, xcp.n)
            for c in range(NCH):
                half = (c % 4) * 128
                p.op("tensor", (lambda tt, c, half: lambda e: e.transpose(
                    pst[:, (c // 4) * 512 + half: (c // 4) * 512 + half + 128],
                    xtb[:, c * TB + tt * 128: c * TB + tt * 128 + 128],
                    consts.id128[:, :]))(tt, c, half),
                    inc=xtr if c == NCH - 1 else None)
            p.wait("vector", xtr, xtr.n)
            p.op("vector", (lambda tt: lambda e: e.tensor_copy(
                xt[:, tt * 1024:(tt + 1) * 1024], pst[:, :]))(tt), inc=xcp)
        if tbb > 0:
            p.wait("vector", xbt, xbt.n)
        for tt in range(4):
            emit_quant2(p, xt[:, tt * 1024:(tt + 1) * 1024],
                        xqt[:, tt * 1024:(tt + 1) * 1024], 64, qs2,
                        src_ready=(xcp, xcp.n), done=xqd)
        for tt in range(4):
            p.wait("tensor", xqd, xqd.n)
            if not (tbb == 0 and tt == 0):
                p.wait("tensor", xhc, xhc.n)
            for c in range(NCH):
                p.op("tensor", (lambda tt, c: lambda e: e.transpose(
                    pstb[:, c * 128:(c + 1) * 128],
                    xqt[:, tt * 1024 + c * 128: tt * 1024 + c * 128 + 128],
                    consts.id128b[:, :]))(tt, c),
                    inc=xbt if c == NCH - 1 else None)
            p.wait("vector", xbt, xbt.n)
            p.op("vector", (lambda tbb, tt: lambda e: e.tensor_copy(
                xq.rearrange("p (c t) -> p c t", c=NCH)[
                    :, :, tbb * TB + tt * 128: tbb * TB + tt * 128 + 128],
                pstb[:, :].rearrange("p (c t) -> p c t", c=NCH)))(tbb, tt),
                inc=xhc)
    p.barrier()
    scx.close()
    if phases <= 2:
        return bail(s_sb)

    for e in ENGINES:
        p.wait(e, stg["ts0"][0], stg["ts0"][1])
        p.wait(e, stg["ts1"][0], stg["ts1"][1])
    p.barrier()
    scm = p.scope()
    wsl = wdq_sb[:, 0:4 * NCH * 256]
    hg = bigf[:, 8192:9216]
    hh = hg
    ht = bigf[:, 9216:10240]
    hqt = bigf[:, 10240:10752].bitcast(BF16)
    qs3 = quant_scratch_views(wdq_sb[:, 8192:22528].bitcast(F32), 1024, 64)
    ppg = p.psum("ppg", [128, 1024], F32, scm)
    ppu = p.psum("ppu", [128, 1024], F32, scm)
    pst2 = p.psum("pst2", [128, 512], F32, scm)
    pstb2 = p.psum("pstb2", [128, 512], BF16, scm)
    wld2 = [p.ctr("fwla"), p.ctr("fwlb")]
    b_wld2 = [wld2[0].n, wld2[1].n]
    mmg = p.ctr("mmg")
    b_mmg = mmg.n
    hdone = p.ctr("mhd")
    b_hd = hdone.n
    ftr = p.ctr("mftr")
    fcp = p.ctr("mfcp")
    mqd = p.ctr("mqd")
    mbt = p.ctr("mbt")
    mhc = p.ctr("mhc")

    def back_trans(it):
        isl_, tbb_ = divmod(it, NTB)
        p.wait("tensor", mqd, mqd.n)
        for tt in range(4):
            if not (it == 0 and tt == 0):
                p.wait("tensor", mhc, mhc.n)
            for sc_ in range(SLC):
                p.op("tensor", (lambda tt, sc_: lambda e: e.transpose(
                    pstb2[:, sc_ * 128:(sc_ + 1) * 128],
                    hqt[:, tt * 256 + sc_ * 128: tt * 256 + sc_ * 128 + 128],
                    consts.id128b[:, :]))(tt, sc_),
                    inc=mbt if sc_ == SLC - 1 else None)
            p.wait("vector", mbt, mbt.n)
            for sc_ in range(SLC):
                ic = isl_ * SLC + sc_
                p.op("vector", (lambda ic, tbb_, tt, sc_: lambda e:
                     e.tensor_copy(
                    hq_sb[:, ic * TC + tbb_ * TB + tt * 128:
                          ic * TC + tbb_ * TB + tt * 128 + 128],
                    pstb2[:, sc_ * 128:(sc_ + 1) * 128]))(ic, tbb_, tt, sc_),
                    inc=mhc if sc_ == SLC - 1 else None)

    for isl in range(NSL):
        par = isl % 2
        if isl >= 2:
            p.wait("sync", mmg, b_mmg + 16 * (isl - 1))
        for wi, wfm in enumerate(("wgq_fm", "wuq_fm")):
            dst = wsl[:, (par * 2 + wi) * NCH * 256:
                      (par * 2 + wi + 1) * NCH * 256].rearrange(
                "p (c i) -> p c i", c=NCH)
            srcw = io[wfm].rearrange("(c p) i -> p c i", p=128)[
                :, :, isl * 256:(isl + 1) * 256]
            p.op("sync", (lambda dst, srcw: lambda e: e.dma_start(
                out=dst, in_=srcw))(dst, srcw), inc=wld2[par], dma=True)
        p.wait("tensor", wld2[par], b_wld2[par] + 32 * (isl // 2 + 1))
        for tbb in range(NTB):
            it = isl * NTB + tbb
            if it >= 1:
                p.wait("tensor", hdone, b_hd + it)
            for wi, ppx in ((0, ppg), (1, ppu)):
                for sc_ in range(SLC):
                    bank = ppx[:, sc_ * 512:(sc_ + 1) * 512]
                    for ci in range(NCH):
                        wslc = wsl[:, (par * 2 + wi) * NCH * 256 + ci * 256
                                   + sc_ * 128: (par * 2 + wi) * NCH * 256
                                   + ci * 256 + sc_ * 128 + 128]
                        rsl = xq[:, ci * TC + tbb * TB: ci * TC + tbb * TB + TB]
                        p.op("tensor", (lambda bank, wslc, rsl, first, last:
                                        lambda e: e.matmul(
                            bank, wslc, rsl, start=first, stop=last))(
                            bank, wslc, rsl, ci == 0, ci == NCH - 1),
                            inc=mmg if ci == NCH - 1 else None)
            if it >= 1:
                back_trans(it - 1)
            p.wait("scalar", mmg, b_mmg + 4 * it + 2)
            if it >= 1:
                p.wait("scalar", hdone, b_hd + it)
            msil = p.ctr("msil")
            p.op("scalar", lambda e: e.activation(
                hg[:, 0:1024], ppg[:, 0:1024], AF.Silu), inc=msil)
            p.wait("vector", mmg, b_mmg + 4 * it + 4)
            p.wait("vector", msil, msil.n)
            p.op("vector", lambda e: e.tensor_tensor(
                hh[:, 0:1024], hg[:, 0:1024], ppu[:, 0:1024], ALU.mult),
                inc=hdone)
            for tt in range(4):
                p.wait("tensor", hdone, b_hd + it + 1)
                if not (it == 0 and tt == 0):
                    p.wait("tensor", fcp, fcp.n)
                for sc_ in range(SLC):
                    p.op("tensor", (lambda tt, sc_: lambda e: e.transpose(
                        pst2[:, sc_ * 128:(sc_ + 1) * 128],
                        hh[:, sc_ * 512 + tt * 128:
                           sc_ * 512 + tt * 128 + 128],
                        consts.id128[:, :]))(tt, sc_),
                        inc=ftr if sc_ == SLC - 1 else None)
                p.wait("vector", ftr, ftr.n)
                p.op("vector", (lambda tt: lambda e: e.tensor_copy(
                    ht[:, tt * 256:(tt + 1) * 256], pst2[:, 0:256]))(tt),
                    inc=fcp)
            if it >= 1:
                p.wait("vector", mbt, mbt.n)
            emit_quant2(p, ht[:, :], hqt[:, :], 64, qs3,
                        src_ready=(fcp, fcp.n), done=mqd)
    back_trans(NSL * NTB - 1)
    p.barrier()
    scm.close()
    if phases <= 3:
        return bail(s_sb)

    scd = p.scope()
    dld = p.ctr("dld")
    p.op("gpsimd", lambda e: e.dma_start(
        out=wdq_sb[:, :].rearrange("p (c f) -> p c f", c=ICH),
        in_=io["wdq_im"].rearrange("(c p) f -> p c f", p=128)), inc=dld, dma=True)
    p.barrier()
    xstr = p.sbuf("xstr", [128, 4 * TB], F32, scd)
    ppd = p.psum("ppd", [128, 1024], F32, scd)
    emit_proj_resid(p, w_sb=wdq_sb, w_col0=0, cout=D, cin_chunks=ICH,
                    rhs_sb=hq_sb, bias_sb=None, x_dram=io["x_in"],
                    s_sb=s_sb, xstr=xstr, pp=ppd)
    p.barrier()
    scd.close()
    if phases <= 4:
        return bail(s_sb)

    scl = p.scope()
    lnsc = dict(
        bf_a=hq_sb[:, 0:NCH * TC],
        bf_b=hq_sb[:, NCH * TC:2 * NCH * TC],
        rows=wdq_sb[0:1, 0:12288].bitcast(F32),
        ps=p.psum("lnps", [128, 2048], F32, scl),
        ps_row=p.psum("lnpr", [1, 2048], F32, scl),
    )
    emit_ln(p, consts, s_sb=s_sb, out_sb=s_sb, sc=lnsc)
    od = p.ctr("od")
    p.op("sync", lambda e: e.dma_start(
        out=io["x_out"].rearrange("(c p) t -> p c t", p=128),
        in_=s_sb.rearrange("p (c t) -> p c t", c=NCH)), inc=od, dma=True)
    p.barrier()
    scl.close()
    return od


def build_f0_launch(phases=8.0):
    p = Prog()
    nc = p.nc
    io = {}
    io["x_in"] = nc.dram_tensor("x", [1024, 2048], F32, kind="ExternalInput").ap()
    io["x_out"] = nc.dram_tensor("xo", [1024, 2048], F32,
                                 kind="ExternalOutput").ap()
    io["wg_shard"] = nc.dram_tensor("wg_shard", [352, 1024], F32,
                                    kind="ExternalInput").ap()
    io["wu_shard"] = nc.dram_tensor("wu_shard", [352, 1024], F32,
                                    kind="ExternalInput").ap()
    io["wd_shard"] = nc.dram_tensor("wd_shard", [128, 2816], F32,
                                    kind="ExternalInput").ap()
    for nm, shp in (("wgq_in", [352, 1024]), ("wgq_g", [2816, 1024]),
                    ("wuq_in", [352, 1024]), ("wuq_g", [2816, 1024]),
                    ("wdq_in", [128, 2816]), ("wdq_g", [1024, 2816]),
                    ("wgq_fm", [1024, 2816]), ("wuq_fm", [1024, 2816]),
                    ("wdq_im", [2816, 1024])):
        io[nm] = nc.dram_tensor(nm, shp, BF16).ap()
    consts = Consts(p)
    od = emit_f0(p, consts, io, phases=phases)
    return p.finalize(final_waits=[(od, od.n)])


def stage_f0(inputs, xs):
    consts = {("c_" + k): v for k, v in make_const_arrays().items()}
    wg = np.ascontiguousarray(inputs["L0_ffn_w_gate"], np.float32)
    wu = np.ascontiguousarray(inputs["L0_ffn_w_up"], np.float32)
    wd = np.ascontiguousarray(inputs["L0_ffn_w_down"], np.float32)
    maps = []
    for c in range(NCORES):
        maps.append(dict(
            x=xs[c],
            wg_shard=np.ascontiguousarray(wg[c * 352:(c + 1) * 352]),
            wu_shard=np.ascontiguousarray(wu[c * 352:(c + 1) * 352]),
            wd_shard=np.ascontiguousarray(wd[c * 128:(c + 1) * 128]),
            **consts,
        ))
    return maps


def emit_rope_path(p, consts, *, w_sb, w_col0, cin_chunks, bias_sl, rhs_sb,
                   cs_sb, out_sb, krtb, ktok, rtmp, pp, pst, add_out=True):
    for tb in range(NTB):
        emit_proj(p, w_sb=w_sb, w_col0=w_col0, cout=D, cin_chunks=cin_chunks,
                  cout_chunks=NCH, rhs_sb=rhs_sb, rhs_col0=tb * TB, pp=pp,
                  n_tok=TB, tbs=TB, rhs_tok=TC,
                  out_cb=evac_act(krtb, 0, bias_sb=bias_sl, chunk_stride=TB))
        p.barrier()
        for tt in range(4):
            for c in range(NCH):
                half = (c % 4) * 128
                p.op("tensor", (lambda tt, c, half: lambda e: e.transpose(
                    pst[:, (c // 4) * 512 + half:(c // 4) * 512 + half + 128],
                    krtb[:, c * TB + tt * 128: c * TB + tt * 128 + 128],
                    consts.id128[:, :]))(tt, c, half))
            p.barrier()
            p.op("vector", (lambda tt: lambda e: e.tensor_copy(
                ktok[:, tt * 1024:(tt + 1) * 1024], pst[:, :]))(tt))
            p.barrier()
        for tt in range(4):
            ttg = tb * 4 + tt
            base = tt * 1024
            x0 = ktok[:, base:base + 1024].rearrange(
                "p (i two) -> p i two", two=2)[:, :, 0:1].squeeze(-1)
            x1 = ktok[:, base:base + 1024].rearrange(
                "p (i two) -> p i two", two=2)[:, :, 1:2].squeeze(-1)
            cosv = cs_sb[:, ttg * 512:(ttg + 1) * 512]
            sinv = cs_sb[:, 8192 + ttg * 512: 8192 + (ttg + 1) * 512]
            e0 = rtmp[:, 0:512]
            e1 = rtmp[:, 512:1024]
            p.op("vector", (lambda e0, x0, cosv: lambda e: e.tensor_tensor(
                e0, x0, cosv, ALU.mult))(e0, x0, cosv))
            p.op("vector", (lambda e1, x1, sinv: lambda e: e.tensor_tensor(
                e1, x1, sinv, ALU.mult))(e1, x1, sinv))
            p.op("vector", (lambda e0, e1: lambda e: e.tensor_tensor(
                e0, e0, e1, ALU.subtract))(e0, e1))
            p.op("vector", (lambda e1, x1, cosv: lambda e: e.tensor_tensor(
                e1, x1, cosv, ALU.mult))(e1, x1, cosv))
            p.op("vector", (lambda x1, x0, sinv: lambda e: e.tensor_tensor(
                x1, x0, sinv, ALU.mult))(x1, x0, sinv))
            p.op("vector", (lambda x1, e1: lambda e: e.tensor_tensor(
                x1, e1, x1, ALU.add))(x1, e1))
            p.op("vector", (lambda x0, e0: lambda e: e.tensor_copy(
                x0, e0))(x0, e0))
            p.barrier()
        for tt in range(4):
            for c in range(NCH):
                half = (c % 4) * 128
                p.op("tensor", (lambda tt, c, half: lambda e: e.transpose(
                    pst[:, (c // 4) * 512 + half:(c // 4) * 512 + half + 128],
                    ktok[:, tt * 1024 + c * 128: tt * 1024 + c * 128 + 128],
                    consts.id128[:, :]))(tt, c, half))
            p.barrier()
            osl = out_sb.rearrange("p (c t) -> p c t", c=NCH)[
                :, :, tb * TB + tt * 128: tb * TB + tt * 128 + 128]
            psv = pst[:, :].rearrange("p (c t) -> p c t", c=NCH)
            if add_out:
                p.op("vector", (lambda osl, psv: lambda e: e.tensor_tensor(
                    osl, psv, osl, ALU.add))(osl, psv))
            else:
                p.op("vector", (lambda osl, psv: lambda e: e.tensor_copy(
                    osl, psv))(osl, psv))
            p.barrier()


def emit_a1(p, consts, io, phases=8.0):
    nc = p.nc
    wld = p.ctr("wld")
    vld = p.ctr("vld")
    wbuf = p.sbuf("wbuf1", [128, 2 * NCH * D], BF16)
    qsb = p.sbuf("qsb1", [128, NCH * TC], BF16)
    buA = p.sbuf("buA1", [128, NCH * TC], BF16)
    cs_sb = p.sbuf("cs1", [128, NCH * TC], BF16)
    big0 = p.sbuf("big1", [128, NCH * TC], F32)
    xbf = big0[:, 0:8192].bitcast(BF16)
    valb = xbf
    ctq = big0[:, 8192:12288].bitcast(BF16)
    ctkv = big0[:, 12288:16384].bitcast(BF16)
    s_sb = big0[:, :]
    bias_sb = p.sbuf("bias1", [128, 56], F32)
    srow = p.sbuf("srow1", [1, 64], F32)
    cells = p.sbuf("cells1", [1, 8], F32)
    statscr = p.sbuf("statscr1", [128, 16], F32)
    negb_sb = p.sbuf("negb1", [128, 1], F32)
    SC1 = float(1.0 / np.sqrt(np.float32(16 + 512 + 512)))

    def bail(buf):
        od = p.ctr("od")
        p.op("vector", (lambda buf: lambda e: e.memset(buf, 0.5))(buf))
        p.barrier()
        p.op("sync", (lambda buf: lambda e: e.dma_start(
            out=io["x_out"].rearrange("(c p) t -> p c t", p=128),
            in_=buf.rearrange("p (c t) -> p c t", c=NCH)))(buf), inc=od, dma=True)
        p.barrier()
        return od

    BD = dict(dq=(0, 4), uq=(4, 12), qr=(12, 20), kr=(20, 28), dkv=(28, 32),
              uk=(32, 40), uv=(40, 48), o=(48, 56))

    def bsl(key):
        a, b = BD[key]
        return bias_sb[:, a:b]

    p.op("gpsimd", lambda e: e.dma_start(
        out=xbf.rearrange("p (c t) -> p c t", c=NCH),
        in_=io["x_in"].rearrange("(c p) t -> p c t", p=128)), inc=wld, dma=True)
    p.op("gpsimd", lambda e: e.dma_start(
        out=cs_sb[:, :], in_=io["cs"]), inc=wld, dma=True)
    load_weight_cast(p, io["wdq_w"], wbuf, NCH, 512, wld, col0=0)
    load_weight_cast(p, io["wdkv_w"], wbuf, NCH, 512, wld, col0=4096)
    load_weight_cast(p, io["wuk_w"], wbuf, 4, D, wld, col0=8192)
    load_vec(p, io["bias_all"], bias_sb, vld)
    p.barrier()

    sc_pp = p.scope()
    pp = p.psum("pp1", [128, 1024], F32, sc_pp)
    pst = p.psum("pstA", [128, 1024], F32, sc_pp)
    emit_proj(p, w_sb=wbuf, w_col0=0, cout=512, cin_chunks=NCH, cout_chunks=4,
              rhs_sb=xbf, rhs_col0=0, pp=pp,
              out_cb=evac_act(ctq, 0, bias_sb=bsl("dq")))
    p.barrier()
    emit_proj(p, w_sb=wbuf, w_col0=4096, cout=512, cin_chunks=NCH,
              cout_chunks=4, rhs_sb=xbf, rhs_col0=0, pp=pp,
              out_cb=evac_act(ctkv, 0, bias_sb=bsl("dkv")))
    p.barrier()
    if phases <= 1:
        sc_pp.close()
        return bail(s_sb)

    emit_proj(p, w_sb=wbuf, w_col0=8192, cout=D, cin_chunks=4, cout_chunks=NCH,
              rhs_sb=ctkv, rhs_col0=0, pp=pp,
              out_cb=evac_act(buA, 0, bias_sb=bsl("uk")))
    load_weight_cast(p, io["wkr_w"], wbuf, NCH, D, wld, col0=0)
    p.barrier()
    krtb = qsb[:, 0:8192].bitcast(F32)
    ktok = qsb[:, 8192:16384].bitcast(F32)
    rtmp = p.sbuf("rtmp1", [128, 1024], F32)
    emit_rope_path(p, consts, w_sb=wbuf, w_col0=0, cin_chunks=NCH,
                   bias_sl=bsl("kr"), rhs_sb=xbf, cs_sb=cs_sb, out_sb=buA,
                   krtb=krtb, ktok=ktok, rtmp=rtmp, pp=pp, pst=pst)
    if phases <= 2:
        sc_pp.close()
        return bail(s_sb)

    emit_absmax_cell(p, consts, buA, 0, NCH, TC, statscr, pp, cells[0:1, 1:2])
    kd = p.ctr("kd")
    p.op("sync", lambda e: e.dma_start(out=io["kg_in"][:, :].rearrange(
        "(c p) t -> p c t", p=128), in_=buA[:, :].rearrange(
        "p (c t) -> p c t", c=NCH)), inc=kd, dma=True)
    cck = p.ctr("cck1", unique=True)
    p.wait("gpsimd", kd)
    p.op("gpsimd", lambda e: e.collective_compute(
        "AllGather", ALU.bypass, replica_groups=[[0, 1, 2, 3], [4, 5, 6, 7]],
        ins=[io["kg_in"].opt()], outs=[io["kg_out"].opt()]), inc=(cck, 1))
    load_weight_cast(p, io["wuv_w"], wbuf, 4, D, wld, col0=8192)
    p.barrier()
    emit_proj(p, w_sb=wbuf, w_col0=8192, cout=D, cin_chunks=4, cout_chunks=NCH,
              rhs_sb=ctkv, rhs_col0=0, pp=pp,
              out_cb=evac_act(valb, 0, bias_sb=bsl("uv")))
    p.barrier()
    scv = p.scope()
    emit_v_stage(p, consts, valb, io["vg_in"], scv)
    p.barrier()
    scv.close()
    if phases <= 3:
        sc_pp.close()
        return bail(s_sb)

    load_weight_cast(p, io["wuq_w"], wbuf, 4, D, wld, col0=0)
    load_weight_cast(p, io["wqr_w"], wbuf, 4, D, wld, col0=4096)
    p.barrier()
    emit_proj(p, w_sb=wbuf, w_col0=0, cout=D, cin_chunks=4, cout_chunks=NCH,
              rhs_sb=ctq, rhs_col0=0, pp=pp,
              out_cb=evac_act(qsb, 0, bias_sb=bsl("uq")))
    p.barrier()
    krtb2 = valb[0:128, 0:8192].bitcast(F32)
    ktok2 = valb[0:128, 8192:16384].bitcast(F32)
    emit_rope_path(p, consts, w_sb=wbuf, w_col0=4096, cin_chunks=4,
                   bias_sl=bsl("qr"), rhs_sb=ctq, cs_sb=cs_sb, out_sb=qsb,
                   krtb=krtb2, ktok=ktok2, rtmp=rtmp, pp=pp, pst=pst)
    if phases <= 4:
        sc_pp.close()
        return bail(s_sb)

    emit_absmax_cell(p, consts, qsb, 0, NCH, TC, statscr, pp, cells[0:1, 0:1])
    p.op("sync", lambda e: e.dma_start(out=io["st_in"], in_=cells[0:1, 0:2]),
         inc=kd, dma=True)
    p.barrier()
    ccv = p.ctr("ccv1", unique=True)
    ccs = p.ctr("ccs1", unique=True)
    grp4 = [[0, 1, 2, 3], [4, 5, 6, 7]]
    p.op("gpsimd", lambda e: e.collective_compute(
        "AllGather", ALU.bypass, replica_groups=grp4,
        ins=[io["vg_in"].opt()], outs=[io["vg_out"].opt()]), inc=(ccv, 1))
    p.op("gpsimd", lambda e: e.collective_compute(
        "AllGather", ALU.bypass, replica_groups=[[0, 1, 2, 3, 4, 5, 6, 7]],
        ins=[io["st_in"].opt()], outs=[io["st_out"].opt()]), inc=(ccs, 1))
    for e in ENGINES:
        p.wait(e, cck, 1)
        p.wait(e, ccv, 1)
        p.wait(e, ccs, 1)
    emit_negb(p, consts, io["st_out"], srow, negb_sb, SC1, pp)
    load_weight_cast(p, io["wo_w"], wbuf, NCH, D, wld, col0=0)
    sc_pp.close()
    p.barrier()
    if phases <= 5:
        return bail(s_sb)

    scf = p.scope()
    fl = alloc_flash(p, scf, cs_sb[:, :], big0)
    emit_flash(p, consts, q_sb=qsb, kg_ap=io["kg_out"], vg_ap=io["vg_out"],
               negb_sb=negb_sb, scale=SC1, attn_sb=buA, fl=fl)
    scf.close()
    if phases <= 6:
        return bail(s_sb)

    scw = p.scope()
    xstr = p.sbuf("xstr1", [128, 4 * TB], F32, scw)
    ppw = p.psum("ppw1", [128, 1024], F32, scw)
    emit_proj_resid(p, w_sb=wbuf, w_col0=0, cout=D, cin_chunks=NCH,
                    rhs_sb=buA, bias_sb=bsl("o"), x_dram=io["x_in"],
                    s_sb=s_sb, xstr=xstr, pp=ppw)
    p.barrier()
    scw.close()

    scl = p.scope()
    lnsc = dict(
        bf_a=buA[:, :], bf_b=qsb[:, :],
        rows=wbuf[0:1, 0:12288].bitcast(F32),
        ps=p.psum("lnps1", [128, 2048], F32, scl),
        ps_row=p.psum("lnpr1", [1, 2048], F32, scl),
    )
    emit_ln(p, consts, s_sb=s_sb, out_sb=s_sb, sc=lnsc)
    od = p.ctr("od")
    p.op("sync", lambda e: e.dma_start(
        out=io["x_out"].rearrange("(c p) t -> p c t", p=128),
        in_=s_sb.rearrange("p (c t) -> p c t", c=NCH)), inc=od, dma=True)
    p.barrier()
    scl.close()
    return od


def build_a1_launch(phases=8.0):
    p = Prog()
    nc = p.nc
    io = {}
    io["x_in"] = nc.dram_tensor("x", [1024, 2048], F32, kind="ExternalInput").ap()
    io["x_out"] = nc.dram_tensor("xo", [1024, 2048], F32,
                                 kind="ExternalOutput").ap()
    for nm, shp in (("wdq_w", [1024, 512]), ("wuq_w", [512, 1024]),
                    ("wqr_w", [512, 1024]), ("wkr_w", [1024, 1024]),
                    ("wdkv_w", [1024, 512]), ("wuk_w", [512, 1024]),
                    ("wuv_w", [512, 1024]), ("wo_w", [1024, 1024]),
                    ("bias_all", [128, 56]), ("cs", [128, 16384])):
        io[nm] = nc.dram_tensor(nm, shp, F32, kind="ExternalInput").ap()
    io.update(_attn_gather_tensors(nc, "1"))
    consts = Consts(p)
    od = emit_a1(p, consts, io, phases=phases)
    return p.finalize(final_waits=[(od, od.n)])


def host_cos_sin():
    ids = np.arange(512, dtype=np.float32)
    theta = (np.float32(10000.0) ** (np.float32(-2.0) * ids)) / np.float32(512)
    out = []
    for c in range(NCORES):
        t0 = (c % 4) * TC
        pos = np.arange(t0, t0 + TC, dtype=np.float32)[:, None]
        r = (pos * theta[None, :]).astype(np.float32)
        cos = np.cos(r).astype(np.float32)
        sin = np.sin(r).astype(np.float32)
        arr = np.zeros((128, 16384), np.float32)
        for tt in range(16):
            arr[:, tt * 512:(tt + 1) * 512] = cos[tt * 128:(tt + 1) * 128]
            arr[:, 8192 + tt * 512: 8192 + (tt + 1) * 512] = \
                sin[tt * 128:(tt + 1) * 128]
        out.append(arr)
    return out


def stage_a1(inputs, xs):
    consts = {("c_" + k): v for k, v in make_const_arrays().items()}
    cs = host_cos_sin()
    bias = np.concatenate([
        pack_vec(inputs["L1_attn_Wd_q_b"], 4),
        pack_vec(inputs["L1_attn_Wu_q_b"], 8),
        pack_vec(inputs["L1_attn_Wq_r_b"], 8),
        pack_vec(inputs["L1_attn_Wk_r_b"], 8),
        pack_vec(inputs["L1_attn_Wd_kv_b"], 4),
        pack_vec(inputs["L1_attn_Wu_k_b"], 8),
        pack_vec(inputs["L1_attn_Wu_v_b"], 8),
        pack_vec(inputs["L1_attn_Wo_b"], 8),
    ], axis=1)
    shared = dict(
        wdq_w=np.ascontiguousarray(inputs["L1_attn_Wd_q_w"], np.float32),
        wuq_w=np.ascontiguousarray(inputs["L1_attn_Wu_q_w"], np.float32),
        wqr_w=np.ascontiguousarray(inputs["L1_attn_Wq_r_w"], np.float32),
        wkr_w=np.ascontiguousarray(inputs["L1_attn_Wk_r_w"], np.float32),
        wdkv_w=np.ascontiguousarray(inputs["L1_attn_Wd_kv_w"], np.float32),
        wuk_w=np.ascontiguousarray(inputs["L1_attn_Wu_k_w"], np.float32),
        wuv_w=np.ascontiguousarray(inputs["L1_attn_Wu_v_w"], np.float32),
        wo_w=np.ascontiguousarray(inputs["L1_attn_Wo_w"], np.float32),
        bias_all=bias,
        **consts,
    )
    return [dict(shared, x=xs[c], cs=cs[c]) for c in range(NCORES)]


def emit_f1(p, consts, io, phases=8.0):
    nc = p.nc
    wld = p.ctr("wld")
    vld = p.ctr("vld")
    wbuf = p.sbuf("wbufF", [128, 3 * NCH * D], BF16)
    s_sb = p.sbuf("bigF", [128, NCH * TC], F32)
    lnb = p.sbuf("lnbF", [128, NCH * TC], BF16)
    x_tb = p.sbuf("x_tbF", [128, NCH * 513], F32)
    xr_tb = p.sbuf("xrF", [128, NCH * TB], BF16)
    xk_tb = p.sbuf("xkF", [128, NCH * TB], BF16)
    r_tb = p.sbuf("rF", [128, NCH * TB], F32)
    ka_tb = p.sbuf("kaF", [128, NCH * TB], BF16)
    t1 = p.sbuf("t1F", [128, TB], F32)
    mus = p.sbuf("musF", [128, 32], F32)
    msk = p.sbuf("mskF", [128, 64], F32)
    g64 = p.sbuf("g64F", [128, 64], F32)
    bcol = p.sbuf("bcolF", [128, 8], F32)

    def bail(buf):
        od = p.ctr("od")
        p.op("vector", (lambda buf: lambda e: e.memset(buf, 0.5))(buf))
        p.barrier()
        p.op("sync", (lambda buf: lambda e: e.dma_start(
            out=io["x_out"].rearrange("(c p) t -> p c t", p=128),
            in_=buf.rearrange("p (c t) -> p c t", c=NCH)))(buf), inc=od, dma=True)
        p.barrier()
        return od

    load_weight_cast(p, io["wr"], wbuf, NCH, D, wld, col0=0)
    load_weight_cast(p, io["wk"], wbuf, NCH, D, wld, col0=8192)
    load_weight_cast(p, io["wv"], wbuf, NCH, D, wld, col0=16384)
    p.op("sync", lambda e: e.dma_start(out=mus[:, 0:16], in_=io["mus"]),
         inc=vld, dma=True)
    p.op("sync", lambda e: e.dma_start(out=msk[:, :], in_=io["msk"]),
         inc=vld, dma=True)
    p.op("sync", lambda e: e.dma_start(
        out=bcol[:, :].rearrange("p c -> p c ()"),
        in_=io["x_in"].rearrange("(c p) t -> p c t", p=128)[:, :, 2047:2048],
        allow_slow_non_contiguous=True), inc=vld, dma=True)
    p.barrier()
    cc = p.ctr("ccF", unique=True)
    p.op("sync", lambda e: e.dma_start(out=io["bc_in"], in_=bcol[:, :]),
         inc=vld, dma=True)
    p.barrier()
    p.op("gpsimd", lambda e: e.collective_compute(
        "AllGather", ALU.bypass, replica_groups=[[0, 1, 2, 3, 4, 5, 6, 7]],
        ins=[io["bc_in"].opt()], outs=[io["bc_g"].opt()]), inc=(cc, 1))
    for e in ENGINES:
        p.wait(e, cc, 1)
    p.op("sync", lambda e: e.dma_start(
        out=g64[:, :].rearrange("p (r c) -> p r c", r=8),
        in_=io["bc_g"].rearrange("(r p) c -> p r c", p=128)), inc=vld, dma=True)
    p.barrier()
    p.op("vector", lambda e: e.tensor_tensor(
        g64[:, :].rearrange("p (r c) -> p c r", r=8),
        g64[:, :].rearrange("p (r c) -> p c r", r=8),
        msk[:, :].rearrange("p (r c) -> p c r", r=8), ALU.mult))
    p.barrier()
    p.op("vector", lambda e: e.tensor_reduce(
        bcol[:, :], g64[:, :].rearrange("p (r c) -> p c r", r=8), AX.X, ALU.add))
    p.op("vector", lambda e: e.tensor_scalar(
        mus[:, 16:32], mus[:, 0:16], -1.0, 1.0, ALU.mult, ALU.add))
    p.barrier()
    if phases <= 1:
        return bail(s_sb)

    sc_pp = p.scope()
    pp = p.psum("ppF", [128, 1024], F32, sc_pp)
    xldF = p.ctr("xldF")
    for tbb in range(NTB):
        if tbb == 0:
            p.op("sync", lambda e: e.dma_start(
                out=x_tb[:, :].rearrange("p (c t) -> p c t", c=NCH)[:, :, 1:513],
                in_=io["x_in"].rearrange("(c p) t -> p c t", p=128)[
                    :, :, 0:512]), inc=xldF, dma=True)
            p.barrier()
            p.op("vector", lambda e: e.tensor_copy(
                x_tb[:, :].rearrange("p (c t) -> p c t", c=NCH)[:, :, 0:1],
                bcol[:, :].rearrange("p c -> p c ()")))
        else:
            p.op("sync", (lambda tbb: lambda e: e.dma_start(
                out=x_tb[:, :].rearrange("p (c t) -> p c t", c=NCH),
                in_=io["x_in"].rearrange("(c p) t -> p c t", p=128)[
                    :, :, tbb * TB - 1: tbb * TB + 512]))(tbb), inc=xldF,
                dma=True)
        p.barrier()
        for which, xo, mu0, om0 in ((0, xr_tb, 0, 16), (1, xk_tb, 8, 24)):
            for c in range(NCH):
                xc = x_tb[:, c * 513 + 1: c * 513 + 513]
                xp = x_tb[:, c * 513: c * 513 + 512]
                p.op("vector", (lambda xc, mu: lambda e: e.tensor_scalar(
                    t1[:, :], xc, mu, None, ALU.mult, ...))(
                    xc, None) if False else
                    (lambda xc, c, mu0: lambda e: e.tensor_single_scalar(
                        t1[:, :], xc, mus[:, mu0 + c: mu0 + c + 1], ALU.mult))(
                        xc, c, mu0))
                p.op("vector", (lambda xo, xp, c, om0: lambda e:
                     e.scalar_tensor_tensor(
                    xo[:, c * TB:(c + 1) * TB], xp,
                    mus[:, om0 + c: om0 + c + 1], t1[:, :],
                    ALU.mult, ALU.add))(xo, xp, c, om0))
        p.barrier()
        emit_proj(p, w_sb=wbuf, w_col0=0, cout=D, cin_chunks=NCH,
                  cout_chunks=NCH, rhs_sb=xr_tb, rhs_col0=0, pp=pp,
                  n_tok=TB, tbs=TB, rhs_tok=TB,
                  out_cb=evac_act(r_tb, 0, func=AF.Sigmoid, chunk_stride=TB))
        p.barrier()

        def cb_k(p_, bank, f, t0, nt, mm, ev):
            p_.wait("vector", mm)
            p_.op("vector", (lambda bank: lambda e: e.tensor_scalar_max(
                t1[:, :], bank, 0.0))(bank))
            p_.op("vector", (lambda f: lambda e: e.tensor_tensor(
                ka_tb[:, f * TB: f * TB + TB], t1[:, :], t1[:, :],
                ALU.mult))(f), inc=ev)
        emit_proj(p, w_sb=wbuf, w_col0=8192, cout=D, cin_chunks=NCH,
                  cout_chunks=NCH, rhs_sb=xk_tb, rhs_col0=0, pp=pp,
                  n_tok=TB, tbs=TB, rhs_tok=TB, out_cb=cb_k)
        p.barrier()

        def cb_v(p_, bank, f, t0, nt, mm, ev):
            p_.wait("vector", mm)
            rsl = r_tb[:, f * TB: f * TB + nt]
            ssl = s_sb[:, f * TC + tbb * TB: f * TC + tbb * TB + nt]
            xsl = x_tb[:, f * 513 + 1: f * 513 + 513]
            p_.op("vector", (lambda ssl, bank, rsl: lambda e: e.tensor_tensor(
                ssl, bank, rsl, ALU.mult))(ssl, bank, rsl))
            p_.op("vector", (lambda ssl, xsl: lambda e: e.tensor_tensor(
                ssl, ssl, xsl, ALU.add))(ssl, xsl), inc=ev)
        emit_proj(p, w_sb=wbuf, w_col0=16384, cout=D, cin_chunks=NCH,
                  cout_chunks=NCH, rhs_sb=ka_tb, rhs_col0=0, pp=pp,
                  n_tok=TB, tbs=TB, rhs_tok=TB, out_cb=cb_v)
        p.barrier()
    sc_pp.close()
    if phases <= 2:
        return bail(s_sb)

    scl = p.scope()
    lnsc = dict(
        bf_a=wbuf[:, 8192:24576], bf_b=lnb[:, :],
        rows=wbuf[0:1, 0:12288].bitcast(F32),
        ps=p.psum("lnpsF", [128, 2048], F32, scl),
        ps_row=p.psum("lnprF", [1, 2048], F32, scl),
    )
    emit_ln(p, consts, s_sb=s_sb, out_sb=s_sb, sc=lnsc)
    od = p.ctr("od")
    p.op("sync", lambda e: e.dma_start(
        out=io["x_out"].rearrange("(c p) t -> p c t", p=128),
        in_=s_sb.rearrange("p (c t) -> p c t", c=NCH)), inc=od, dma=True)
    p.barrier()
    scl.close()
    return od


def build_f1_launch(phases=8.0):
    p = Prog()
    nc = p.nc
    io = {}
    io["x_in"] = nc.dram_tensor("x", [1024, 2048], F32, kind="ExternalInput").ap()
    io["x_out"] = nc.dram_tensor("xo", [1024, 2048], F32,
                                 kind="ExternalOutput").ap()
    for nm, shp in (("wr", [1024, 1024]), ("wk", [1024, 1024]),
                    ("wv", [1024, 1024]), ("mus", [128, 16]),
                    ("msk", [128, 64])):
        io[nm] = nc.dram_tensor(nm, shp, F32, kind="ExternalInput").ap()
    io["bc_in"] = nc.dram_tensor("bc_in", [128, 8], F32).ap()
    io["bc_g"] = nc.dram_tensor("bc_g", [1024, 8], F32).ap()
    consts = Consts(p)
    od = emit_f1(p, consts, io, phases=phases)
    return p.finalize(final_waits=[(od, od.n)])


def stage_f1(inputs, xs):
    consts = {("c_" + k): v for k, v in make_const_arrays().items()}
    mus = np.concatenate([pack_vec(inputs["L1_ffn_mu_r"], 8),
                          pack_vec(inputs["L1_ffn_mu_k"], 8)], axis=1)
    shared = dict(
        wr=np.ascontiguousarray(np.asarray(inputs["L1_ffn_W_r"],
                                           np.float32).T),
        wk=np.ascontiguousarray(np.asarray(inputs["L1_ffn_W_k"],
                                           np.float32).T),
        wv=np.ascontiguousarray(np.asarray(inputs["L1_ffn_W_v"],
                                           np.float32).T),
        mus=mus,
        **consts,
    )
    maps = []
    for c in range(NCORES):
        m = np.zeros((128, 8, 8), np.float32)
        if c % 4 != 0:
            m[:, c - 1, :] = 1.0
        maps.append(dict(shared, x=xs[c], msk=m.reshape(128, 64)))
    return maps


def build_fused_launch():
    p = Prog()
    nc = p.nc

    def din(nm, shp):
        return nc.dram_tensor(nm, shp, F32, kind="ExternalInput").ap()

    x0 = din("x", [1024, 2048])
    xo = nc.dram_tensor("xo", [1024, 2048], F32, kind="ExternalOutput").ap()
    x1 = nc.dram_tensor("x1", [1024, 2048], F32).ap()
    x2 = nc.dram_tensor("x2", [1024, 2048], F32).ap()
    x3 = nc.dram_tensor("x3", [1024, 2048], F32).ap()

    io_a0 = dict(x_in=x0, x_out=x1)
    for w in ("wq", "wk", "wv", "wl", "wo"):
        io_a0[w] = din("a0_" + w, [1024, 1024])
    io_a0["bias_all"] = din("a0_bias", [128, 40])
    io_a0.update(_attn_gather_tensors(nc, "0"))

    io_f0 = dict(x_in=x1, x_out=x2,
                 wg_shard=din("wg_shard", [352, 1024]),
                 wu_shard=din("wu_shard", [352, 1024]),
                 wd_shard=din("wd_shard", [128, 2816]))
    for nm, shp in (("wgq_in", [352, 1024]), ("wgq_g", [2816, 1024]),
                    ("wuq_in", [352, 1024]), ("wuq_g", [2816, 1024]),
                    ("wdq_in", [128, 2816]), ("wdq_g", [1024, 2816]),
                    ("wgq_fm", [1024, 2816]), ("wuq_fm", [1024, 2816]),
                    ("wdq_im", [2816, 1024])):
        io_f0[nm] = nc.dram_tensor(nm, shp, BF16).ap()

    io_a1 = dict(x_in=x2, x_out=x3)
    for nm, shp in (("wdq_w", [1024, 512]), ("wuq_w", [512, 1024]),
                    ("wqr_w", [512, 1024]), ("wkr_w", [1024, 1024]),
                    ("wdkv_w", [1024, 512]), ("wuk_w", [512, 1024]),
                    ("wuv_w", [512, 1024]), ("wo_w", [1024, 1024])):
        io_a1[nm] = din("a1_" + nm, shp)
    io_a1["bias_all"] = din("a1_bias", [128, 56])
    io_a1["cs"] = din("a1_cs", [128, 16384])
    io_a1.update(_attn_gather_tensors(nc, "1"))

    io_f1 = dict(x_in=x3, x_out=xo,
                 wr=din("f1_wr", [1024, 1024]), wk=din("f1_wk", [1024, 1024]),
                 wv=din("f1_wv", [1024, 1024]), mus=din("f1_mus", [128, 16]),
                 msk=din("f1_msk", [128, 64]))
    io_f1["bc_in"] = nc.dram_tensor("bc_in", [128, 8], F32).ap()
    io_f1["bc_g"] = nc.dram_tensor("bc_g", [1024, 8], F32).ap()

    consts = Consts(p)
    od = None
    for emitter, io in ((emit_a0, io_a0), (emit_f0, io_f0),
                        (emit_a1, io_a1), (emit_f1, io_f1)):
        ds = p.scope()
        p._default_scope = ds
        od = emitter(p, consts, io)
        p._default_scope = None
        p.barrier()
        ds.close()
    return p.finalize(final_waits=[(od, od.n)])


def stage_fused(inputs, xs):
    m_a0 = stage_a0(inputs, xs)
    m_f0 = stage_f0(inputs, xs)
    m_a1 = stage_a1(inputs, xs)
    m_f1 = stage_f1(inputs, xs)
    maps = []
    for c in range(NCORES):
        d = {"x": xs[c]}
        for k, v in m_a0[c].items():
            if k in ("wq", "wk", "wv", "wl", "wo"):
                d["a0_" + k] = v
            elif k == "bias_all":
                d["a0_bias"] = v
            elif k.startswith("c_"):
                d[k] = v
        for k, v in m_f0[c].items():
            if k in ("wg_shard", "wu_shard", "wd_shard"):
                d[k] = v
        for k, v in m_a1[c].items():
            if k in ("wdq_w", "wuq_w", "wqr_w", "wkr_w", "wdkv_w",
                     "wuk_w", "wuv_w", "wo_w"):
                d["a1_" + k] = v
            elif k == "bias_all":
                d["a1_bias"] = v
            elif k == "cs":
                d["a1_cs"] = v
        for k, v in m_f1[c].items():
            if k in ("wr", "wk", "wv", "mus", "msk"):
                d["f1_" + k] = v
        maps.append(d)
    return maps


def kernel_fused(**inputs):
    xs = shard_x(inputs["L0_attn_x"])
    res = run_spmd(build_fused_launch(), stage_fused(inputs, xs))
    return unshard_x([np.asarray(r["xo"], np.float32) for r in res])


QMIDS = (0.25, 0.75, 1.25, 1.75, 2.5, 3.5, 5.0)
QSTEPS = (0.5, 0.5, 0.5, 0.5, 1.0, 1.0, 2.0)


def emit_quant_pool(p, src, dst, nblk, qs, done=None, eng="gpsimd",
                    background=False, src_ready=None):
    absx, y, ta, tb, tc, sgn = (qs["absx"], qs["y"], qs["ta"], qs["tb"],
                                qs["tc"], qs["sgn"])
    s, rv = qs["s"], qs["rv"]
    n = 16 * nblk
    P = src.shape[0]
    bg = background
    qvr = p.ctr("qvr")
    qvs = p.ctr("qvs")

    def dv(emit, inc=None):
        p.op("vector", emit, inc=inc, background=bg)

    def pl(emit, inc=None):
        p.op(eng, emit, inc=inc, background=bg)

    if done is not None:
        p.wait("vector", done, done.n, background=bg)
    if src_ready is not None:
        p.wait("vector", src_ready[0], src_ready[1], background=bg)
    dv((lambda o, i: lambda e: e.tensor_single_scalar(
        o, i, 0x7FFFFFFF, ALU.bitwise_and))(
        absx[:P, 0:n].bitcast(I32), src.bitcast(I32)))
    dv((lambda o, i: lambda e: e.tensor_reduce(
        o, i, AX.X, ALU.max))(
        s[:P, 0:nblk], absx[:P, 0:n].rearrange("p (b k) -> p b k", k=16)),
        inc=qvs)
    p.wait("vector", qvs, qvs.n, background=bg)
    dv((lambda o, i: lambda e: e.tensor_scalar(
        o, i, 1.0 / 6.0, 1e-12, ALU.mult, ALU.max))(s[:P, 0:nblk],
                                                    s[:P, 0:nblk]))
    dv((lambda o, i: lambda e: e.tensor_single_scalar(
        o, i, 0.0, ALU.is_gt))(sgn[:P, 0:n], src))
    dv((lambda o, i: lambda e: e.tensor_single_scalar(
        o, i, 0.0, ALU.is_lt))(y[:P, 0:n], src))
    dv((lambda o, a, b2: lambda e: e.tensor_tensor(
        o, a, b2, ALU.subtract))(sgn[:P, 0:n], sgn[:P, 0:n], y[:P, 0:n]),
        inc=qvs)
    dv((lambda o, i: lambda e: e.reciprocal(o, i))(rv[:P, 0:nblk],
                                                   s[:P, 0:nblk]), inc=qvr)
    p.wait("vector", qvr, qvr.n, background=bg)
    p.wait(eng, qvr, qvr.n, background=bg)
    pl((lambda o, i, r: lambda e: e.tensor_tensor(
        o, i, r, ALU.mult))(
        y[:P, 0:n].rearrange("p (b k) -> p b k", k=16),
        absx[:P, 0:n].rearrange("p (b k) -> p b k", k=16),
        rv[:P, 0:nblk].rearrange("p b -> p b ()").broadcast_to([P, nblk, 16])))
    pl((lambda o, i: lambda e: e.tensor_scalar(
        o, i, 2.0, 2.0, ALU.min, ALU.mult))(ta[:P, 0:n], y[:P, 0:n]))
    pl((lambda o, i: lambda e: e.tensor_scalar(
        o, i, MAGIC, MAGIC, ALU.add, ALU.subtract))(ta[:P, 0:n], ta[:P, 0:n]))
    pl((lambda o, i: lambda e: e.tensor_scalar(
        o, i, 2.0, MAGIC, ALU.max, ALU.add))(tb[:P, 0:n], y[:P, 0:n]))
    pl((lambda o, i: lambda e: e.tensor_scalar(
        o, i, MAGIC, 4.0, ALU.subtract, ALU.min))(tb[:P, 0:n], tb[:P, 0:n]))
    pl((lambda o, i: lambda e: e.tensor_scalar(
        o, i, 0.5, MAGIC, ALU.mult, ALU.add))(tc[:P, 0:n], y[:P, 0:n]))
    pl((lambda o, i: lambda e: e.tensor_scalar(
        o, i, MAGIC, 2.0, ALU.subtract, ALU.mult))(tc[:P, 0:n], tc[:P, 0:n]))
    pl((lambda o, i: lambda e: e.tensor_scalar(
        o, i, 4.0, -6.0, ALU.max, ALU.add))(tc[:P, 0:n], tc[:P, 0:n]))
    pl((lambda o, i: lambda e: e.tensor_single_scalar(
        o, i, 0.5, ALU.mult))(ta[:P, 0:n], ta[:P, 0:n]))
    pl((lambda o, a, b2: lambda e: e.tensor_tensor(
        o, a, b2, ALU.add))(ta[:P, 0:n], ta[:P, 0:n], tb[:P, 0:n]))
    pl((lambda o, a, b2: lambda e: e.tensor_tensor(
        o, a, b2, ALU.add))(ta[:P, 0:n], ta[:P, 0:n], tc[:P, 0:n]))
    pl((lambda o, a, sb: lambda e: e.tensor_tensor(
        o, a, sb, ALU.mult))(
        ta[:P, 0:n].rearrange("p (b k) -> p b k", k=16),
        ta[:P, 0:n].rearrange("p (b k) -> p b k", k=16),
        s[:P, 0:nblk].rearrange("p b -> p b ()").broadcast_to([P, nblk, 16])))
    p.wait(eng, qvs, qvs.n, background=bg)
    pl((lambda o, a, sg: lambda e: e.tensor_tensor(
        o, a, sg, ALU.mult))(dst, ta[:P, 0:n], sgn[:P, 0:n]), inc=done)


def emit_f0_wprep(p, io, *, background=False, scope=None, bufs=None):
    bg = background
    sc = scope
    if bufs is None:
        bufs = p.sbuf("wp_reg", [128, 1984], F32, sc)
    tile_in = bufs[:, 0:256]
    tile_out = bufs[:, 256:384].bitcast(BF16)
    qs = dict(
        absx=bufs[:, 384:640],
        y=bufs[:, 640:896],
        ta=bufs[:, 896:1152],
        tb=bufs[:, 1152:1408],
        tc=bufs[:, 1408:1664],
        sgn=bufs[:, 1664:1920],
        s=bufs[:, 1920:1936],
        rv=bufs[:, 1936:1952],
    )
    wtmp = None
    ld = p.ctr("wpld")
    qd = p.ctr("wpqd")
    st = p.ctr("wpst")
    pieces = []
    for name_in, name_q in (("wg_shard", "wgq_in"), ("wu_shard", "wuq_in")):
        for r0, nr in ((0, 128), (128, 128), (256, 96)):
            for c0 in range(0, 1024, 256):
                pieces.append((name_in, name_q, r0, nr, c0, 256))
    for c0 in range(0, 2816, 256):
        pieces.append(("wd_shard", "wdq_in", 0, 128, c0, 256))
    for name_in, name_q, r0, nr, c0, nc in pieces:
        p.wait("gpsimd", qd, qd.n, background=bg)
        p.op("gpsimd", (lambda name_in, r0, nr, c0, nc: lambda e: e.dma_start(
            out=tile_in[0:nr, 0:nc], in_=io[name_in][r0:r0 + nr, c0:c0 + nc]))(
            name_in, r0, nr, c0, nc), inc=ld, dma=True, background=bg)
        emit_quant_pool(p, tile_in[0:nr, 0:nc], tile_out[0:nr, 0:nc],
                        nc // 16, qs, done=qd, background=bg,
                        src_ready=(ld, ld.n))
        p.op("gpsimd", (lambda name_q, r0, nr, c0, nc: lambda e: e.dma_start(
            out=io[name_q][r0:r0 + nr, c0:c0 + nc],
            in_=tile_out[0:nr, 0:nc]))(name_q, r0, nr, c0, nc),
            inc=st, dma=True, background=bg)
    ccs = [p.ctr("wpcc", unique=True) for _ in range(3)]
    grp8 = [[0, 1, 2, 3, 4, 5, 6, 7]]
    p.wait("gpsimd", st, st.n, background=bg)
    for i, (a, b2) in enumerate((("wgq_in", "wgq_g"), ("wuq_in", "wuq_g"),
                                 ("wdq_in", "wdq_g"))):
        p.op("gpsimd", (lambda a, b2: lambda e: e.collective_compute(
            "AllGather", ALU.bypass, replica_groups=grp8,
            ins=[io[a].opt()], outs=[io[b2].opt()]))(a, b2),
            inc=(ccs[i], 1), background=bg)
    return dict(cc0=(ccs[0], 1), cc1=(ccs[1], 1), cc2=(ccs[2], 1))



# version 13
# speedup vs baseline: 1.0965x; 1.0965x over previous
import contextlib
import math
import numpy as np
import ml_dtypes
import concourse.bass as bass
import concourse.mybir as mybir
from concourse.bass_utils import run_bass_kernel_spmd

F32 = mybir.dt.float32
BF16 = mybir.dt.bfloat16
I32 = mybir.dt.int32
AF = mybir.ActivationFunctionType
ALU = mybir.AluOpType
AX = mybir.AxisListType

NCORES = 8
D = 1024
NCH = 8
TC = 2048
TB = 512
NTB = TC // TB
NH = 16
HD = 64
IF = 2816
ICH = IF // 128
MAGIC = float(12582912.0)
ENGINES = ("tensor", "vector", "scalar", "gpsimd", "sync")

bf16 = ml_dtypes.bfloat16


class Ctr:
    def __init__(self, sem):
        self.sem = sem
        self.n = 0


class Prog:
    def __init__(self):
        self.nc = bass.Bass("TRN2", target_bir_lowering=False)
        self.records = {e: [] for e in ENGINES}
        self.stack = contextlib.ExitStack()
        self.pending_waits = {e: [] for e in ENGINES}
        self.phase_ctrs = set()
        self.dirty = {e: False for e in ENGINES}
        self.last_rec = {e: None for e in ENGINES}
        self._semcnt = 0

    def _uname(self, name):
        self._tname = getattr(self, "_tname", 0) + 1
        return f"{name}_{self._tname}"

    def sbuf(self, name, shape, dtype, scope=None):
        cm = self.nc.sbuf_tensor(self._uname(name), shape, dtype)
        tgt = scope or getattr(self, "_default_scope", None) or self.stack
        return tgt.enter_context(cm)

    def psum(self, name, shape, dtype=F32, scope=None):
        cm = self.nc.psum_tensor(self._uname(name), shape, dtype)
        tgt = scope or getattr(self, "_default_scope", None) or self.stack
        return tgt.enter_context(cm)

    def ctr(self, name=None, unique=False):
        if not hasattr(self, "_ctrs"):
            self._ctrs = {}
        if name is not None and not unique and name in self._ctrs:
            return self._ctrs[name]
        self._semcnt += 1
        sem = self.stack.enter_context(
            self.nc.semaphore(f"{name or 's'}{self._semcnt}"))
        c = Ctr(sem)
        if name is not None and not unique:
            self._ctrs[name] = c
        return c

    def scope(self):
        return self.stack.enter_context(contextlib.ExitStack())

    def op(self, engine, emit, inc=None, dma=False, background=False):
        incs = []
        if inc is not None:
            ctr, amt = inc if isinstance(inc, tuple) else (inc, 16 if dma else 1)
            ctr.n += amt
            incs.append((ctr.sem, amt))
            if not background:
                self.phase_ctrs.add(ctr)
        rec = ["op", emit, incs, dma]
        if not background and self.pending_waits[engine]:
            for sem, cnt in self.pending_waits[engine]:
                self.records[engine].append(["wait", sem, cnt])
            self.pending_waits[engine] = []
        self.records[engine].append(rec)
        if not background:
            self.dirty[engine] = True
            self.last_rec[engine] = rec
        return rec

    def wait(self, engine, ctr, count=None, background=False):
        if not background and self.pending_waits[engine]:
            for sem, cnt in self.pending_waits[engine]:
                self.records[engine].append(["wait", sem, cnt])
            self.pending_waits[engine] = []
        self.records[engine].append(
            ["wait", ctr.sem, ctr.n if count is None else count])

    def barrier(self):
        targets = [(c.sem, c.n) for c in self.phase_ctrs]
        bar = None
        for e in ENGINES:
            if not self.dirty[e]:
                continue
            rec = self.last_rec[e]
            if rec is not None and not rec[2]:
                if bar is None:
                    bar = self.ctr("bar")
                amt = 16 if rec[3] else 1
                bar.n += amt
                rec[2].append((bar.sem, amt))
            self.dirty[e] = False
            self.last_rec[e] = None
        if bar is not None:
            targets = [t for t in targets if t[0] is not bar.sem]
            targets.append((bar.sem, bar.n))
        if targets:
            for e in ENGINES:
                self.pending_waits[e] = list(targets)
        self.phase_ctrs = set()

    def finalize(self, final_waits=()):
        block = self.stack.enter_context(self.nc.Block())

        def make(engname):
            def fn(eng):
                for rec in self.records[engname]:
                    if rec[0] == "wait":
                        eng.wait_ge(rec[1], rec[2])
                    else:
                        instr = rec[1](eng)
                        for sem, amt in rec[2]:
                            instr = instr.then_inc(sem, amt)
            return fn

        for ctr, cnt in final_waits:
            self.records["sync"].append(["wait", ctr.sem, cnt])
        block.tensor(make("tensor"))
        block.vector(make("vector"))
        block.scalar(make("scalar"))
        block.gpsimd(make("gpsimd"))
        block.sync(make("sync"))
        self.stack.close()
        return self.nc


CONST_SPEC = {
    "id128": ([128, 128], np.float32),
    "id128b": ([128, 128], np.float32),
    "onescol_b": ([128, 1], np.float32),
    "onesrow_f": ([1, 128], np.float32),
    "onesrow_b": ([1, 128], np.float32),
}


def make_const_arrays():
    return {
        "id128": np.eye(128, dtype=np.float32),
        "id128b": np.eye(128, dtype=np.float32),
        "onescol_b": np.ones((128, 1), np.float32),
        "onesrow_f": np.ones((1, 128), np.float32),
        "onesrow_b": np.ones((1, 128), np.float32),
    }


class Consts:

    def __init__(self, p):
        self.aps = {}
        for name, (shape, _) in CONST_SPEC.items():
            self.aps[name] = p.nc.dram_tensor(
                "c_" + name, shape, F32, kind="ExternalInput").ap()
        self.id128 = p.sbuf("id128", [128, 128], F32)
        self.id128b = p.sbuf("id128b", [128, 128], BF16)
        self.onescol_b = p.sbuf("onescol_b", [128, 1], BF16)
        self.onesrow_f = p.sbuf("onesrow_f", [1, 128], F32)
        self.onesrow_b = p.sbuf("onesrow_b", [1, 128], BF16)
        lds = p.ctr("clds")
        ldg = p.ctr("cldg")
        p.op("sync", lambda e: e.dma_start(out=self.id128[:, :],
                                           in_=self.aps["id128"]), inc=lds, dma=True)
        p.op("gpsimd", lambda e: e.dma_start(out=self.id128b[:, :],
                                             in_=self.aps["id128b"]), inc=ldg, dma=True)
        p.op("gpsimd", lambda e: e.dma_start(out=self.onescol_b[:, :],
                                             in_=self.aps["onescol_b"]), inc=ldg, dma=True)
        p.op("sync", lambda e: e.dma_start(out=self.onesrow_f[:, :],
                                           in_=self.aps["onesrow_f"]), inc=lds, dma=True)
        p.op("gpsimd", lambda e: e.dma_start(out=self.onesrow_b[:, :],
                                             in_=self.aps["onesrow_b"]), inc=ldg, dma=True)


def load_weight_cast(p, dram_ap, sb, cin_chunks, cout, ctr, col0=0):
    src = dram_ap.rearrange("(c p) f -> p c f", p=128)
    dst = sb[:, col0:col0 + cin_chunks * cout].rearrange(
        "p (c f) -> p c f", c=cin_chunks)
    p.op("gpsimd", (lambda dst, src: lambda e: e.dma_start(out=dst, in_=src))(
        dst, src), inc=ctr, dma=True)


def load_vec(p, dram_ap, sb, ctr):
    p.op("sync", (lambda sb, dram_ap: lambda e: e.dma_start(
        out=sb[:, :], in_=dram_ap))(sb, dram_ap), inc=ctr, dma=True)


def emit_proj(p, *, w_sb, cout, cin_chunks, cout_chunks, rhs_sb, rhs_col0,
              out_cb, pp, n_tok=TC, tbs=TB, w_col0=0, tb_outer=False,
              rhs_tok=None):
    if rhs_tok is None:
        rhs_tok = n_tok
    mm = p.ctr("pmm")
    ev = p.ctr("pev")
    b_ev = ev.n
    g = 0
    ntb = n_tok // tbs
    loop = ([(f, tb) for tb in range(ntb) for f in range(cout_chunks)]
            if tb_outer else
            [(f, tb) for f in range(cout_chunks) for tb in range(ntb)])
    for f, tb in loop:
        bank = pp[:, (g % 2) * tbs:(g % 2) * tbs + tbs]
        if g >= 2:
            p.wait("tensor", ev, b_ev + g - 1)
        for ci in range(cin_chunks):
            wsl = w_sb[:, w_col0 + ci * cout + f * 128:
                       w_col0 + ci * cout + f * 128 + 128]
            rsl = rhs_sb[:, rhs_col0 + ci * rhs_tok + tb * tbs:
                         rhs_col0 + ci * rhs_tok + tb * tbs + tbs]
            last = ci == cin_chunks - 1
            p.op("tensor",
                 (lambda bank, wsl, rsl, first, last:
                  lambda e: e.matmul(bank, wsl, rsl, start=first, stop=last)
                  )(bank, wsl, rsl, ci == 0, last),
                 inc=mm if last else None)
        out_cb(p, bank, f, tb * tbs, tbs, mm, ev)
        assert ev.n == b_ev + g + 1, "evac callback must inc ev exactly once"
        g += 1
    return mm, ev


def evac_act(out_sb, out_col0, bias_sb=None, func=AF.Identity, scale=1.0,
             chunk_stride=TC):
    def cb(p, bank, f, t0, nt, mm, ev):
        osl = out_sb[:, out_col0 + f * chunk_stride + t0:
                     out_col0 + f * chunk_stride + t0 + nt]
        p.wait("scalar", mm)
        if bias_sb is not None:
            bsl = bias_sb[:, f:f + 1]
            p.op("scalar", (lambda o, b, bs: lambda e: e.activation(
                o, b, func, bias=bs, scale=scale))(osl, bank, bsl), inc=ev)
        else:
            p.op("scalar", (lambda o, b: lambda e: e.activation(
                o, b, func, scale=scale))(osl, bank), inc=ev)
    return cb


def evac_resid(s_sb, bias_sb, x_sb, x_col0=0, chunk_stride=TC):
    def cb(p, bank, f, t0, nt, mm, ev):
        osl = s_sb[:, f * chunk_stride + t0: f * chunk_stride + t0 + nt]
        xsl = x_sb[:, x_col0 + f * chunk_stride + t0:
                   x_col0 + f * chunk_stride + t0 + nt]
        p.wait("vector", mm)
        if bias_sb is not None:
            bsl = bias_sb[:, f:f + 1]
            p.op("vector", (lambda o, b, bs, x: lambda e: e.scalar_tensor_tensor(
                o, b, bs, x, ALU.add, ALU.add))(osl, bank, bsl, xsl), inc=ev)
        else:
            p.op("vector", (lambda o, b, x: lambda e: e.tensor_tensor(
                o, b, x, ALU.add))(osl, bank, xsl), inc=ev)
    return cb


def emit_ln(p, consts, *, s_sb, out_sb, sc):
    bf_a, bf_b, rows, ps, ps_row = (sc["bf_a"], sc["bf_b"], sc["rows"],
                                    sc["ps"], sc["ps_row"])
    for c in range(NCH):
        sl = s_sb[:, c * TC:(c + 1) * TC]
        p.op("vector", (lambda a, b: lambda e: e.tensor_copy(a, b))(
            bf_a[:, c * TC:(c + 1) * TC], sl))
        p.op("vector", (lambda o, a: lambda e: e.tensor_tensor(
            o, a, a, ALU.mult))(bf_b[:, c * TC:(c + 1) * TC], sl))
    p.barrier()
    for which, src_sb in ((0, bf_a), (1, bf_b)):
        for tb in range(NTB):
            psl = ps_row[0:1, (tb % 4) * TB:(tb % 4) * TB + TB]
            for c in range(NCH):
                p.op("tensor", (lambda psl, a, first, last: lambda e: e.matmul(
                    psl, consts.onescol_b[:, 0:1], a, start=first, stop=last))(
                    psl,
                    src_sb[:, c * TC + tb * TB: c * TC + tb * TB + TB],
                    c == 0, c == NCH - 1))
            p.barrier()
            p.op("vector", (lambda o, i: lambda e: e.tensor_copy(o, i))(
                rows[0:1, which * TC + tb * TB: which * TC + tb * TB + TB], psl))
            p.barrier()
    p.op("vector", lambda e: e.tensor_single_scalar(
        rows[0:1, 0:TC], rows[0:1, 0:TC], 1.0 / D, ALU.mult))
    p.op("vector", lambda e: e.tensor_single_scalar(
        rows[0:1, TC:2 * TC], rows[0:1, TC:2 * TC], 1.0 / D, ALU.mult))
    p.barrier()
    p.op("vector", lambda e: e.tensor_tensor(
        rows[0:1, 2 * TC:3 * TC], rows[0:1, 0:TC], rows[0:1, 0:TC], ALU.mult))
    p.barrier()
    p.op("vector", lambda e: e.scalar_tensor_tensor(
        rows[0:1, 2 * TC:3 * TC], rows[0:1, 2 * TC:3 * TC], -1.0,
        rows[0:1, TC:2 * TC], ALU.mult, ALU.add))
    p.barrier()
    p.op("vector", lambda e: e.tensor_single_scalar(
        rows[0:1, 2 * TC:3 * TC], rows[0:1, 2 * TC:3 * TC], 1e-5, ALU.add))
    p.barrier()
    p.op("scalar", lambda e: e.activation(
        rows[0:1, 2 * TC:3 * TC], rows[0:1, 2 * TC:3 * TC], AF.Ln))
    p.op("scalar", lambda e: e.activation(
        rows[0:1, 2 * TC:3 * TC], rows[0:1, 2 * TC:3 * TC], AF.Exp, scale=-0.5))
    p.barrier()
    p.op("vector", lambda e: e.scalar_tensor_tensor(
        rows[0:1, 0:TC], rows[0:1, 0:TC], -1.0, rows[0:1, 2 * TC:3 * TC],
        ALU.mult, ALU.mult))
    p.barrier()
    for tb in range(NTB):
        pa = ps[:, 0:TB]
        pb = ps[:, TB:2 * TB]
        p.op("tensor", (lambda pa, tb: lambda e: e.matmul(
            pa, consts.onesrow_f[0:1, :],
            rows[0:1, 2 * TC + tb * TB: 2 * TC + tb * TB + TB],
            start=True, stop=True))(pa, tb))
        p.op("tensor", (lambda pb, tb: lambda e: e.matmul(
            pb, consts.onesrow_f[0:1, :],
            rows[0:1, tb * TB: tb * TB + TB],
            start=True, stop=True))(pb, tb))
        p.barrier()
        for c in range(NCH):
            ssl = s_sb[:, c * TC + tb * TB: c * TC + tb * TB + TB]
            osl = out_sb[:, c * TC + tb * TB: c * TC + tb * TB + TB]
            p.op("vector", (lambda o, s, pa: lambda e: e.tensor_tensor(
                o, s, pa, ALU.mult))(osl, ssl, pa))
            p.op("vector", (lambda o, pb: lambda e: e.tensor_tensor(
                o, o, pb, ALU.add))(osl, pb))
        p.barrier()


def emit_ln_fast(p, consts, *, s_sb, out_sb, sc):
    bf_a, bf_b, rows, ps, ps_row = (sc["bf_a"], sc["bf_b"], sc["rows"],
                                    sc["ps"], sc["ps_row"])
    lcp = p.ctr("lcp")
    lmm = p.ctr("lmm")
    lev = p.ctr("lev")
    lst = p.ctr("lst")
    lex = p.ctr("lex")
    lbr = p.ctr("lbr")
    lbc = p.ctr("lbc")
    lap = p.ctr("lap")
    b_lcp, b_lmm, b_lev = lcp.n, lmm.n, lev.n
    b_lbc, b_lap = lbc.n, lap.n
    for c in range(NCH):
        sl = s_sb[:, c * TC:(c + 1) * TC]
        p.op("vector", (lambda a, b: lambda e: e.tensor_copy(a, b))(
            bf_a[:, c * TC:(c + 1) * TC], sl))
        p.op("vector", (lambda o, a: lambda e: e.tensor_tensor(
            o, a, a, ALU.mult))(bf_b[:, c * TC:(c + 1) * TC], sl),
            inc=lcp)
    p.wait("tensor", lcp, b_lcp + NCH)
    groups = [(0, tb) for tb in range(NTB)] + [(1, tb) for tb in range(NTB)]
    for j, (which, tb) in enumerate(groups):
        src_sb = bf_a if which == 0 else bf_b
        psl = ps_row[0:1, (j % 4) * TB:(j % 4) * TB + TB]
        if j >= 4:
            p.wait("tensor", lev, b_lev + j - 3)
        for c in range(NCH):
            p.op("tensor", (lambda psl, a, first, last: lambda e: e.matmul(
                psl, consts.onescol_b[:, 0:1], a, start=first, stop=last))(
                psl,
                src_sb[:, c * TC + tb * TB: c * TC + tb * TB + TB],
                c == 0, c == NCH - 1), inc=lmm if c == NCH - 1 else None)
        p.wait("vector", lmm, b_lmm + j + 1)
        p.op("vector", (lambda o, psl: lambda e: e.tensor_copy(o, psl))(
            rows[0:1, which * TC + tb * TB: which * TC + tb * TB + TB], psl),
            inc=lev)
    p.op("vector", lambda e: e.tensor_single_scalar(
        rows[0:1, 0:TC], rows[0:1, 0:TC], 1.0 / D, ALU.mult))
    p.op("vector", lambda e: e.tensor_single_scalar(
        rows[0:1, TC:2 * TC], rows[0:1, TC:2 * TC], 1.0 / D, ALU.mult))
    p.op("vector", lambda e: e.tensor_tensor(
        rows[0:1, 2 * TC:3 * TC], rows[0:1, 0:TC], rows[0:1, 0:TC], ALU.mult))
    p.op("vector", lambda e: e.scalar_tensor_tensor(
        rows[0:1, 2 * TC:3 * TC], rows[0:1, 2 * TC:3 * TC], -1.0,
        rows[0:1, TC:2 * TC], ALU.mult, ALU.add))
    p.op("vector", lambda e: e.tensor_single_scalar(
        rows[0:1, 2 * TC:3 * TC], rows[0:1, 2 * TC:3 * TC], 1e-5, ALU.add),
        inc=lst)
    p.wait("scalar", lst)
    p.op("scalar", lambda e: e.activation(
        rows[0:1, 2 * TC:3 * TC], rows[0:1, 2 * TC:3 * TC], AF.Ln))
    p.op("scalar", lambda e: e.activation(
        rows[0:1, 2 * TC:3 * TC], rows[0:1, 2 * TC:3 * TC], AF.Exp,
        scale=-0.5), inc=lex)
    p.wait("vector", lex)
    p.op("vector", lambda e: e.scalar_tensor_tensor(
        rows[0:1, 0:TC], rows[0:1, 0:TC], -1.0, rows[0:1, 2 * TC:3 * TC],
        ALU.mult, ALU.mult), inc=lbr)
    p.wait("tensor", lex)
    p.wait("tensor", lbr)
    for tb in range(NTB):
        sl = (tb % 2) * 1024
        pa = ps[:, sl:sl + TB]
        pb = ps[:, sl + TB:sl + 2 * TB]
        if tb >= 2:
            p.wait("tensor", lap, b_lap + (tb - 1) * NCH)
        p.op("tensor", (lambda pa, tb: lambda e: e.matmul(
            pa, consts.onesrow_f[0:1, :],
            rows[0:1, 2 * TC + tb * TB: 2 * TC + tb * TB + TB],
            start=True, stop=True))(pa, tb))
        p.op("tensor", (lambda pb, tb: lambda e: e.matmul(
            pb, consts.onesrow_f[0:1, :],
            rows[0:1, tb * TB: tb * TB + TB],
            start=True, stop=True))(pb, tb), inc=(lbc, 1))
        p.wait("vector", lbc, b_lbc + tb + 1)
        for c in range(NCH):
            ssl = s_sb[:, c * TC + tb * TB: c * TC + tb * TB + TB]
            osl = out_sb[:, c * TC + tb * TB: c * TC + tb * TB + TB]
            p.op("vector", (lambda o, s, pa: lambda e: e.tensor_tensor(
                o, s, pa, ALU.mult))(osl, ssl, pa))
            p.op("vector", (lambda o, pb: lambda e: e.tensor_tensor(
                o, o, pb, ALU.add))(osl, pb), inc=lap)


def emit_absmax_cell(p, consts, src_sb, col0, n_chunks, n_tok, scr, psc, out_cell):
    for c in range(n_chunks):
        p.op("vector", (lambda o, i: lambda e: e.tensor_reduce(
            o, i, AX.X, ALU.max, apply_absolute_value=True))(
            scr[:, c:c + 1], src_sb[:, col0 + c * n_tok: col0 + (c + 1) * n_tok]))
    p.barrier()
    p.op("vector", lambda e: e.tensor_reduce(
        scr[:, n_chunks:n_chunks + 1], scr[:, 0:n_chunks], AX.X, ALU.max))
    p.barrier()
    p.op("tensor", lambda e: e.transpose(
        psc[0:1, 0:128], scr[:, n_chunks:n_chunks + 1], consts.id128[:, :]))
    p.barrier()
    p.op("vector", lambda e: e.tensor_reduce(
        out_cell, psc[0:1, 0:128], AX.X, ALU.max))
    p.barrier()


def emit_flash(p, consts, *, q_sb, kg_ap, vg_ap, negb_sb, scale, attn_sb,
               fl, n_blocks=4):
    TK = n_blocks * 2048
    NKT = TK // 128
    kbuf, vbuf, pT, rec, otmp = fl["kbuf"], fl["vbuf"], fl["pT"], fl["rec"], fl["otmp"]
    S = (fl["S0"], fl["S1"])
    O = fl["O"]
    kv = [p.ctr("kv0"), p.ctr("kv1")]
    mm = p.ctr("fmm")
    ex = p.ctr("fex")
    pv = p.ctr("fpv")
    rc = p.ctr("frc")
    bc = p.ctr("fbc")
    oc = p.ctr("foc")
    ev = p.ctr("fev")
    b_kv = [kv[0].n, kv[1].n]
    b_mm, b_ex, b_pv = mm.n, ex.n, pv.n
    b_rc, b_bc, b_oc, b_ev = rc.n, bc.n, oc.n, ev.n

    def issue_loads(hp):
        b = hp % 2
        for r in range(n_blocks):
            src = kg_ap[r * 1024 + hp * 128: r * 1024 + hp * 128 + 128, :]
            dst = kbuf[:, b * TK + r * 2048: b * TK + r * 2048 + 2048]
            p.op("sync", (lambda dst, src: lambda e: e.dma_start(
                out=dst, in_=src))(dst, src), inc=kv[b], dma=True)
        vsrc = vg_ap.rearrange("(k p) c -> p k c", p=128)[
            :, :, hp * 130:(hp + 1) * 130]
        vdst = vbuf[:, b * NKT * 130:(b + 1) * NKT * 130].rearrange(
            "p (k c) -> p k c", c=130)
        p.op("sync", (lambda vdst, vsrc: lambda e: e.dma_start(
            out=vdst, in_=vsrc))(vdst, vsrc), inc=kv[b], dma=True)

    issue_loads(0)
    deferred = [None]

    def flush_evac(u):
        o1, o2 = O[2 * (u % 2)], O[2 * (u % 2) + 1]
        ru = rec[0:1, (u % 2) * 1024:(u % 2) * 1024 + 1024]
        hp, qb = u // 4, u % 4
        p.wait("tensor", rc, b_rc + 2 * (u + 1))
        p.op("tensor", (lambda o1, ru: lambda e: e.matmul(
            o1[64:128, :], consts.onesrow_b[0:1, 0:64], ru[0:1, 0:512],
            start=True, stop=True))(o1, ru))
        p.op("tensor", (lambda o2, ru: lambda e: e.matmul(
            o2[64:128, :], consts.onesrow_b[0:1, 0:64], ru[0:1, 512:1024],
            start=True, stop=True))(o2, ru), inc=(bc, 2))
        ot = otmp[:, (u % 2) * 512:(u % 2) * 512 + 512]
        p.wait("scalar", bc, b_bc + 2 * (u + 1))
        p.op("scalar", (lambda ot, o1: lambda e: e.activation(
            ot[0:64, :], o1[0:64, :], AF.Copy))(ot, o1))
        p.op("scalar", (lambda ot, o2: lambda e: e.activation(
            ot[64:128, :], o2[0:64, :], AF.Copy))(ot, o2), inc=(oc, 2))
        a0 = attn_sb[:, hp * TC + qb * 512: hp * TC + qb * 512 + 512]
        p.wait("vector", oc, b_oc + 2 * (u + 1))
        p.op("vector", (lambda a0, ot, o1: lambda e: e.tensor_tensor(
            a0[0:64, :], ot[0:64, :], o1[64:128, :], ALU.mult))(a0, ot, o1))
        p.op("vector", (lambda a0, ot, o2: lambda e: e.tensor_tensor(
            a0[64:128, :], ot[64:128, :], o2[64:128, :], ALU.mult))(a0, ot, o2),
            inc=ev)

    for hp in range(8):
        b = hp % 2
        if hp + 1 < 8:
            p.wait("sync", pv, b_pv + 256 * hp)
            issue_loads(hp + 1)
        for qb in range(4):
            u = hp * 4 + qb
            o1, o2 = O[2 * (u % 2)], O[2 * (u % 2) + 1]
            qsl = q_sb[:, hp * TC + qb * 512: hp * TC + qb * 512 + 512]
            vb = vbuf[:, b * NKT * 130:(b + 1) * NKT * 130]
            kb = kbuf[:, b * TK:(b + 1) * TK]

            def emit_pv(j, o1=o1, o2=o2, u=u, vb=vb):
                gj = u * NKT + j
                ptj = pT[:, (gj % 2) * 1024:(gj % 2) * 1024 + 1024]
                p.wait("tensor", ex, b_ex + gj + 1)
                if j == 0 and u >= 2:
                    p.wait("tensor", ev, b_ev + u - 1)
                p.op("tensor", (lambda o1, vbs, pt, first, last: lambda e:
                     e.matmul(o1[0:65, :], vbs[:, 0:65], pt[:, 0:512],
                              start=first, stop=last))(
                    o1, vb[:, j * 130:j * 130 + 130], ptj, j == 0,
                    j == NKT - 1))
                p.op("tensor", (lambda o2, vbs, pt, first, last: lambda e:
                     e.matmul(o2[0:65, :], vbs[:, 65:130], pt[:, 512:1024],
                              start=first, stop=last))(
                    o2, vb[:, j * 130:j * 130 + 130], ptj, j == 0,
                    j == NKT - 1), inc=pv)

            for kt in range(NKT):
                g = u * NKT + kt
                pr = S[g % 2]
                if deferred[0] is not None and kt == 2:
                    flush_evac(deferred[0])
                    deferred[0] = None
                if kt == 0:
                    p.wait("tensor", kv[b], b_kv[b] + 80 * (hp // 2 + 1))
                if g >= 2:
                    p.wait("tensor", ex, b_ex + g - 1)
                for (tp, prow, pcol, osl) in (
                        ((0, 0), 0, 0, pr[0:64, 0:512]),
                        ((0, 64), 0, 64, pr[64:128, 0:512]),
                        ((64, 0), 64, 0, pr[0:64, 512:1024]),
                        ((64, 64), 64, 64, pr[64:128, 512:1024])):
                    lhs = kb[prow:prow + 64, kt * 128 + pcol: kt * 128 + pcol + 64]
                    rhs = qsl[prow:prow + 64, :]
                    last = tp == (64, 64)
                    p.op("tensor", (lambda o, l, r, tp: lambda e: e.matmul(
                        o, l, r, start=True, stop=True, tile_position=tp))(
                        osl, lhs, rhs, tp), inc=mm if last else None)
                p.wait("scalar", mm, b_mm + g + 1)
                pt = pT[:, (g % 2) * 1024:(g % 2) * 1024 + 1024]
                p.op("scalar", (lambda pt, pr: lambda e: e.activation(
                    pt, pr[:, 0:1024], AF.Exp, bias=negb_sb[:, 0:1],
                    scale=scale))(pt, pr), inc=ex)
                if kt >= 1:
                    emit_pv(kt - 1)
            emit_pv(NKT - 1)
            p.wait("vector", pv, b_pv + (u + 1) * NKT)
            ru = rec[0:1, (u % 2) * 1024:(u % 2) * 1024 + 1024]
            recf = fl["recf"]
            frc2 = p.ctr("frc2")
            p.op("vector", (lambda recf, o1: lambda e: e.reciprocal(
                recf[0:1, 0:512], o1[64:65, :]))(recf, o1))
            p.op("vector", (lambda recf, o2: lambda e: e.reciprocal(
                recf[0:1, 512:1024], o2[64:65, :]))(recf, o2), inc=frc2)
            p.wait("vector", frc2)
            p.op("vector", (lambda ru, recf: lambda e: e.tensor_copy(
                ru[0:1, 0:1024], recf[0:1, 0:1024]))(ru, recf), inc=(rc, 2))
            deferred[0] = u
    flush_evac(deferred[0])
    deferred[0] = None
    p.barrier()


def alloc_flash(p, scope, kbuf_bf16, big_f32):
    return dict(
        kbuf=kbuf_bf16,
        vbuf=big_f32[:, 0:8320].bitcast(BF16),
        pT=big_f32[:, 8320:9344].bitcast(BF16),
        otmp=big_f32[:, 9344:10368],
        rec=big_f32[0:1, 10368:11392].bitcast(BF16),
        recf=big_f32[0:1, 11392:12416],
        S0=p.psum("fS0", [128, 1024], F32, scope),
        S1=p.psum("fS1", [128, 1024], F32, scope),
        O=[p.psum(f"fO{i}", [128, 512], F32, scope) for i in range(4)],
    )


def emit_quant2(p, src, dst, nblk, qs, src_ready=None, done=None):
    y, ta, tb_, tc_, sgn = qs["y"], qs["ta"], qs["tb_"], qs["tc_"], qs["sgn"]
    bm, sv, rv = qs["bm"], qs["sv"], qs["rv"]
    n = 16 * nblk
    P = src.shape[0]
    sg = p.ctr("qsg")
    if src_ready is not None:
        p.wait("scalar", src_ready[0], src_ready[1])
        p.wait("vector", src_ready[0], src_ready[1])
    if done is not None:
        p.wait("scalar", done, done.n)
    p.op("scalar", (lambda o, s: lambda e: e.activation(o, s, AF.Sign))(
        sgn[:P, 0:n], src), inc=sg)
    qrc = p.ctr("qrc")
    p.op("vector", (lambda o, s: lambda e: e.tensor_reduce(
        o, s, AX.X, ALU.max, apply_absolute_value=True))(
        bm[:P, 0:nblk], src.rearrange("p (b k) -> p b k", k=16)), inc=qrc)
    p.wait("vector", qrc)
    p.op("vector", (lambda o, i: lambda e: e.tensor_scalar(
        o, i, 1.0 / 6.0, 1e-12, ALU.mult, ALU.max))(sv[:P, 0:nblk], bm[:P, 0:nblk]))
    p.op("vector", (lambda o, i: lambda e: e.reciprocal(o, i))(
        rv[:P, 0:nblk], sv[:P, 0:nblk]), inc=qrc)
    p.op("vector", (lambda o, i: lambda e: e.tensor_single_scalar(
        o, i, 0x7FFFFFFF, ALU.bitwise_and))(
        y[:P, 0:n].bitcast(I32), src.bitcast(I32)))
    p.wait("vector", qrc)
    p.op("vector", (lambda o, i, r: lambda e: e.tensor_tensor(
        o, i, r, ALU.mult))(
        y[:P, 0:n].rearrange("p (b k) -> p b k", k=16),
        y[:P, 0:n].rearrange("p (b k) -> p b k", k=16),
        rv[:P, 0:nblk].rearrange("p b -> p b ()").broadcast_to([P, nblk, 16])))
    p.op("vector", (lambda o, i: lambda e: e.tensor_scalar(
        o, i, 2.0, 2.0, ALU.min, ALU.mult))(ta[:P, 0:n], y[:P, 0:n]))
    p.op("vector", (lambda o, i: lambda e: e.tensor_scalar(
        o, i, MAGIC, MAGIC, ALU.add, ALU.subtract))(ta[:P, 0:n], ta[:P, 0:n]))
    p.op("vector", (lambda o, i: lambda e: e.tensor_scalar(
        o, i, 2.0, MAGIC, ALU.max, ALU.add))(tb_[:P, 0:n], y[:P, 0:n]))
    p.op("vector", (lambda o, i: lambda e: e.tensor_scalar(
        o, i, MAGIC, 4.0, ALU.subtract, ALU.min))(tb_[:P, 0:n], tb_[:P, 0:n]))
    p.op("vector", (lambda o, i: lambda e: e.tensor_scalar(
        o, i, 0.5, MAGIC, ALU.mult, ALU.add))(tc_[:P, 0:n], y[:P, 0:n]))
    p.op("vector", (lambda o, i: lambda e: e.tensor_scalar(
        o, i, MAGIC, 2.0, ALU.subtract, ALU.mult))(tc_[:P, 0:n], tc_[:P, 0:n]))
    p.op("vector", (lambda o, i: lambda e: e.tensor_scalar(
        o, i, 4.0, -6.0, ALU.max, ALU.add))(tc_[:P, 0:n], tc_[:P, 0:n]))
    p.op("vector", (lambda o, a, b: lambda e: e.scalar_tensor_tensor(
        o, a, 0.5, b, ALU.mult, ALU.add))(ta[:P, 0:n], ta[:P, 0:n], tb_[:P, 0:n]))
    p.op("vector", (lambda o, a, c: lambda e: e.tensor_tensor(
        o, a, c, ALU.add))(ta[:P, 0:n], ta[:P, 0:n], tc_[:P, 0:n]))
    p.op("vector", (lambda o, a, s: lambda e: e.tensor_tensor(
        o, a, s, ALU.mult))(
        ta[:P, 0:n].rearrange("p (b k) -> p b k", k=16),
        ta[:P, 0:n].rearrange("p (b k) -> p b k", k=16),
        sv[:P, 0:nblk].rearrange("p b -> p b ()").broadcast_to([P, nblk, 16])))
    p.wait("vector", sg)
    p.op("vector", (lambda o, a, s: lambda e: e.tensor_tensor(
        o, a, s, ALU.mult))(dst, ta[:P, 0:n], sgn[:P, 0:n]), inc=done)


def alloc_quant_scratch(p, scope, n, nblk):
    return dict(
        y=p.sbuf("q_y", [128, n], F32, scope),
        ta=p.sbuf("q_ta", [128, n], F32, scope),
        tb_=p.sbuf("q_tb", [128, n], F32, scope),
        tc_=p.sbuf("q_tc", [128, n], F32, scope),
        sgn=p.sbuf("q_sgn", [128, n], F32, scope),
        bm=p.sbuf("q_bm", [128, nblk], F32, scope),
        sv=p.sbuf("q_sv", [128, nblk], F32, scope),
        rv=p.sbuf("q_rv", [128, nblk], F32, scope),
    )


def quant_scratch_views(reg, n, nblk):
    d = {}
    o = 0
    for nm in ("y", "ta", "tb_", "tc_", "sgn"):
        d[nm] = reg[:, o:o + n]
        o += n
    for nm in ("bm", "sv", "rv"):
        d[nm] = reg[:, o:o + nblk]
        o += nblk
    return d


def emit_v_stage(p, consts, v_sb, vg_in, scope):
    psv = [p.psum("vps0", [128, 1024], BF16, scope),
           p.psum("vps1", [128, 1024], BF16, scope)]
    vstage = p.sbuf("vstage", [128, 2 * 1040], BF16, scope)
    tr = p.ctr("vtr")
    cpy = p.ctr("vcp")
    st = [p.ctr("vsta"), p.ctr("vstb")]
    b_tr, b_cpy = tr.n, cpy.n
    b_st = [st[0].n, st[1].n]
    for b in range(2):
        p.op("vector", (lambda b: lambda e: e.memset(
            vstage[:, b * 1040:(b + 1) * 1040].rearrange(
                "p (h c) -> p h c", c=65)[:, :, 64:65], 1.0))(b))
    for tt in range(16):
        b = tt % 2
        ps = psv[b]
        if tt >= 2:
            p.wait("tensor", cpy, b_cpy + tt - 1)
        for c in range(NCH):
            in_ = v_sb[:, c * TC + tt * 128: c * TC + tt * 128 + 128]
            p.op("tensor", (lambda ps, in_, c: lambda e: e.transpose(
                ps[:, c * 128:(c + 1) * 128], in_, consts.id128b[:, :]))(
                ps, in_, c), inc=tr if c == NCH - 1 else None)
        p.wait("vector", tr, b_tr + tt + 1)
        if tt >= 2:
            p.wait("vector", st[b], b_st[b] + 16 * ((tt - 2) // 2 + 1))
        p.op("vector", (lambda b, ps: lambda e: e.tensor_copy(
            vstage[:, b * 1040:(b + 1) * 1040].rearrange(
                "p (h c) -> p h c", c=65)[:, :, 0:64],
            ps[:, :].rearrange("p (h c) -> p h c", c=64)))(b, ps), inc=cpy)
        p.wait("sync", cpy, b_cpy + tt + 1)
        p.op("sync", (lambda b, tt: lambda e: e.dma_start(
            out=vg_in[tt * 128:(tt + 1) * 128, :],
            in_=vstage[:, b * 1040:(b + 1) * 1040]))(b, tt), inc=st[b], dma=True)


def emit_proj_resid(p, *, w_sb, w_col0, cout, cin_chunks, rhs_sb, bias_sb,
                    x_dram, s_sb, xstr, pp):
    xld = [p.ctr("xlda"), p.ctr("xldb")]
    b_xld = [c.n for c in xld]
    g_of = {}
    g = 0
    for f in range(NCH):
        for tb in range(NTB):
            g_of[(f, tb)] = g
            g += 1
    done_ev = [None, 0]

    def x_dma(f, tb, gg):
        src = x_dram[f * 128:(f + 1) * 128, tb * TB:(tb + 1) * TB]
        dst = xstr[:, (gg % 2) * TB:(gg % 2) * TB + TB]
        if gg >= 2:
            p.wait("sync", done_ev[0], done_ev[1] + gg - 1)
        p.op("sync", (lambda dst, src: lambda e: e.dma_start(
            out=dst, in_=src))(dst, src), inc=xld[gg % 2], dma=True)

    def cb(p_, bank, f, t0, nt, mm, ev):
        gg = g_of[(f, t0 // TB)]
        if done_ev[0] is None:
            done_ev[0] = ev
            done_ev[1] = ev.n
            for g0 in range(2):
                x_dma(*divmod(g0, NTB), g0)
        if gg + 2 < NCH * NTB:
            f2, tb2 = divmod(gg + 2, NTB)
            x_dma(f2, tb2, gg + 2)
        osl = s_sb[:, f * TC + t0: f * TC + t0 + nt]
        xsl = xstr[:, (gg % 2) * TB:(gg % 2) * TB + TB]
        p_.wait("vector", mm)
        p_.wait("vector", xld[gg % 2], b_xld[gg % 2] + 16 * (gg // 2 + 1))
        if bias_sb is not None:
            bsl = bias_sb[:, f:f + 1]
            p_.op("vector", (lambda o, b, bs, x: lambda e: e.scalar_tensor_tensor(
                o, b, bs, x, ALU.add, ALU.add))(osl, bank, bsl, xsl), inc=ev)
        else:
            p_.op("vector", (lambda o, b, x: lambda e: e.tensor_tensor(
                o, b, x, ALU.add))(osl, bank, xsl), inc=ev)

    return emit_proj(p, w_sb=w_sb, w_col0=w_col0, cout=cout,
                     cin_chunks=cin_chunks, cout_chunks=NCH, rhs_sb=rhs_sb,
                     rhs_col0=0, out_cb=cb, pp=pp)


def emit_negb(p, consts, st_out, srow, negb_sb, scale, psB):
    ld = p.ctr("nld")
    p.op("sync", lambda e: e.dma_start(
        out=srow[0:1, 0:16],
        in_=st_out.rearrange("r c -> (r c)").unsqueeze(0)), inc=ld, dma=True)
    p.barrier()
    nrd = p.ctr("nrd")
    p.op("vector", lambda e: e.tensor_reduce(
        srow[0:1, 16:17],
        srow[0:1, 0:16].rearrange("p (r c) -> p c r", c=2)[:, 0:1, :].squeeze(1),
        AX.X, ALU.max))
    p.op("vector", lambda e: e.tensor_reduce(
        srow[0:1, 17:18],
        srow[0:1, 0:16].rearrange("p (r c) -> p c r", c=2)[:, 1:2, :].squeeze(1),
        AX.X, ALU.max), inc=nrd)
    p.wait("vector", nrd)
    p.op("vector", lambda e: e.tensor_tensor(
        srow[0:1, 16:17], srow[0:1, 16:17], srow[0:1, 17:18], ALU.mult))
    p.op("vector", lambda e: e.tensor_single_scalar(
        srow[0:1, 16:17], srow[0:1, 16:17], -64.0 * scale, ALU.mult))
    p.barrier()
    p.op("tensor", lambda e: e.matmul(
        psB[:, 0:1], consts.onesrow_f[0:1, :], srow[0:1, 16:17],
        start=True, stop=True))
    p.barrier()
    p.op("vector", lambda e: e.tensor_copy(negb_sb[:, 0:1], psB[:, 0:1]))
    p.barrier()


def emit_a0(p, consts, io, phases=8.0, pre_flash_cb=None):
    def bail(buf):
        od = p.ctr("od")
        p.op("vector", (lambda buf: lambda e: e.memset(buf, 0.5))(buf))
        p.barrier()
        p.op("sync", (lambda buf: lambda e: e.dma_start(
            out=io["x_out"].rearrange("(c p) t -> p c t", p=128),
            in_=buf.rearrange("p (c t) -> p c t", c=NCH)))(buf), inc=od, dma=True)
        p.barrier()
        return od
    nc = p.nc
    wld = p.ctr("wld")
    vld = p.ctr("vld")
    wbuf = p.sbuf("wbuf", [128, 2 * NCH * D], BF16)
    qsb = p.sbuf("qsb", [128, NCH * TC], BF16)
    buA = p.sbuf("buA", [128, NCH * TC], BF16)
    buB = p.sbuf("buB", [128, NCH * TC], BF16)
    big0 = p.sbuf("big0", [128, NCH * TC], F32)
    xbf = big0[:, 0:NCH * TC // 2].bitcast(BF16)
    s_sb = big0[:, :]
    bias_sb = p.sbuf("bias_sb", [128, 40], F32)
    srow = p.sbuf("srow", [1, 64], F32)
    cells = p.sbuf("cells", [1, 8], F32)
    statscr = p.sbuf("statscr", [128, 16], F32)
    negb_sb = p.sbuf("negb", [128, 1], F32)

    p.op("gpsimd", lambda e: e.dma_start(
        out=xbf.rearrange("p (c t) -> p c t", c=NCH),
        in_=io["x_in"].rearrange("(c p) t -> p c t", p=128)), inc=wld, dma=True)
    load_weight_cast(p, io["wq"], wbuf, NCH, D, wld, col0=0)
    load_weight_cast(p, io["wk"], wbuf, NCH, D, wld, col0=NCH * D)
    load_vec(p, io["bias_all"], bias_sb, vld)
    p.barrier()
    if phases <= 1:
        return bail(s_sb)

    sc_pp = p.scope()
    pp = p.psum("pp0", [128, 1024], F32, sc_pp)
    mm_q, _ = emit_proj(
        p, w_sb=wbuf, w_col0=0, cout=D, cin_chunks=NCH, cout_chunks=NCH,
        rhs_sb=xbf, rhs_col0=0, pp=pp,
        out_cb=evac_act(qsb, 0, bias_sb=bias_sb[:, 0:8]))
    n_mm_q = mm_q.n
    mm_kk, _ = emit_proj(
        p, w_sb=wbuf, w_col0=NCH * D, cout=D, cin_chunks=NCH,
        cout_chunks=NCH, rhs_sb=xbf, rhs_col0=0, pp=pp,
        out_cb=evac_act(buA, 0, bias_sb=bias_sb[:, 8:16]))
    if phases <= 2:
        p.barrier()
        return bail(s_sb)

    p.wait("gpsimd", mm_q, n_mm_q)
    load_weight_cast(p, io["wl"], wbuf, NCH, D, wld, col0=0)
    p.wait("gpsimd", mm_kk)
    load_weight_cast(p, io["wv"], wbuf, NCH, D, wld, col0=NCH * D)
    p.wait("tensor", wld)
    emit_proj(p, w_sb=wbuf, w_col0=0, cout=D, cin_chunks=NCH, cout_chunks=NCH,
              rhs_sb=buA, rhs_col0=0, pp=pp,
              out_cb=evac_act(buB, 0, bias_sb=bias_sb[:, 24:32]))
    p.barrier()
    if phases <= 2.3:
        return bail(s_sb)
    emit_absmax_cell(p, consts, qsb, 0, NCH, TC, statscr, pp, cells[0:1, 0:1])
    emit_absmax_cell(p, consts, buB, 0, NCH, TC, statscr, pp, cells[0:1, 1:2])
    kd = p.ctr("kd")
    p.op("sync", lambda e: e.dma_start(out=io["kg_in"][:, :].rearrange(
        "(c p) t -> p c t", p=128), in_=buB[:, :].rearrange(
        "p (c t) -> p c t", c=NCH)), inc=kd, dma=True)
    cck = p.ctr("cck", unique=True)
    p.wait("gpsimd", kd)
    p.op("gpsimd", lambda e: e.collective_compute(
        "AllGather", ALU.bypass, replica_groups=[[0, 1, 2, 3], [4, 5, 6, 7]],
        ins=[io["kg_in"].opt()], outs=[io["kg_out"].opt()]), inc=(cck, 1))
    if phases <= 2.6:
        p.barrier()
        return bail(s_sb)
    emit_proj(p, w_sb=wbuf, w_col0=NCH * D, cout=D, cin_chunks=NCH,
              cout_chunks=NCH, rhs_sb=xbf, rhs_col0=0, pp=pp,
              out_cb=evac_act(buA, 0, bias_sb=bias_sb[:, 16:24]))
    p.barrier()
    emit_proj(p, w_sb=wbuf, w_col0=0, cout=D, cin_chunks=NCH, cout_chunks=NCH,
              rhs_sb=buA, rhs_col0=0, pp=pp,
              out_cb=evac_act(buB, 0, bias_sb=bias_sb[:, 24:32]))
    p.barrier()
    if phases <= 2.8:
        return bail(s_sb)
    scv = p.scope()
    emit_v_stage(p, consts, buB, io["vg_in"], scv)
    p.op("sync", lambda e: e.dma_start(out=io["st_in"], in_=cells[0:1, 0:2]),
         inc=kd, dma=True)
    p.barrier()
    scv.close()
    if phases <= 3:
        return bail(s_sb)

    ccv = p.ctr("ccv", unique=True)
    ccs = p.ctr("ccs", unique=True)
    grp4 = [[0, 1, 2, 3], [4, 5, 6, 7]]
    p.op("gpsimd", lambda e: e.collective_compute(
        "AllGather", ALU.bypass, replica_groups=grp4,
        ins=[io["vg_in"].opt()], outs=[io["vg_out"].opt()]), inc=(ccv, 1))
    p.op("gpsimd", lambda e: e.collective_compute(
        "AllGather", ALU.bypass, replica_groups=[[0, 1, 2, 3, 4, 5, 6, 7]],
        ins=[io["st_in"].opt()], outs=[io["st_out"].opt()]), inc=(ccs, 1))
    for e in ENGINES:
        p.wait(e, cck, 1)
        p.wait(e, ccv, 1)
        p.wait(e, ccs, 1)
    if phases <= 4:
        return bail(s_sb)

    emit_negb(p, consts, io["st_out"], srow, negb_sb, 0.125, pp)
    load_weight_cast(p, io["wo"], wbuf, NCH, D, wld, col0=0)
    sc_pp.close()
    p.barrier()
    if phases <= 5:
        return bail(s_sb)

    if pre_flash_cb is not None:
        pre_flash_cb()
    scf = p.scope()
    fl = alloc_flash(p, scf, buB[:, :], big0)
    emit_flash(p, consts, q_sb=qsb, kg_ap=io["kg_out"], vg_ap=io["vg_out"],
               negb_sb=negb_sb, scale=0.125, attn_sb=buA, fl=fl)
    scf.close()
    if phases <= 6:
        return bail(s_sb)

    scw = p.scope()
    xstr = p.sbuf("xstr", [128, 4 * TB], F32, scw)
    ppw = p.psum("ppw", [128, 1024], F32, scw)
    emit_proj_resid(p, w_sb=wbuf, w_col0=0, cout=D, cin_chunks=NCH,
                    rhs_sb=buA, bias_sb=bias_sb[:, 32:40], x_dram=io["x_in"],
                    s_sb=s_sb, xstr=xstr, pp=ppw)
    p.barrier()
    scw.close()
    if phases <= 7:
        return bail(s_sb)

    scl = p.scope()
    lnsc = dict(
        bf_a=buB[:, :], bf_b=qsb[:, :],
        rows=wbuf[0:1, 0:12288].bitcast(F32),
        ps=p.psum("lnps", [128, 2048], F32, scl),
        ps_row=p.psum("lnpr", [1, 2048], F32, scl),
    )
    emit_ln(p, consts, s_sb=s_sb, out_sb=s_sb, sc=lnsc)
    od = p.ctr("od")
    p.op("sync", lambda e: e.dma_start(
        out=io["x_out"].rearrange("(c p) t -> p c t", p=128),
        in_=s_sb.rearrange("p (c t) -> p c t", c=NCH)), inc=od, dma=True)
    p.barrier()
    scl.close()
    return od


def _attn_gather_tensors(nc, sfx=""):
    return dict(
        kg_in=nc.dram_tensor("kg_in" + sfx, [1024, 2048], BF16).ap(),
        kg_out=nc.dram_tensor("kg_out" + sfx, [4096, 2048], BF16).ap(),
        vg_in=nc.dram_tensor("vg_in" + sfx, [2048, 1040], BF16).ap(),
        vg_out=nc.dram_tensor("vg_out" + sfx, [8192, 1040], BF16).ap(),
        st_in=nc.dram_tensor("st_in" + sfx, [1, 2], F32).ap(),
        st_out=nc.dram_tensor("st_out" + sfx, [8, 2], F32).ap(),
    )


def build_a0_launch(phases=8):
    p = Prog()
    nc = p.nc
    io = {}
    io["x_in"] = nc.dram_tensor("x", [1024, 2048], F32, kind="ExternalInput").ap()
    io["x_out"] = nc.dram_tensor("xo", [1024, 2048], F32,
                                 kind="ExternalOutput").ap()
    for w in ("wq", "wk", "wv", "wl", "wo"):
        io[w] = nc.dram_tensor(w, [1024, 1024], F32, kind="ExternalInput").ap()
    io["bias_all"] = nc.dram_tensor("bias_all", [128, 40], F32,
                                    kind="ExternalInput").ap()
    io.update(_attn_gather_tensors(nc))
    consts = Consts(p)
    od = emit_a0(p, consts, io, phases=phases)
    return p.finalize(final_waits=[(od, od.n)])


def pack_vec(v, n_chunks=NCH):
    v = np.asarray(v, np.float32).reshape(n_chunks, 128)
    return np.ascontiguousarray(v.T)


def shard_x(x):
    out = []
    for c in range(NCORES):
        sl = np.asarray(x[c // 4, (c % 4) * TC:(c % 4 + 1) * TC, :], np.float32)
        out.append(np.ascontiguousarray(sl.T))
    return out


def unshard_x(parts):
    x = np.zeros((2, 4 * TC, D), np.float32)
    for c in range(NCORES):
        x[c // 4, (c % 4) * TC:(c % 4 + 1) * TC, :] = np.asarray(
            parts[c], np.float32).T
    return x


def stage_a0(inputs, xs):
    consts = {("c_" + k): v for k, v in make_const_arrays().items()}
    shared = dict(
        wq=np.ascontiguousarray(inputs["L0_attn_Wq"], np.float32),
        wk=np.ascontiguousarray(inputs["L0_attn_Wk"], np.float32),
        wv=np.ascontiguousarray(inputs["L0_attn_Wv"], np.float32),
        wl=np.ascontiguousarray(inputs["L0_attn_Wl"], np.float32),
        wo=np.ascontiguousarray(inputs["L0_attn_Wo"], np.float32),
        bias_all=np.concatenate(
            [pack_vec(inputs["L0_attn_b" + t]) for t in "qkvlo"], axis=1),
        **consts,
    )
    return [dict(shared, x=xs[c]) for c in range(NCORES)]


def run_spmd(nc, in_maps):
    res = run_bass_kernel_spmd(nc, in_maps, core_ids=list(range(NCORES)))
    return res.results


def kernel(**inputs):
    xs = shard_x(inputs["L0_attn_x"])
    res = run_spmd(build_fused_launch(), stage_fused(inputs, xs))
    return unshard_x([np.asarray(r["xo"], np.float32) for r in res])


NSL = 11
SLC = 2


def emit_f0_staging(p, io, wtmp, *, background=False):
    bg = background
    tl = [p.ctr("wptla"), p.ctr("wptlb")]
    ts_ = [p.ctr("wptsa"), p.ctr("wptsb")]
    b_tl = [tl[0].n, tl[1].n]
    b_ts = [ts_[0].n, ts_[1].n]
    jobs = []
    for fc in range(8):
        for rh in range(4):
            jobs.append(("wgq_g", "wgq_fm", fc, rh * 704, 704))
            jobs.append(("wuq_g", "wuq_fm", fc, rh * 704, 704))
    for ic in range(22):
        jobs.append(("wdq_g", "wdq_im", ic, 0, 704))
        jobs.append(("wdq_g", "wdq_im", ic, 704, 320))
    for j, (a, b2, blk, r0, ncols) in enumerate(jobs):
        par = j % 2
        if j >= 2:
            p.wait("sync", ts_[par], b_ts[par] + 16 * (j // 2), background=bg)
        p.op("sync", (lambda a, blk, r0, ncols, par: lambda e:
             e.dma_start_transpose(
            wtmp[:, par * 704: par * 704 + ncols],
            io[a][r0:r0 + ncols, blk * 128:(blk + 1) * 128]))(
            a, blk, r0, ncols, par), inc=tl[par], dma=True, background=bg)
        p.wait("gpsimd", tl[par], b_tl[par] + 16 * (j // 2 + 1), background=bg)
        p.op("gpsimd", (lambda b2, blk, r0, ncols, par: lambda e: e.dma_start(
            out=io[b2][blk * 128:(blk + 1) * 128, r0:r0 + ncols],
            in_=wtmp[:, par * 704: par * 704 + ncols]))(
            b2, blk, r0, ncols, par), inc=ts_[par], dma=True, background=bg)
    return dict(ts0=(ts_[0], ts_[0].n), ts1=(ts_[1], ts_[1].n))


def emit_f0(p, consts, io, phases=8.0, wprep=None):
    nc = p.nc
    hq_sb = p.sbuf("hq_sb", [128, ICH * TC], BF16)
    wdq_sb = p.sbuf("wdq_sb", [128, ICH * D], BF16)
    bigf = p.sbuf("bigf", [128, 16384], F32)
    xq = bigf[:, 0:8192].bitcast(BF16)
    s_sb = bigf[:, :]

    def bail(buf):
        od = p.ctr("od")
        p.op("vector", (lambda buf: lambda e: e.memset(buf, 0.5))(buf))
        p.barrier()
        p.op("sync", (lambda buf: lambda e: e.dma_start(
            out=io["x_out"].rearrange("(c p) t -> p c t", p=128),
            in_=buf.rearrange("p (c t) -> p c t", c=NCH)))(buf), inc=od, dma=True)
        p.barrier()
        return od

    if wprep is None:
        scw = p.scope()
        wprep = emit_f0_wprep(p, io, background=False, scope=scw,
                              bufs=hq_sb[:, 0:24576].bitcast(F32))
        p.barrier()
        scw.close()
    for e in ENGINES:
        for k in ("cc0", "cc1", "cc2"):
            p.wait(e, wprep[k][0], wprep[k][1])
    p.barrier()
    stg = emit_f0_staging(p, io, hq_sb[:, 20480:21888], background=True)
    if phases <= 1:
        return bail(s_sb)

    scx = p.scope()
    xtb = hq_sb[:, 0:8192].bitcast(F32)
    xt = hq_sb[:, 8192:16384].bitcast(F32)
    xqt = hq_sb[:, 16384:20480]
    qs2 = quant_scratch_views(wdq_sb[:, 0:22528].bitcast(F32), 1024, 64)
    pst = p.psum("pst", [128, 1024], F32, scx)
    pstb = p.psum("pstb", [128, 1024], BF16, scx)
    xld = p.ctr("xld")
    xtr = p.ctr("xtr")
    xcp = p.ctr("xcp")
    xqd = p.ctr("xqd")
    xbt = p.ctr("xbt")
    xhc = p.ctr("xhc")
    for tbb in range(NTB):
        if tbb > 0:
            p.wait("gpsimd", xtr, xtr.n)
        p.op("gpsimd", (lambda tbb: lambda e: e.dma_start(
            out=xtb[:, :].rearrange("p (c t) -> p c t", c=NCH),
            in_=io["x_in"][:, tbb * TB:(tbb + 1) * TB].rearrange(
                "(c p) t -> p c t", p=128)))(tbb), inc=xld, dma=True)
        for tt in range(4):
            p.wait("tensor", xld, xld.n)
            if not (tbb == 0 and tt == 0):
                p.wait("tensor", xcp, xcp.n)
            for c in range(NCH):
                half = (c % 4) * 128
                p.op("tensor", (lambda tt, c, half: lambda e: e.transpose(
                    pst[:, (c // 4) * 512 + half: (c // 4) * 512 + half + 128],
                    xtb[:, c * TB + tt * 128: c * TB + tt * 128 + 128],
                    consts.id128[:, :]))(tt, c, half),
                    inc=xtr if c == NCH - 1 else None)
            p.wait("vector", xtr, xtr.n)
            p.op("vector", (lambda tt: lambda e: e.tensor_copy(
                xt[:, tt * 1024:(tt + 1) * 1024], pst[:, :]))(tt), inc=xcp)
        if tbb > 0:
            p.wait("vector", xbt, xbt.n)
        for tt in range(4):
            emit_quant2(p, xt[:, tt * 1024:(tt + 1) * 1024],
                        xqt[:, tt * 1024:(tt + 1) * 1024], 64, qs2,
                        src_ready=(xcp, xcp.n), done=xqd)
        for tt in range(4):
            p.wait("tensor", xqd, xqd.n)
            if not (tbb == 0 and tt == 0):
                p.wait("tensor", xhc, xhc.n)
            for c in range(NCH):
                p.op("tensor", (lambda tt, c: lambda e: e.transpose(
                    pstb[:, c * 128:(c + 1) * 128],
                    xqt[:, tt * 1024 + c * 128: tt * 1024 + c * 128 + 128],
                    consts.id128b[:, :]))(tt, c),
                    inc=xbt if c == NCH - 1 else None)
            p.wait("vector", xbt, xbt.n)
            p.op("vector", (lambda tbb, tt: lambda e: e.tensor_copy(
                xq.rearrange("p (c t) -> p c t", c=NCH)[
                    :, :, tbb * TB + tt * 128: tbb * TB + tt * 128 + 128],
                pstb[:, :].rearrange("p (c t) -> p c t", c=NCH)))(tbb, tt),
                inc=xhc)
    p.barrier()
    scx.close()
    if phases <= 2:
        return bail(s_sb)

    for e in ENGINES:
        p.wait(e, stg["ts0"][0], stg["ts0"][1])
        p.wait(e, stg["ts1"][0], stg["ts1"][1])
    p.barrier()
    scm = p.scope()
    wsl = wdq_sb[:, 0:4 * NCH * 256]
    hg = bigf[:, 8192:9216]
    hh = hg
    ht = bigf[:, 9216:10240]
    hqt = bigf[:, 10240:10752].bitcast(BF16)
    qs3 = quant_scratch_views(wdq_sb[:, 8192:22528].bitcast(F32), 1024, 64)
    ppg = p.psum("ppg", [128, 1024], F32, scm)
    ppu = p.psum("ppu", [128, 1024], F32, scm)
    pst2 = p.psum("pst2", [128, 512], F32, scm)
    pstb2 = p.psum("pstb2", [128, 512], BF16, scm)
    wld2 = [p.ctr("fwla"), p.ctr("fwlb")]
    b_wld2 = [wld2[0].n, wld2[1].n]
    mmg = p.ctr("mmg")
    b_mmg = mmg.n
    hdone = p.ctr("mhd")
    b_hd = hdone.n
    ftr = p.ctr("mftr")
    fcp = p.ctr("mfcp")
    mqd = p.ctr("mqd")
    mbt = p.ctr("mbt")
    mhc = p.ctr("mhc")

    def back_trans(it):
        isl_, tbb_ = divmod(it, NTB)
        p.wait("tensor", mqd, mqd.n)
        for tt in range(4):
            if not (it == 0 and tt == 0):
                p.wait("tensor", mhc, mhc.n)
            for sc_ in range(SLC):
                p.op("tensor", (lambda tt, sc_: lambda e: e.transpose(
                    pstb2[:, sc_ * 128:(sc_ + 1) * 128],
                    hqt[:, tt * 256 + sc_ * 128: tt * 256 + sc_ * 128 + 128],
                    consts.id128b[:, :]))(tt, sc_),
                    inc=mbt if sc_ == SLC - 1 else None)
            p.wait("vector", mbt, mbt.n)
            for sc_ in range(SLC):
                ic = isl_ * SLC + sc_
                p.op("vector", (lambda ic, tbb_, tt, sc_: lambda e:
                     e.tensor_copy(
                    hq_sb[:, ic * TC + tbb_ * TB + tt * 128:
                          ic * TC + tbb_ * TB + tt * 128 + 128],
                    pstb2[:, sc_ * 128:(sc_ + 1) * 128]))(ic, tbb_, tt, sc_),
                    inc=mhc if sc_ == SLC - 1 else None)

    for isl in range(NSL):
        par = isl % 2
        if isl >= 2:
            p.wait("sync", mmg, b_mmg + 16 * (isl - 1))
        for wi, wfm in enumerate(("wgq_fm", "wuq_fm")):
            dst = wsl[:, (par * 2 + wi) * NCH * 256:
                      (par * 2 + wi + 1) * NCH * 256].rearrange(
                "p (c i) -> p c i", c=NCH)
            srcw = io[wfm].rearrange("(c p) i -> p c i", p=128)[
                :, :, isl * 256:(isl + 1) * 256]
            p.op("sync", (lambda dst, srcw: lambda e: e.dma_start(
                out=dst, in_=srcw))(dst, srcw), inc=wld2[par], dma=True)
        p.wait("tensor", wld2[par], b_wld2[par] + 32 * (isl // 2 + 1))
        for tbb in range(NTB):
            it = isl * NTB + tbb
            if it >= 1:
                p.wait("tensor", hdone, b_hd + it)
            for wi, ppx in ((0, ppg), (1, ppu)):
                for sc_ in range(SLC):
                    bank = ppx[:, sc_ * 512:(sc_ + 1) * 512]
                    for ci in range(NCH):
                        wslc = wsl[:, (par * 2 + wi) * NCH * 256 + ci * 256
                                   + sc_ * 128: (par * 2 + wi) * NCH * 256
                                   + ci * 256 + sc_ * 128 + 128]
                        rsl = xq[:, ci * TC + tbb * TB: ci * TC + tbb * TB + TB]
                        p.op("tensor", (lambda bank, wslc, rsl, first, last:
                                        lambda e: e.matmul(
                            bank, wslc, rsl, start=first, stop=last))(
                            bank, wslc, rsl, ci == 0, ci == NCH - 1),
                            inc=mmg if ci == NCH - 1 else None)
            if it >= 1:
                back_trans(it - 1)
            p.wait("scalar", mmg, b_mmg + 4 * it + 2)
            if it >= 1:
                p.wait("scalar", hdone, b_hd + it)
            msil = p.ctr("msil")
            p.op("scalar", lambda e: e.activation(
                hg[:, 0:1024], ppg[:, 0:1024], AF.Silu), inc=msil)
            p.wait("vector", mmg, b_mmg + 4 * it + 4)
            p.wait("vector", msil, msil.n)
            p.op("vector", lambda e: e.tensor_tensor(
                hh[:, 0:1024], hg[:, 0:1024], ppu[:, 0:1024], ALU.mult),
                inc=hdone)
            for tt in range(4):
                p.wait("tensor", hdone, b_hd + it + 1)
                if not (it == 0 and tt == 0):
                    p.wait("tensor", fcp, fcp.n)
                for sc_ in range(SLC):
                    p.op("tensor", (lambda tt, sc_: lambda e: e.transpose(
                        pst2[:, sc_ * 128:(sc_ + 1) * 128],
                        hh[:, sc_ * 512 + tt * 128:
                           sc_ * 512 + tt * 128 + 128],
                        consts.id128[:, :]))(tt, sc_),
                        inc=ftr if sc_ == SLC - 1 else None)
                p.wait("vector", ftr, ftr.n)
                p.op("vector", (lambda tt: lambda e: e.tensor_copy(
                    ht[:, tt * 256:(tt + 1) * 256], pst2[:, 0:256]))(tt),
                    inc=fcp)
            if it >= 1:
                p.wait("vector", mbt, mbt.n)
            emit_quant2(p, ht[:, :], hqt[:, :], 64, qs3,
                        src_ready=(fcp, fcp.n), done=mqd)
    back_trans(NSL * NTB - 1)
    p.barrier()
    scm.close()
    if phases <= 3:
        return bail(s_sb)

    scd = p.scope()
    dld = p.ctr("dld")
    p.op("gpsimd", lambda e: e.dma_start(
        out=wdq_sb[:, :].rearrange("p (c f) -> p c f", c=ICH),
        in_=io["wdq_im"].rearrange("(c p) f -> p c f", p=128)), inc=dld, dma=True)
    p.barrier()
    xstr = p.sbuf("xstr", [128, 4 * TB], F32, scd)
    ppd = p.psum("ppd", [128, 1024], F32, scd)
    emit_proj_resid(p, w_sb=wdq_sb, w_col0=0, cout=D, cin_chunks=ICH,
                    rhs_sb=hq_sb, bias_sb=None, x_dram=io["x_in"],
                    s_sb=s_sb, xstr=xstr, pp=ppd)
    p.barrier()
    scd.close()
    if phases <= 4:
        return bail(s_sb)

    scl = p.scope()
    lnsc = dict(
        bf_a=hq_sb[:, 0:NCH * TC],
        bf_b=hq_sb[:, NCH * TC:2 * NCH * TC],
        rows=wdq_sb[0:1, 0:12288].bitcast(F32),
        ps=p.psum("lnps", [128, 2048], F32, scl),
        ps_row=p.psum("lnpr", [1, 2048], F32, scl),
    )
    emit_ln(p, consts, s_sb=s_sb, out_sb=s_sb, sc=lnsc)
    od = p.ctr("od")
    p.op("sync", lambda e: e.dma_start(
        out=io["x_out"].rearrange("(c p) t -> p c t", p=128),
        in_=s_sb.rearrange("p (c t) -> p c t", c=NCH)), inc=od, dma=True)
    p.barrier()
    scl.close()
    return od


def build_f0_launch(phases=8.0):
    p = Prog()
    nc = p.nc
    io = {}
    io["x_in"] = nc.dram_tensor("x", [1024, 2048], F32, kind="ExternalInput").ap()
    io["x_out"] = nc.dram_tensor("xo", [1024, 2048], F32,
                                 kind="ExternalOutput").ap()
    io["wg_shard"] = nc.dram_tensor("wg_shard", [352, 1024], F32,
                                    kind="ExternalInput").ap()
    io["wu_shard"] = nc.dram_tensor("wu_shard", [352, 1024], F32,
                                    kind="ExternalInput").ap()
    io["wd_shard"] = nc.dram_tensor("wd_shard", [128, 2816], F32,
                                    kind="ExternalInput").ap()
    for nm, shp in (("wgq_in", [352, 1024]), ("wgq_g", [2816, 1024]),
                    ("wuq_in", [352, 1024]), ("wuq_g", [2816, 1024]),
                    ("wdq_in", [128, 2816]), ("wdq_g", [1024, 2816]),
                    ("wgq_fm", [1024, 2816]), ("wuq_fm", [1024, 2816]),
                    ("wdq_im", [2816, 1024])):
        io[nm] = nc.dram_tensor(nm, shp, BF16).ap()
    consts = Consts(p)
    od = emit_f0(p, consts, io, phases=phases)
    return p.finalize(final_waits=[(od, od.n)])


def stage_f0(inputs, xs):
    consts = {("c_" + k): v for k, v in make_const_arrays().items()}
    wg = np.ascontiguousarray(inputs["L0_ffn_w_gate"], np.float32)
    wu = np.ascontiguousarray(inputs["L0_ffn_w_up"], np.float32)
    wd = np.ascontiguousarray(inputs["L0_ffn_w_down"], np.float32)
    maps = []
    for c in range(NCORES):
        maps.append(dict(
            x=xs[c],
            wg_shard=np.ascontiguousarray(wg[c * 352:(c + 1) * 352]),
            wu_shard=np.ascontiguousarray(wu[c * 352:(c + 1) * 352]),
            wd_shard=np.ascontiguousarray(wd[c * 128:(c + 1) * 128]),
            **consts,
        ))
    return maps


def emit_rope_path(p, consts, *, w_sb, w_col0, cin_chunks, bias_sl, rhs_sb,
                   cs_sb, out_sb, krtb, ktok, rtmp, pp, pst, add_out=True):
    for tb in range(NTB):
        emit_proj(p, w_sb=w_sb, w_col0=w_col0, cout=D, cin_chunks=cin_chunks,
                  cout_chunks=NCH, rhs_sb=rhs_sb, rhs_col0=tb * TB, pp=pp,
                  n_tok=TB, tbs=TB, rhs_tok=TC,
                  out_cb=evac_act(krtb, 0, bias_sb=bias_sl, chunk_stride=TB))
        p.barrier()
        for tt in range(4):
            for c in range(NCH):
                half = (c % 4) * 128
                p.op("tensor", (lambda tt, c, half: lambda e: e.transpose(
                    pst[:, (c // 4) * 512 + half:(c // 4) * 512 + half + 128],
                    krtb[:, c * TB + tt * 128: c * TB + tt * 128 + 128],
                    consts.id128[:, :]))(tt, c, half))
            p.barrier()
            p.op("vector", (lambda tt: lambda e: e.tensor_copy(
                ktok[:, tt * 1024:(tt + 1) * 1024], pst[:, :]))(tt))
            p.barrier()
        for tt in range(4):
            ttg = tb * 4 + tt
            base = tt * 1024
            x0 = ktok[:, base:base + 1024].rearrange(
                "p (i two) -> p i two", two=2)[:, :, 0:1].squeeze(-1)
            x1 = ktok[:, base:base + 1024].rearrange(
                "p (i two) -> p i two", two=2)[:, :, 1:2].squeeze(-1)
            cosv = cs_sb[:, ttg * 512:(ttg + 1) * 512]
            sinv = cs_sb[:, 8192 + ttg * 512: 8192 + (ttg + 1) * 512]
            e0 = rtmp[:, 0:512]
            e1 = rtmp[:, 512:1024]
            p.op("vector", (lambda e0, x0, cosv: lambda e: e.tensor_tensor(
                e0, x0, cosv, ALU.mult))(e0, x0, cosv))
            p.op("vector", (lambda e1, x1, sinv: lambda e: e.tensor_tensor(
                e1, x1, sinv, ALU.mult))(e1, x1, sinv))
            p.op("vector", (lambda e0, e1: lambda e: e.tensor_tensor(
                e0, e0, e1, ALU.subtract))(e0, e1))
            p.op("vector", (lambda e1, x1, cosv: lambda e: e.tensor_tensor(
                e1, x1, cosv, ALU.mult))(e1, x1, cosv))
            p.op("vector", (lambda x1, x0, sinv: lambda e: e.tensor_tensor(
                x1, x0, sinv, ALU.mult))(x1, x0, sinv))
            p.op("vector", (lambda x1, e1: lambda e: e.tensor_tensor(
                x1, e1, x1, ALU.add))(x1, e1))
            p.op("vector", (lambda x0, e0: lambda e: e.tensor_copy(
                x0, e0))(x0, e0))
            p.barrier()
        for tt in range(4):
            for c in range(NCH):
                half = (c % 4) * 128
                p.op("tensor", (lambda tt, c, half: lambda e: e.transpose(
                    pst[:, (c // 4) * 512 + half:(c // 4) * 512 + half + 128],
                    ktok[:, tt * 1024 + c * 128: tt * 1024 + c * 128 + 128],
                    consts.id128[:, :]))(tt, c, half))
            p.barrier()
            osl = out_sb.rearrange("p (c t) -> p c t", c=NCH)[
                :, :, tb * TB + tt * 128: tb * TB + tt * 128 + 128]
            psv = pst[:, :].rearrange("p (c t) -> p c t", c=NCH)
            if add_out:
                p.op("vector", (lambda osl, psv: lambda e: e.tensor_tensor(
                    osl, psv, osl, ALU.add))(osl, psv))
            else:
                p.op("vector", (lambda osl, psv: lambda e: e.tensor_copy(
                    osl, psv))(osl, psv))
            p.barrier()


def emit_a1(p, consts, io, phases=8.0):
    nc = p.nc
    wld = p.ctr("wld")
    vld = p.ctr("vld")
    wbuf = p.sbuf("wbuf1", [128, 2 * NCH * D], BF16)
    qsb = p.sbuf("qsb1", [128, NCH * TC], BF16)
    buA = p.sbuf("buA1", [128, NCH * TC], BF16)
    cs_sb = p.sbuf("cs1", [128, NCH * TC], BF16)
    big0 = p.sbuf("big1", [128, NCH * TC], F32)
    xbf = big0[:, 0:8192].bitcast(BF16)
    valb = xbf
    ctq = big0[:, 8192:12288].bitcast(BF16)
    ctkv = big0[:, 12288:16384].bitcast(BF16)
    s_sb = big0[:, :]
    bias_sb = p.sbuf("bias1", [128, 56], F32)
    srow = p.sbuf("srow1", [1, 64], F32)
    cells = p.sbuf("cells1", [1, 8], F32)
    statscr = p.sbuf("statscr1", [128, 16], F32)
    negb_sb = p.sbuf("negb1", [128, 1], F32)
    SC1 = float(1.0 / np.sqrt(np.float32(16 + 512 + 512)))

    def bail(buf):
        od = p.ctr("od")
        p.op("vector", (lambda buf: lambda e: e.memset(buf, 0.5))(buf))
        p.barrier()
        p.op("sync", (lambda buf: lambda e: e.dma_start(
            out=io["x_out"].rearrange("(c p) t -> p c t", p=128),
            in_=buf.rearrange("p (c t) -> p c t", c=NCH)))(buf), inc=od, dma=True)
        p.barrier()
        return od

    BD = dict(dq=(0, 4), uq=(4, 12), qr=(12, 20), kr=(20, 28), dkv=(28, 32),
              uk=(32, 40), uv=(40, 48), o=(48, 56))

    def bsl(key):
        a, b = BD[key]
        return bias_sb[:, a:b]

    p.op("gpsimd", lambda e: e.dma_start(
        out=xbf.rearrange("p (c t) -> p c t", c=NCH),
        in_=io["x_in"].rearrange("(c p) t -> p c t", p=128)), inc=wld, dma=True)
    p.op("gpsimd", lambda e: e.dma_start(
        out=cs_sb[:, :], in_=io["cs"]), inc=wld, dma=True)
    load_weight_cast(p, io["wdq_w"], wbuf, NCH, 512, wld, col0=0)
    load_weight_cast(p, io["wdkv_w"], wbuf, NCH, 512, wld, col0=4096)
    load_weight_cast(p, io["wuk_w"], wbuf, 4, D, wld, col0=8192)
    load_vec(p, io["bias_all"], bias_sb, vld)
    p.barrier()

    sc_pp = p.scope()
    pp = p.psum("pp1", [128, 1024], F32, sc_pp)
    pst = p.psum("pstA", [128, 1024], F32, sc_pp)
    emit_proj(p, w_sb=wbuf, w_col0=0, cout=512, cin_chunks=NCH, cout_chunks=4,
              rhs_sb=xbf, rhs_col0=0, pp=pp,
              out_cb=evac_act(ctq, 0, bias_sb=bsl("dq")))
    p.barrier()
    emit_proj(p, w_sb=wbuf, w_col0=4096, cout=512, cin_chunks=NCH,
              cout_chunks=4, rhs_sb=xbf, rhs_col0=0, pp=pp,
              out_cb=evac_act(ctkv, 0, bias_sb=bsl("dkv")))
    p.barrier()
    if phases <= 1:
        sc_pp.close()
        return bail(s_sb)

    emit_proj(p, w_sb=wbuf, w_col0=8192, cout=D, cin_chunks=4, cout_chunks=NCH,
              rhs_sb=ctkv, rhs_col0=0, pp=pp,
              out_cb=evac_act(buA, 0, bias_sb=bsl("uk")))
    load_weight_cast(p, io["wkr_w"], wbuf, NCH, D, wld, col0=0)
    p.barrier()
    krtb = qsb[:, 0:8192].bitcast(F32)
    ktok = qsb[:, 8192:16384].bitcast(F32)
    rtmp = p.sbuf("rtmp1", [128, 1024], F32)
    emit_rope_path(p, consts, w_sb=wbuf, w_col0=0, cin_chunks=NCH,
                   bias_sl=bsl("kr"), rhs_sb=xbf, cs_sb=cs_sb, out_sb=buA,
                   krtb=krtb, ktok=ktok, rtmp=rtmp, pp=pp, pst=pst)
    if phases <= 2:
        sc_pp.close()
        return bail(s_sb)

    emit_absmax_cell(p, consts, buA, 0, NCH, TC, statscr, pp, cells[0:1, 1:2])
    kd = p.ctr("kd")
    p.op("sync", lambda e: e.dma_start(out=io["kg_in"][:, :].rearrange(
        "(c p) t -> p c t", p=128), in_=buA[:, :].rearrange(
        "p (c t) -> p c t", c=NCH)), inc=kd, dma=True)
    cck = p.ctr("cck1", unique=True)
    p.wait("gpsimd", kd)
    p.op("gpsimd", lambda e: e.collective_compute(
        "AllGather", ALU.bypass, replica_groups=[[0, 1, 2, 3], [4, 5, 6, 7]],
        ins=[io["kg_in"].opt()], outs=[io["kg_out"].opt()]), inc=(cck, 1))
    load_weight_cast(p, io["wuv_w"], wbuf, 4, D, wld, col0=8192)
    p.barrier()
    emit_proj(p, w_sb=wbuf, w_col0=8192, cout=D, cin_chunks=4, cout_chunks=NCH,
              rhs_sb=ctkv, rhs_col0=0, pp=pp,
              out_cb=evac_act(valb, 0, bias_sb=bsl("uv")))
    p.barrier()
    scv = p.scope()
    emit_v_stage(p, consts, valb, io["vg_in"], scv)
    p.barrier()
    scv.close()
    if phases <= 3:
        sc_pp.close()
        return bail(s_sb)

    load_weight_cast(p, io["wuq_w"], wbuf, 4, D, wld, col0=0)
    load_weight_cast(p, io["wqr_w"], wbuf, 4, D, wld, col0=4096)
    p.barrier()
    emit_proj(p, w_sb=wbuf, w_col0=0, cout=D, cin_chunks=4, cout_chunks=NCH,
              rhs_sb=ctq, rhs_col0=0, pp=pp,
              out_cb=evac_act(qsb, 0, bias_sb=bsl("uq")))
    p.barrier()
    krtb2 = valb[0:128, 0:8192].bitcast(F32)
    ktok2 = valb[0:128, 8192:16384].bitcast(F32)
    emit_rope_path(p, consts, w_sb=wbuf, w_col0=4096, cin_chunks=4,
                   bias_sl=bsl("qr"), rhs_sb=ctq, cs_sb=cs_sb, out_sb=qsb,
                   krtb=krtb2, ktok=ktok2, rtmp=rtmp, pp=pp, pst=pst)
    if phases <= 4:
        sc_pp.close()
        return bail(s_sb)

    emit_absmax_cell(p, consts, qsb, 0, NCH, TC, statscr, pp, cells[0:1, 0:1])
    p.op("sync", lambda e: e.dma_start(out=io["st_in"], in_=cells[0:1, 0:2]),
         inc=kd, dma=True)
    p.barrier()
    ccv = p.ctr("ccv1", unique=True)
    ccs = p.ctr("ccs1", unique=True)
    grp4 = [[0, 1, 2, 3], [4, 5, 6, 7]]
    p.op("gpsimd", lambda e: e.collective_compute(
        "AllGather", ALU.bypass, replica_groups=grp4,
        ins=[io["vg_in"].opt()], outs=[io["vg_out"].opt()]), inc=(ccv, 1))
    p.op("gpsimd", lambda e: e.collective_compute(
        "AllGather", ALU.bypass, replica_groups=[[0, 1, 2, 3, 4, 5, 6, 7]],
        ins=[io["st_in"].opt()], outs=[io["st_out"].opt()]), inc=(ccs, 1))
    for e in ENGINES:
        p.wait(e, cck, 1)
        p.wait(e, ccv, 1)
        p.wait(e, ccs, 1)
    emit_negb(p, consts, io["st_out"], srow, negb_sb, SC1, pp)
    load_weight_cast(p, io["wo_w"], wbuf, NCH, D, wld, col0=0)
    sc_pp.close()
    p.barrier()
    if phases <= 5:
        return bail(s_sb)

    scf = p.scope()
    fl = alloc_flash(p, scf, cs_sb[:, :], big0)
    emit_flash(p, consts, q_sb=qsb, kg_ap=io["kg_out"], vg_ap=io["vg_out"],
               negb_sb=negb_sb, scale=SC1, attn_sb=buA, fl=fl)
    scf.close()
    if phases <= 6:
        return bail(s_sb)

    scw = p.scope()
    xstr = p.sbuf("xstr1", [128, 4 * TB], F32, scw)
    ppw = p.psum("ppw1", [128, 1024], F32, scw)
    emit_proj_resid(p, w_sb=wbuf, w_col0=0, cout=D, cin_chunks=NCH,
                    rhs_sb=buA, bias_sb=bsl("o"), x_dram=io["x_in"],
                    s_sb=s_sb, xstr=xstr, pp=ppw)
    p.barrier()
    scw.close()

    scl = p.scope()
    lnsc = dict(
        bf_a=buA[:, :], bf_b=qsb[:, :],
        rows=wbuf[0:1, 0:12288].bitcast(F32),
        ps=p.psum("lnps1", [128, 2048], F32, scl),
        ps_row=p.psum("lnpr1", [1, 2048], F32, scl),
    )
    emit_ln(p, consts, s_sb=s_sb, out_sb=s_sb, sc=lnsc)
    od = p.ctr("od")
    p.op("sync", lambda e: e.dma_start(
        out=io["x_out"].rearrange("(c p) t -> p c t", p=128),
        in_=s_sb.rearrange("p (c t) -> p c t", c=NCH)), inc=od, dma=True)
    p.barrier()
    scl.close()
    return od


def build_a1_launch(phases=8.0):
    p = Prog()
    nc = p.nc
    io = {}
    io["x_in"] = nc.dram_tensor("x", [1024, 2048], F32, kind="ExternalInput").ap()
    io["x_out"] = nc.dram_tensor("xo", [1024, 2048], F32,
                                 kind="ExternalOutput").ap()
    for nm, shp in (("wdq_w", [1024, 512]), ("wuq_w", [512, 1024]),
                    ("wqr_w", [512, 1024]), ("wkr_w", [1024, 1024]),
                    ("wdkv_w", [1024, 512]), ("wuk_w", [512, 1024]),
                    ("wuv_w", [512, 1024]), ("wo_w", [1024, 1024]),
                    ("bias_all", [128, 56]), ("cs", [128, 16384])):
        io[nm] = nc.dram_tensor(nm, shp, F32, kind="ExternalInput").ap()
    io.update(_attn_gather_tensors(nc, "1"))
    consts = Consts(p)
    od = emit_a1(p, consts, io, phases=phases)
    return p.finalize(final_waits=[(od, od.n)])


def host_cos_sin():
    ids = np.arange(512, dtype=np.float32)
    theta = (np.float32(10000.0) ** (np.float32(-2.0) * ids)) / np.float32(512)
    out = []
    for c in range(NCORES):
        t0 = (c % 4) * TC
        pos = np.arange(t0, t0 + TC, dtype=np.float32)[:, None]
        r = (pos * theta[None, :]).astype(np.float32)
        cos = np.cos(r).astype(np.float32)
        sin = np.sin(r).astype(np.float32)
        arr = np.zeros((128, 16384), np.float32)
        for tt in range(16):
            arr[:, tt * 512:(tt + 1) * 512] = cos[tt * 128:(tt + 1) * 128]
            arr[:, 8192 + tt * 512: 8192 + (tt + 1) * 512] = \
                sin[tt * 128:(tt + 1) * 128]
        out.append(arr)
    return out


def stage_a1(inputs, xs):
    consts = {("c_" + k): v for k, v in make_const_arrays().items()}
    cs = host_cos_sin()
    bias = np.concatenate([
        pack_vec(inputs["L1_attn_Wd_q_b"], 4),
        pack_vec(inputs["L1_attn_Wu_q_b"], 8),
        pack_vec(inputs["L1_attn_Wq_r_b"], 8),
        pack_vec(inputs["L1_attn_Wk_r_b"], 8),
        pack_vec(inputs["L1_attn_Wd_kv_b"], 4),
        pack_vec(inputs["L1_attn_Wu_k_b"], 8),
        pack_vec(inputs["L1_attn_Wu_v_b"], 8),
        pack_vec(inputs["L1_attn_Wo_b"], 8),
    ], axis=1)
    shared = dict(
        wdq_w=np.ascontiguousarray(inputs["L1_attn_Wd_q_w"], np.float32),
        wuq_w=np.ascontiguousarray(inputs["L1_attn_Wu_q_w"], np.float32),
        wqr_w=np.ascontiguousarray(inputs["L1_attn_Wq_r_w"], np.float32),
        wkr_w=np.ascontiguousarray(inputs["L1_attn_Wk_r_w"], np.float32),
        wdkv_w=np.ascontiguousarray(inputs["L1_attn_Wd_kv_w"], np.float32),
        wuk_w=np.ascontiguousarray(inputs["L1_attn_Wu_k_w"], np.float32),
        wuv_w=np.ascontiguousarray(inputs["L1_attn_Wu_v_w"], np.float32),
        wo_w=np.ascontiguousarray(inputs["L1_attn_Wo_w"], np.float32),
        bias_all=bias,
        **consts,
    )
    return [dict(shared, x=xs[c], cs=cs[c]) for c in range(NCORES)]


def emit_f1(p, consts, io, phases=8.0):
    nc = p.nc
    wld = p.ctr("wld")
    vld = p.ctr("vld")
    wbuf = p.sbuf("wbufF", [128, 3 * NCH * D], BF16)
    s_sb = p.sbuf("bigF", [128, NCH * TC], F32)
    lnb = p.sbuf("lnbF", [128, NCH * TC], BF16)
    x_tb = p.sbuf("x_tbF", [128, NCH * 513], F32)
    xr_tb = p.sbuf("xrF", [128, NCH * TB], BF16)
    xk_tb = p.sbuf("xkF", [128, NCH * TB], BF16)
    r_tb = p.sbuf("rF", [128, NCH * TB], F32)
    ka_tb = p.sbuf("kaF", [128, NCH * TB], BF16)
    t1 = p.sbuf("t1F", [128, TB], F32)
    mus = p.sbuf("musF", [128, 32], F32)
    msk = p.sbuf("mskF", [128, 64], F32)
    g64 = p.sbuf("g64F", [128, 64], F32)
    bcol = p.sbuf("bcolF", [128, 8], F32)

    def bail(buf):
        od = p.ctr("od")
        p.op("vector", (lambda buf: lambda e: e.memset(buf, 0.5))(buf))
        p.barrier()
        p.op("sync", (lambda buf: lambda e: e.dma_start(
            out=io["x_out"].rearrange("(c p) t -> p c t", p=128),
            in_=buf.rearrange("p (c t) -> p c t", c=NCH)))(buf), inc=od, dma=True)
        p.barrier()
        return od

    load_weight_cast(p, io["wr"], wbuf, NCH, D, wld, col0=0)
    load_weight_cast(p, io["wk"], wbuf, NCH, D, wld, col0=8192)
    load_weight_cast(p, io["wv"], wbuf, NCH, D, wld, col0=16384)
    p.op("sync", lambda e: e.dma_start(out=mus[:, 0:16], in_=io["mus"]),
         inc=vld, dma=True)
    p.op("sync", lambda e: e.dma_start(out=msk[:, :], in_=io["msk"]),
         inc=vld, dma=True)
    p.op("sync", lambda e: e.dma_start(
        out=bcol[:, :].rearrange("p c -> p c ()"),
        in_=io["x_in"].rearrange("(c p) t -> p c t", p=128)[:, :, 2047:2048],
        allow_slow_non_contiguous=True), inc=vld, dma=True)
    p.barrier()
    cc = p.ctr("ccF", unique=True)
    p.op("sync", lambda e: e.dma_start(out=io["bc_in"], in_=bcol[:, :]),
         inc=vld, dma=True)
    p.barrier()
    p.op("gpsimd", lambda e: e.collective_compute(
        "AllGather", ALU.bypass, replica_groups=[[0, 1, 2, 3, 4, 5, 6, 7]],
        ins=[io["bc_in"].opt()], outs=[io["bc_g"].opt()]), inc=(cc, 1))
    for e in ENGINES:
        p.wait(e, cc, 1)
    p.op("sync", lambda e: e.dma_start(
        out=g64[:, :].rearrange("p (r c) -> p r c", r=8),
        in_=io["bc_g"].rearrange("(r p) c -> p r c", p=128)), inc=vld, dma=True)
    p.barrier()
    p.op("vector", lambda e: e.tensor_tensor(
        g64[:, :].rearrange("p (r c) -> p c r", r=8),
        g64[:, :].rearrange("p (r c) -> p c r", r=8),
        msk[:, :].rearrange("p (r c) -> p c r", r=8), ALU.mult))
    p.barrier()
    p.op("vector", lambda e: e.tensor_reduce(
        bcol[:, :], g64[:, :].rearrange("p (r c) -> p c r", r=8), AX.X, ALU.add))
    p.op("vector", lambda e: e.tensor_scalar(
        mus[:, 16:32], mus[:, 0:16], -1.0, 1.0, ALU.mult, ALU.add))
    p.barrier()
    if phases <= 1:
        return bail(s_sb)

    sc_pp = p.scope()
    pp = p.psum("ppF", [128, 1024], F32, sc_pp)
    xldF = p.ctr("xldF")
    for tbb in range(NTB):
        if tbb == 0:
            p.op("sync", lambda e: e.dma_start(
                out=x_tb[:, :].rearrange("p (c t) -> p c t", c=NCH)[:, :, 1:513],
                in_=io["x_in"].rearrange("(c p) t -> p c t", p=128)[
                    :, :, 0:512]), inc=xldF, dma=True)
            p.barrier()
            p.op("vector", lambda e: e.tensor_copy(
                x_tb[:, :].rearrange("p (c t) -> p c t", c=NCH)[:, :, 0:1],
                bcol[:, :].rearrange("p c -> p c ()")))
        else:
            p.op("sync", (lambda tbb: lambda e: e.dma_start(
                out=x_tb[:, :].rearrange("p (c t) -> p c t", c=NCH),
                in_=io["x_in"].rearrange("(c p) t -> p c t", p=128)[
                    :, :, tbb * TB - 1: tbb * TB + 512]))(tbb), inc=xldF,
                dma=True)
        p.barrier()
        for which, xo, mu0, om0 in ((0, xr_tb, 0, 16), (1, xk_tb, 8, 24)):
            for c in range(NCH):
                xc = x_tb[:, c * 513 + 1: c * 513 + 513]
                xp = x_tb[:, c * 513: c * 513 + 512]
                p.op("vector", (lambda xc, mu: lambda e: e.tensor_scalar(
                    t1[:, :], xc, mu, None, ALU.mult, ...))(
                    xc, None) if False else
                    (lambda xc, c, mu0: lambda e: e.tensor_single_scalar(
                        t1[:, :], xc, mus[:, mu0 + c: mu0 + c + 1], ALU.mult))(
                        xc, c, mu0))
                p.op("vector", (lambda xo, xp, c, om0: lambda e:
                     e.scalar_tensor_tensor(
                    xo[:, c * TB:(c + 1) * TB], xp,
                    mus[:, om0 + c: om0 + c + 1], t1[:, :],
                    ALU.mult, ALU.add))(xo, xp, c, om0))
        p.barrier()
        emit_proj(p, w_sb=wbuf, w_col0=0, cout=D, cin_chunks=NCH,
                  cout_chunks=NCH, rhs_sb=xr_tb, rhs_col0=0, pp=pp,
                  n_tok=TB, tbs=TB, rhs_tok=TB,
                  out_cb=evac_act(r_tb, 0, func=AF.Sigmoid, chunk_stride=TB))
        p.barrier()

        def cb_k(p_, bank, f, t0, nt, mm, ev):
            p_.wait("vector", mm)
            p_.op("vector", (lambda bank: lambda e: e.tensor_scalar_max(
                t1[:, :], bank, 0.0))(bank))
            p_.op("vector", (lambda f: lambda e: e.tensor_tensor(
                ka_tb[:, f * TB: f * TB + TB], t1[:, :], t1[:, :],
                ALU.mult))(f), inc=ev)
        emit_proj(p, w_sb=wbuf, w_col0=8192, cout=D, cin_chunks=NCH,
                  cout_chunks=NCH, rhs_sb=xk_tb, rhs_col0=0, pp=pp,
                  n_tok=TB, tbs=TB, rhs_tok=TB, out_cb=cb_k)
        p.barrier()

        def cb_v(p_, bank, f, t0, nt, mm, ev):
            p_.wait("vector", mm)
            rsl = r_tb[:, f * TB: f * TB + nt]
            ssl = s_sb[:, f * TC + tbb * TB: f * TC + tbb * TB + nt]
            xsl = x_tb[:, f * 513 + 1: f * 513 + 513]
            p_.op("vector", (lambda ssl, bank, rsl: lambda e: e.tensor_tensor(
                ssl, bank, rsl, ALU.mult))(ssl, bank, rsl))
            p_.op("vector", (lambda ssl, xsl: lambda e: e.tensor_tensor(
                ssl, ssl, xsl, ALU.add))(ssl, xsl), inc=ev)
        emit_proj(p, w_sb=wbuf, w_col0=16384, cout=D, cin_chunks=NCH,
                  cout_chunks=NCH, rhs_sb=ka_tb, rhs_col0=0, pp=pp,
                  n_tok=TB, tbs=TB, rhs_tok=TB, out_cb=cb_v)
        p.barrier()
    sc_pp.close()
    if phases <= 2:
        return bail(s_sb)

    scl = p.scope()
    lnsc = dict(
        bf_a=wbuf[:, 8192:24576], bf_b=lnb[:, :],
        rows=wbuf[0:1, 0:12288].bitcast(F32),
        ps=p.psum("lnpsF", [128, 2048], F32, scl),
        ps_row=p.psum("lnprF", [1, 2048], F32, scl),
    )
    emit_ln(p, consts, s_sb=s_sb, out_sb=s_sb, sc=lnsc)
    od = p.ctr("od")
    p.op("sync", lambda e: e.dma_start(
        out=io["x_out"].rearrange("(c p) t -> p c t", p=128),
        in_=s_sb.rearrange("p (c t) -> p c t", c=NCH)), inc=od, dma=True)
    p.barrier()
    scl.close()
    return od


def build_f1_launch(phases=8.0):
    p = Prog()
    nc = p.nc
    io = {}
    io["x_in"] = nc.dram_tensor("x", [1024, 2048], F32, kind="ExternalInput").ap()
    io["x_out"] = nc.dram_tensor("xo", [1024, 2048], F32,
                                 kind="ExternalOutput").ap()
    for nm, shp in (("wr", [1024, 1024]), ("wk", [1024, 1024]),
                    ("wv", [1024, 1024]), ("mus", [128, 16]),
                    ("msk", [128, 64])):
        io[nm] = nc.dram_tensor(nm, shp, F32, kind="ExternalInput").ap()
    io["bc_in"] = nc.dram_tensor("bc_in", [128, 8], F32).ap()
    io["bc_g"] = nc.dram_tensor("bc_g", [1024, 8], F32).ap()
    consts = Consts(p)
    od = emit_f1(p, consts, io, phases=phases)
    return p.finalize(final_waits=[(od, od.n)])


def stage_f1(inputs, xs):
    consts = {("c_" + k): v for k, v in make_const_arrays().items()}
    mus = np.concatenate([pack_vec(inputs["L1_ffn_mu_r"], 8),
                          pack_vec(inputs["L1_ffn_mu_k"], 8)], axis=1)
    shared = dict(
        wr=np.ascontiguousarray(np.asarray(inputs["L1_ffn_W_r"],
                                           np.float32).T),
        wk=np.ascontiguousarray(np.asarray(inputs["L1_ffn_W_k"],
                                           np.float32).T),
        wv=np.ascontiguousarray(np.asarray(inputs["L1_ffn_W_v"],
                                           np.float32).T),
        mus=mus,
        **consts,
    )
    maps = []
    for c in range(NCORES):
        m = np.zeros((128, 8, 8), np.float32)
        if c % 4 != 0:
            m[:, c - 1, :] = 1.0
        maps.append(dict(shared, x=xs[c], msk=m.reshape(128, 64)))
    return maps


def build_fused_launch():
    p = Prog()
    nc = p.nc

    def din(nm, shp):
        return nc.dram_tensor(nm, shp, F32, kind="ExternalInput").ap()

    x0 = din("x", [1024, 2048])
    xo = nc.dram_tensor("xo", [1024, 2048], F32, kind="ExternalOutput").ap()
    x1 = nc.dram_tensor("x1", [1024, 2048], F32).ap()
    x2 = nc.dram_tensor("x2", [1024, 2048], F32).ap()
    x3 = nc.dram_tensor("x3", [1024, 2048], F32).ap()

    io_a0 = dict(x_in=x0, x_out=x1)
    for w in ("wq", "wk", "wv", "wl", "wo"):
        io_a0[w] = din("a0_" + w, [1024, 1024])
    io_a0["bias_all"] = din("a0_bias", [128, 40])
    io_a0.update(_attn_gather_tensors(nc, "0"))

    io_f0 = dict(x_in=x1, x_out=x2,
                 wg_shard=din("wg_shard", [352, 1024]),
                 wu_shard=din("wu_shard", [352, 1024]),
                 wd_shard=din("wd_shard", [128, 2816]))
    for nm, shp in (("wgq_in", [352, 1024]), ("wgq_g", [2816, 1024]),
                    ("wuq_in", [352, 1024]), ("wuq_g", [2816, 1024]),
                    ("wdq_in", [128, 2816]), ("wdq_g", [1024, 2816]),
                    ("wgq_fm", [1024, 2816]), ("wuq_fm", [1024, 2816]),
                    ("wdq_im", [2816, 1024])):
        io_f0[nm] = nc.dram_tensor(nm, shp, BF16).ap()

    io_a1 = dict(x_in=x2, x_out=x3)
    for nm, shp in (("wdq_w", [1024, 512]), ("wuq_w", [512, 1024]),
                    ("wqr_w", [512, 1024]), ("wkr_w", [1024, 1024]),
                    ("wdkv_w", [1024, 512]), ("wuk_w", [512, 1024]),
                    ("wuv_w", [512, 1024]), ("wo_w", [1024, 1024])):
        io_a1[nm] = din("a1_" + nm, shp)
    io_a1["bias_all"] = din("a1_bias", [128, 56])
    io_a1["cs"] = din("a1_cs", [128, 16384])
    io_a1.update(_attn_gather_tensors(nc, "1"))

    io_f1 = dict(x_in=x3, x_out=xo,
                 wr=din("f1_wr", [1024, 1024]), wk=din("f1_wk", [1024, 1024]),
                 wv=din("f1_wv", [1024, 1024]), mus=din("f1_mus", [128, 16]),
                 msk=din("f1_msk", [128, 64]))
    io_f1["bc_in"] = nc.dram_tensor("bc_in", [128, 8], F32).ap()
    io_f1["bc_g"] = nc.dram_tensor("bc_g", [1024, 8], F32).ap()

    consts = Consts(p)
    od = None
    for emitter, io in ((emit_a0, io_a0), (emit_f0, io_f0),
                        (emit_a1, io_a1), (emit_f1, io_f1)):
        ds = p.scope()
        p._default_scope = ds
        od = emitter(p, consts, io)
        p._default_scope = None
        p.barrier()
        ds.close()
    return p.finalize(final_waits=[(od, od.n)])


def stage_fused(inputs, xs):
    m_a0 = stage_a0(inputs, xs)
    m_f0 = stage_f0(inputs, xs)
    m_a1 = stage_a1(inputs, xs)
    m_f1 = stage_f1(inputs, xs)
    maps = []
    for c in range(NCORES):
        d = {"x": xs[c]}
        for k, v in m_a0[c].items():
            if k in ("wq", "wk", "wv", "wl", "wo"):
                d["a0_" + k] = v
            elif k == "bias_all":
                d["a0_bias"] = v
            elif k.startswith("c_"):
                d[k] = v
        for k, v in m_f0[c].items():
            if k in ("wg_shard", "wu_shard", "wd_shard"):
                d[k] = v
        for k, v in m_a1[c].items():
            if k in ("wdq_w", "wuq_w", "wqr_w", "wkr_w", "wdkv_w",
                     "wuk_w", "wuv_w", "wo_w"):
                d["a1_" + k] = v
            elif k == "bias_all":
                d["a1_bias"] = v
            elif k == "cs":
                d["a1_cs"] = v
        for k, v in m_f1[c].items():
            if k in ("wr", "wk", "wv", "mus", "msk"):
                d["f1_" + k] = v
        maps.append(d)
    return maps


def kernel_fused(**inputs):
    xs = shard_x(inputs["L0_attn_x"])
    res = run_spmd(build_fused_launch(), stage_fused(inputs, xs))
    return unshard_x([np.asarray(r["xo"], np.float32) for r in res])


QMIDS = (0.25, 0.75, 1.25, 1.75, 2.5, 3.5, 5.0)
QSTEPS = (0.5, 0.5, 0.5, 0.5, 1.0, 1.0, 2.0)


def emit_quant_pool(p, src, dst, nblk, qs, done=None, eng="gpsimd",
                    background=False, src_ready=None):
    absx, y, ta, tb, tc, sgn = (qs["absx"], qs["y"], qs["ta"], qs["tb"],
                                qs["tc"], qs["sgn"])
    s, rv = qs["s"], qs["rv"]
    n = 16 * nblk
    P = src.shape[0]
    bg = background
    qvr = p.ctr("qvr")
    qvs = p.ctr("qvs")

    def dv(emit, inc=None):
        p.op("vector", emit, inc=inc, background=bg)

    def pl(emit, inc=None):
        p.op(eng, emit, inc=inc, background=bg)

    if done is not None:
        p.wait("vector", done, done.n, background=bg)
    if src_ready is not None:
        p.wait("vector", src_ready[0], src_ready[1], background=bg)
    dv((lambda o, i: lambda e: e.tensor_single_scalar(
        o, i, 0x7FFFFFFF, ALU.bitwise_and))(
        absx[:P, 0:n].bitcast(I32), src.bitcast(I32)))
    dv((lambda o, i: lambda e: e.tensor_reduce(
        o, i, AX.X, ALU.max))(
        s[:P, 0:nblk], absx[:P, 0:n].rearrange("p (b k) -> p b k", k=16)),
        inc=qvs)
    p.wait("vector", qvs, qvs.n, background=bg)
    dv((lambda o, i: lambda e: e.tensor_scalar(
        o, i, 1.0 / 6.0, 1e-12, ALU.mult, ALU.max))(s[:P, 0:nblk],
                                                    s[:P, 0:nblk]))
    dv((lambda o, i: lambda e: e.tensor_single_scalar(
        o, i, 0.0, ALU.is_gt))(sgn[:P, 0:n], src))
    dv((lambda o, i: lambda e: e.tensor_single_scalar(
        o, i, 0.0, ALU.is_lt))(y[:P, 0:n], src))
    dv((lambda o, a, b2: lambda e: e.tensor_tensor(
        o, a, b2, ALU.subtract))(sgn[:P, 0:n], sgn[:P, 0:n], y[:P, 0:n]),
        inc=qvs)
    dv((lambda o, i: lambda e: e.reciprocal(o, i))(rv[:P, 0:nblk],
                                                   s[:P, 0:nblk]), inc=qvr)
    p.wait("vector", qvr, qvr.n, background=bg)
    p.wait(eng, qvr, qvr.n, background=bg)
    pl((lambda o, i, r: lambda e: e.tensor_tensor(
        o, i, r, ALU.mult))(
        y[:P, 0:n].rearrange("p (b k) -> p b k", k=16),
        absx[:P, 0:n].rearrange("p (b k) -> p b k", k=16),
        rv[:P, 0:nblk].rearrange("p b -> p b ()").broadcast_to([P, nblk, 16])))
    pl((lambda o, i: lambda e: e.tensor_scalar(
        o, i, 2.0, 2.0, ALU.min, ALU.mult))(ta[:P, 0:n], y[:P, 0:n]))
    pl((lambda o, i: lambda e: e.tensor_scalar(
        o, i, MAGIC, MAGIC, ALU.add, ALU.subtract))(ta[:P, 0:n], ta[:P, 0:n]))
    pl((lambda o, i: lambda e: e.tensor_scalar(
        o, i, 2.0, MAGIC, ALU.max, ALU.add))(tb[:P, 0:n], y[:P, 0:n]))
    pl((lambda o, i: lambda e: e.tensor_scalar(
        o, i, MAGIC, 4.0, ALU.subtract, ALU.min))(tb[:P, 0:n], tb[:P, 0:n]))
    pl((lambda o, i: lambda e: e.tensor_scalar(
        o, i, 0.5, MAGIC, ALU.mult, ALU.add))(tc[:P, 0:n], y[:P, 0:n]))
    pl((lambda o, i: lambda e: e.tensor_scalar(
        o, i, MAGIC, 2.0, ALU.subtract, ALU.mult))(tc[:P, 0:n], tc[:P, 0:n]))
    pl((lambda o, i: lambda e: e.tensor_scalar(
        o, i, 4.0, -6.0, ALU.max, ALU.add))(tc[:P, 0:n], tc[:P, 0:n]))
    pl((lambda o, i: lambda e: e.tensor_single_scalar(
        o, i, 0.5, ALU.mult))(ta[:P, 0:n], ta[:P, 0:n]))
    pl((lambda o, a, b2: lambda e: e.tensor_tensor(
        o, a, b2, ALU.add))(ta[:P, 0:n], ta[:P, 0:n], tb[:P, 0:n]))
    pl((lambda o, a, b2: lambda e: e.tensor_tensor(
        o, a, b2, ALU.add))(ta[:P, 0:n], ta[:P, 0:n], tc[:P, 0:n]))
    pl((lambda o, a, sb: lambda e: e.tensor_tensor(
        o, a, sb, ALU.mult))(
        ta[:P, 0:n].rearrange("p (b k) -> p b k", k=16),
        ta[:P, 0:n].rearrange("p (b k) -> p b k", k=16),
        s[:P, 0:nblk].rearrange("p b -> p b ()").broadcast_to([P, nblk, 16])))
    p.wait(eng, qvs, qvs.n, background=bg)
    pl((lambda o, a, sg: lambda e: e.tensor_tensor(
        o, a, sg, ALU.mult))(dst, ta[:P, 0:n], sgn[:P, 0:n]), inc=done)


def emit_f0_wprep(p, io, *, background=False, scope=None, bufs=None):
    ws = bufs[:, 0:3072]
    wqo = bufs[:, 3072:4608].bitcast(BF16)
    qs = quant_scratch_views(bufs[:, 4608:11912], 1408, 88)
    ld = p.ctr("wpld")
    st = p.ctr("wpst")
    for name_in, name_q in (("wg_shard", "wgq_in"), ("wu_shard", "wuq_in")):
        rows = [(0, 128), (128, 128), (256, 96)]
        for i, (r0, nr) in enumerate(rows):
            p.op("sync", (lambda name_in, r0, nr, i: lambda e: e.dma_start(
                out=ws[0:nr, i * 1024:(i + 1) * 1024],
                in_=io[name_in][r0:r0 + nr, :]))(name_in, r0, nr, i),
                inc=ld, dma=True)
        p.barrier()
        qd = p.ctr("wpqd", unique=True)
        for i, (r0, nr) in enumerate(rows):
            emit_quant2(p, ws[0:nr, i * 1024:(i + 1) * 1024],
                        wqo[0:nr, i * 1024:(i + 1) * 1024], 64, qs, done=qd)
        p.barrier()
        for i, (r0, nr) in enumerate(rows):
            p.op("sync", (lambda name_q, r0, nr, i: lambda e: e.dma_start(
                out=io[name_q][r0:r0 + nr, :],
                in_=wqo[0:nr, i * 1024:(i + 1) * 1024]))(name_q, r0, nr, i),
                inc=st, dma=True)
        p.barrier()
    p.op("sync", lambda e: e.dma_start(
        out=ws[:, 0:2816], in_=io["wd_shard"]), inc=ld, dma=True)
    p.barrier()
    qd = p.ctr("wpqd", unique=True)
    emit_quant2(p, ws[:, 0:1408], wqo[:, 0:1408], 88, qs, done=qd)
    emit_quant2(p, ws[:, 1408:2816], wqo[:, 1408:2816], 88, qs, done=qd)
    p.barrier()
    p.op("sync", lambda e: e.dma_start(
        out=io["wdq_in"], in_=wqo[:, 0:2816]), inc=st, dma=True)
    p.barrier()
    grp8 = [[0, 1, 2, 3, 4, 5, 6, 7]]
    ccs = [p.ctr("wpcc", unique=True) for _ in range(3)]
    for i, (a, b2) in enumerate((("wgq_in", "wgq_g"), ("wuq_in", "wuq_g"),
                                 ("wdq_in", "wdq_g"))):
        p.op("gpsimd", (lambda a, b2: lambda e: e.collective_compute(
            "AllGather", ALU.bypass, replica_groups=grp8,
            ins=[io[a].opt()], outs=[io[b2].opt()]))(a, b2),
            inc=(ccs[i], 1))
    return dict(cc0=(ccs[0], 1), cc1=(ccs[1], 1), cc2=(ccs[2], 1))



# version 14
# speedup vs baseline: 1.1280x; 1.0287x over previous
import contextlib
import math
import numpy as np
import ml_dtypes
import concourse.bass as bass
import concourse.mybir as mybir
from concourse.bass_utils import run_bass_kernel_spmd

F32 = mybir.dt.float32
BF16 = mybir.dt.bfloat16
I32 = mybir.dt.int32
AF = mybir.ActivationFunctionType
ALU = mybir.AluOpType
AX = mybir.AxisListType

NCORES = 8
D = 1024
NCH = 8
TC = 2048
TB = 512
NTB = TC // TB
NH = 16
HD = 64
IF = 2816
ICH = IF // 128
MAGIC = float(12582912.0)
ENGINES = ("tensor", "vector", "scalar", "gpsimd", "sync")

bf16 = ml_dtypes.bfloat16


class Ctr:
    def __init__(self, sem):
        self.sem = sem
        self.n = 0


class Prog:
    def __init__(self):
        self.nc = bass.Bass("TRN2", target_bir_lowering=False)
        self.records = {e: [] for e in ENGINES}
        self.stack = contextlib.ExitStack()
        self.pending_waits = {e: [] for e in ENGINES}
        self.phase_ctrs = set()
        self.dirty = {e: False for e in ENGINES}
        self.last_rec = {e: None for e in ENGINES}
        self._semcnt = 0

    def _uname(self, name):
        self._tname = getattr(self, "_tname", 0) + 1
        return f"{name}_{self._tname}"

    def sbuf(self, name, shape, dtype, scope=None):
        cm = self.nc.sbuf_tensor(self._uname(name), shape, dtype)
        tgt = scope or getattr(self, "_default_scope", None) or self.stack
        return tgt.enter_context(cm)

    def psum(self, name, shape, dtype=F32, scope=None):
        cm = self.nc.psum_tensor(self._uname(name), shape, dtype)
        tgt = scope or getattr(self, "_default_scope", None) or self.stack
        return tgt.enter_context(cm)

    def ctr(self, name=None, unique=False):
        if not hasattr(self, "_ctrs"):
            self._ctrs = {}
        if name is not None and not unique and name in self._ctrs:
            return self._ctrs[name]
        self._semcnt += 1
        sem = self.stack.enter_context(
            self.nc.semaphore(f"{name or 's'}{self._semcnt}"))
        c = Ctr(sem)
        if name is not None and not unique:
            self._ctrs[name] = c
        return c

    def scope(self):
        return self.stack.enter_context(contextlib.ExitStack())

    def op(self, engine, emit, inc=None, dma=False, background=False):
        incs = []
        if inc is not None:
            ctr, amt = inc if isinstance(inc, tuple) else (inc, 16 if dma else 1)
            ctr.n += amt
            incs.append((ctr.sem, amt))
            if not background:
                self.phase_ctrs.add(ctr)
        rec = ["op", emit, incs, dma]
        if not background and self.pending_waits[engine]:
            for sem, cnt in self.pending_waits[engine]:
                self.records[engine].append(["wait", sem, cnt])
            self.pending_waits[engine] = []
        self.records[engine].append(rec)
        if not background:
            self.dirty[engine] = True
            self.last_rec[engine] = rec
        return rec

    def wait(self, engine, ctr, count=None, background=False):
        if not background and self.pending_waits[engine]:
            for sem, cnt in self.pending_waits[engine]:
                self.records[engine].append(["wait", sem, cnt])
            self.pending_waits[engine] = []
        self.records[engine].append(
            ["wait", ctr.sem, ctr.n if count is None else count])

    def barrier(self):
        targets = [(c.sem, c.n) for c in self.phase_ctrs]
        bar = None
        for e in ENGINES:
            if not self.dirty[e]:
                continue
            rec = self.last_rec[e]
            if rec is not None and not rec[2]:
                if bar is None:
                    bar = self.ctr("bar")
                amt = 16 if rec[3] else 1
                bar.n += amt
                rec[2].append((bar.sem, amt))
            self.dirty[e] = False
            self.last_rec[e] = None
        if bar is not None:
            targets = [t for t in targets if t[0] is not bar.sem]
            targets.append((bar.sem, bar.n))
        if targets:
            for e in ENGINES:
                self.pending_waits[e] = list(targets)
        self.phase_ctrs = set()

    def finalize(self, final_waits=()):
        block = self.stack.enter_context(self.nc.Block())

        def make(engname):
            def fn(eng):
                for rec in self.records[engname]:
                    if rec[0] == "wait":
                        eng.wait_ge(rec[1], rec[2])
                    else:
                        instr = rec[1](eng)
                        for sem, amt in rec[2]:
                            instr = instr.then_inc(sem, amt)
            return fn

        for ctr, cnt in final_waits:
            self.records["sync"].append(["wait", ctr.sem, cnt])
        block.tensor(make("tensor"))
        block.vector(make("vector"))
        block.scalar(make("scalar"))
        block.gpsimd(make("gpsimd"))
        block.sync(make("sync"))
        self.stack.close()
        return self.nc


CONST_SPEC = {
    "id128": ([128, 128], np.float32),
    "id128b": ([128, 128], np.float32),
    "onescol_b": ([128, 1], np.float32),
    "onesrow_f": ([1, 128], np.float32),
    "onesrow_b": ([1, 128], np.float32),
}


def make_const_arrays():
    return {
        "id128": np.eye(128, dtype=np.float32),
        "id128b": np.eye(128, dtype=np.float32),
        "onescol_b": np.ones((128, 1), np.float32),
        "onesrow_f": np.ones((1, 128), np.float32),
        "onesrow_b": np.ones((1, 128), np.float32),
    }


class Consts:

    def __init__(self, p):
        self.aps = {}
        for name, (shape, _) in CONST_SPEC.items():
            self.aps[name] = p.nc.dram_tensor(
                "c_" + name, shape, F32, kind="ExternalInput").ap()
        self.id128 = p.sbuf("id128", [128, 128], F32)
        self.id128b = p.sbuf("id128b", [128, 128], BF16)
        self.onescol_b = p.sbuf("onescol_b", [128, 1], BF16)
        self.onesrow_f = p.sbuf("onesrow_f", [1, 128], F32)
        self.onesrow_b = p.sbuf("onesrow_b", [1, 128], BF16)
        lds = p.ctr("clds")
        ldg = p.ctr("cldg")
        p.op("sync", lambda e: e.dma_start(out=self.id128[:, :],
                                           in_=self.aps["id128"]), inc=lds, dma=True)
        p.op("gpsimd", lambda e: e.dma_start(out=self.id128b[:, :],
                                             in_=self.aps["id128b"]), inc=ldg, dma=True)
        p.op("gpsimd", lambda e: e.dma_start(out=self.onescol_b[:, :],
                                             in_=self.aps["onescol_b"]), inc=ldg, dma=True)
        p.op("sync", lambda e: e.dma_start(out=self.onesrow_f[:, :],
                                           in_=self.aps["onesrow_f"]), inc=lds, dma=True)
        p.op("gpsimd", lambda e: e.dma_start(out=self.onesrow_b[:, :],
                                             in_=self.aps["onesrow_b"]), inc=ldg, dma=True)


def load_weight_cast(p, dram_ap, sb, cin_chunks, cout, ctr, col0=0):
    src = dram_ap.rearrange("(c p) f -> p c f", p=128)
    dst = sb[:, col0:col0 + cin_chunks * cout].rearrange(
        "p (c f) -> p c f", c=cin_chunks)
    p.op("gpsimd", (lambda dst, src: lambda e: e.dma_start(out=dst, in_=src))(
        dst, src), inc=ctr, dma=True)


def load_vec(p, dram_ap, sb, ctr):
    p.op("sync", (lambda sb, dram_ap: lambda e: e.dma_start(
        out=sb[:, :], in_=dram_ap))(sb, dram_ap), inc=ctr, dma=True)


def emit_proj(p, *, w_sb, cout, cin_chunks, cout_chunks, rhs_sb, rhs_col0,
              out_cb, pp, n_tok=TC, tbs=TB, w_col0=0, tb_outer=False,
              rhs_tok=None):
    if rhs_tok is None:
        rhs_tok = n_tok
    mm = p.ctr("pmm")
    ev = p.ctr("pev")
    b_ev = ev.n
    g = 0
    ntb = n_tok // tbs
    loop = ([(f, tb) for tb in range(ntb) for f in range(cout_chunks)]
            if tb_outer else
            [(f, tb) for f in range(cout_chunks) for tb in range(ntb)])
    for f, tb in loop:
        bank = pp[:, (g % 2) * tbs:(g % 2) * tbs + tbs]
        if g >= 2:
            p.wait("tensor", ev, b_ev + g - 1)
        for ci in range(cin_chunks):
            wsl = w_sb[:, w_col0 + ci * cout + f * 128:
                       w_col0 + ci * cout + f * 128 + 128]
            rsl = rhs_sb[:, rhs_col0 + ci * rhs_tok + tb * tbs:
                         rhs_col0 + ci * rhs_tok + tb * tbs + tbs]
            last = ci == cin_chunks - 1
            p.op("tensor",
                 (lambda bank, wsl, rsl, first, last:
                  lambda e: e.matmul(bank, wsl, rsl, start=first, stop=last)
                  )(bank, wsl, rsl, ci == 0, last),
                 inc=mm if last else None)
        out_cb(p, bank, f, tb * tbs, tbs, mm, ev)
        assert ev.n == b_ev + g + 1, "evac callback must inc ev exactly once"
        g += 1
    return mm, ev


def evac_act(out_sb, out_col0, bias_sb=None, func=AF.Identity, scale=1.0,
             chunk_stride=TC):
    def cb(p, bank, f, t0, nt, mm, ev):
        osl = out_sb[:, out_col0 + f * chunk_stride + t0:
                     out_col0 + f * chunk_stride + t0 + nt]
        p.wait("scalar", mm)
        if bias_sb is not None:
            bsl = bias_sb[:, f:f + 1]
            p.op("scalar", (lambda o, b, bs: lambda e: e.activation(
                o, b, func, bias=bs, scale=scale))(osl, bank, bsl), inc=ev)
        else:
            p.op("scalar", (lambda o, b: lambda e: e.activation(
                o, b, func, scale=scale))(osl, bank), inc=ev)
    return cb


def evac_resid(s_sb, bias_sb, x_sb, x_col0=0, chunk_stride=TC):
    def cb(p, bank, f, t0, nt, mm, ev):
        osl = s_sb[:, f * chunk_stride + t0: f * chunk_stride + t0 + nt]
        xsl = x_sb[:, x_col0 + f * chunk_stride + t0:
                   x_col0 + f * chunk_stride + t0 + nt]
        p.wait("vector", mm)
        if bias_sb is not None:
            bsl = bias_sb[:, f:f + 1]
            p.op("vector", (lambda o, b, bs, x: lambda e: e.scalar_tensor_tensor(
                o, b, bs, x, ALU.add, ALU.add))(osl, bank, bsl, xsl), inc=ev)
        else:
            p.op("vector", (lambda o, b, x: lambda e: e.tensor_tensor(
                o, b, x, ALU.add))(osl, bank, xsl), inc=ev)
    return cb


def emit_ln(p, consts, *, s_sb, out_sb, sc):
    bf_a, bf_b, rows, ps, ps_row = (sc["bf_a"], sc["bf_b"], sc["rows"],
                                    sc["ps"], sc["ps_row"])
    for c in range(NCH):
        sl = s_sb[:, c * TC:(c + 1) * TC]
        p.op("vector", (lambda a, b: lambda e: e.tensor_copy(a, b))(
            bf_a[:, c * TC:(c + 1) * TC], sl))
        p.op("vector", (lambda o, a: lambda e: e.tensor_tensor(
            o, a, a, ALU.mult))(bf_b[:, c * TC:(c + 1) * TC], sl))
    p.barrier()
    for which, src_sb in ((0, bf_a), (1, bf_b)):
        for tb in range(NTB):
            psl = ps_row[0:1, (tb % 4) * TB:(tb % 4) * TB + TB]
            for c in range(NCH):
                p.op("tensor", (lambda psl, a, first, last: lambda e: e.matmul(
                    psl, consts.onescol_b[:, 0:1], a, start=first, stop=last))(
                    psl,
                    src_sb[:, c * TC + tb * TB: c * TC + tb * TB + TB],
                    c == 0, c == NCH - 1))
            p.barrier()
            p.op("vector", (lambda o, i: lambda e: e.tensor_copy(o, i))(
                rows[0:1, which * TC + tb * TB: which * TC + tb * TB + TB], psl))
            p.barrier()
    p.op("vector", lambda e: e.tensor_single_scalar(
        rows[0:1, 0:TC], rows[0:1, 0:TC], 1.0 / D, ALU.mult))
    p.op("vector", lambda e: e.tensor_single_scalar(
        rows[0:1, TC:2 * TC], rows[0:1, TC:2 * TC], 1.0 / D, ALU.mult))
    p.barrier()
    p.op("vector", lambda e: e.tensor_tensor(
        rows[0:1, 2 * TC:3 * TC], rows[0:1, 0:TC], rows[0:1, 0:TC], ALU.mult))
    p.barrier()
    p.op("vector", lambda e: e.scalar_tensor_tensor(
        rows[0:1, 2 * TC:3 * TC], rows[0:1, 2 * TC:3 * TC], -1.0,
        rows[0:1, TC:2 * TC], ALU.mult, ALU.add))
    p.barrier()
    p.op("vector", lambda e: e.tensor_single_scalar(
        rows[0:1, 2 * TC:3 * TC], rows[0:1, 2 * TC:3 * TC], 1e-5, ALU.add))
    p.barrier()
    p.op("scalar", lambda e: e.activation(
        rows[0:1, 2 * TC:3 * TC], rows[0:1, 2 * TC:3 * TC], AF.Ln))
    p.op("scalar", lambda e: e.activation(
        rows[0:1, 2 * TC:3 * TC], rows[0:1, 2 * TC:3 * TC], AF.Exp, scale=-0.5))
    p.barrier()
    p.op("vector", lambda e: e.scalar_tensor_tensor(
        rows[0:1, 0:TC], rows[0:1, 0:TC], -1.0, rows[0:1, 2 * TC:3 * TC],
        ALU.mult, ALU.mult))
    p.barrier()
    for tb in range(NTB):
        pa = ps[:, 0:TB]
        pb = ps[:, TB:2 * TB]
        p.op("tensor", (lambda pa, tb: lambda e: e.matmul(
            pa, consts.onesrow_f[0:1, :],
            rows[0:1, 2 * TC + tb * TB: 2 * TC + tb * TB + TB],
            start=True, stop=True))(pa, tb))
        p.op("tensor", (lambda pb, tb: lambda e: e.matmul(
            pb, consts.onesrow_f[0:1, :],
            rows[0:1, tb * TB: tb * TB + TB],
            start=True, stop=True))(pb, tb))
        p.barrier()
        for c in range(NCH):
            ssl = s_sb[:, c * TC + tb * TB: c * TC + tb * TB + TB]
            osl = out_sb[:, c * TC + tb * TB: c * TC + tb * TB + TB]
            p.op("vector", (lambda o, s, pa: lambda e: e.tensor_tensor(
                o, s, pa, ALU.mult))(osl, ssl, pa))
            p.op("vector", (lambda o, pb: lambda e: e.tensor_tensor(
                o, o, pb, ALU.add))(osl, pb))
        p.barrier()


def emit_ln_fast(p, consts, *, s_sb, out_sb, sc):
    bf_a, bf_b, rows, ps, ps_row = (sc["bf_a"], sc["bf_b"], sc["rows"],
                                    sc["ps"], sc["ps_row"])
    lcp = p.ctr("lcp")
    lmm = p.ctr("lmm")
    lev = p.ctr("lev")
    lst = p.ctr("lst")
    lex = p.ctr("lex")
    lbr = p.ctr("lbr")
    lbc = p.ctr("lbc")
    lap = p.ctr("lap")
    b_lcp, b_lmm, b_lev = lcp.n, lmm.n, lev.n
    b_lbc, b_lap = lbc.n, lap.n
    for c in range(NCH):
        sl = s_sb[:, c * TC:(c + 1) * TC]
        p.op("vector", (lambda a, b: lambda e: e.tensor_copy(a, b))(
            bf_a[:, c * TC:(c + 1) * TC], sl))
        p.op("vector", (lambda o, a: lambda e: e.tensor_tensor(
            o, a, a, ALU.mult))(bf_b[:, c * TC:(c + 1) * TC], sl),
            inc=lcp)
    p.wait("tensor", lcp, b_lcp + NCH)
    groups = [(0, tb) for tb in range(NTB)] + [(1, tb) for tb in range(NTB)]
    for j, (which, tb) in enumerate(groups):
        src_sb = bf_a if which == 0 else bf_b
        psl = ps_row[0:1, (j % 4) * TB:(j % 4) * TB + TB]
        if j >= 4:
            p.wait("tensor", lev, b_lev + j - 3)
        for c in range(NCH):
            p.op("tensor", (lambda psl, a, first, last: lambda e: e.matmul(
                psl, consts.onescol_b[:, 0:1], a, start=first, stop=last))(
                psl,
                src_sb[:, c * TC + tb * TB: c * TC + tb * TB + TB],
                c == 0, c == NCH - 1), inc=lmm if c == NCH - 1 else None)
        p.wait("vector", lmm, b_lmm + j + 1)
        p.op("vector", (lambda o, psl: lambda e: e.tensor_copy(o, psl))(
            rows[0:1, which * TC + tb * TB: which * TC + tb * TB + TB], psl),
            inc=lev)
    p.op("vector", lambda e: e.tensor_single_scalar(
        rows[0:1, 0:TC], rows[0:1, 0:TC], 1.0 / D, ALU.mult))
    p.op("vector", lambda e: e.tensor_single_scalar(
        rows[0:1, TC:2 * TC], rows[0:1, TC:2 * TC], 1.0 / D, ALU.mult))
    p.op("vector", lambda e: e.tensor_tensor(
        rows[0:1, 2 * TC:3 * TC], rows[0:1, 0:TC], rows[0:1, 0:TC], ALU.mult))
    p.op("vector", lambda e: e.scalar_tensor_tensor(
        rows[0:1, 2 * TC:3 * TC], rows[0:1, 2 * TC:3 * TC], -1.0,
        rows[0:1, TC:2 * TC], ALU.mult, ALU.add))
    p.op("vector", lambda e: e.tensor_single_scalar(
        rows[0:1, 2 * TC:3 * TC], rows[0:1, 2 * TC:3 * TC], 1e-5, ALU.add),
        inc=lst)
    p.wait("scalar", lst)
    p.op("scalar", lambda e: e.activation(
        rows[0:1, 2 * TC:3 * TC], rows[0:1, 2 * TC:3 * TC], AF.Ln))
    p.op("scalar", lambda e: e.activation(
        rows[0:1, 2 * TC:3 * TC], rows[0:1, 2 * TC:3 * TC], AF.Exp,
        scale=-0.5), inc=lex)
    p.wait("vector", lex)
    p.op("vector", lambda e: e.scalar_tensor_tensor(
        rows[0:1, 0:TC], rows[0:1, 0:TC], -1.0, rows[0:1, 2 * TC:3 * TC],
        ALU.mult, ALU.mult), inc=lbr)
    p.wait("tensor", lex)
    p.wait("tensor", lbr)
    for tb in range(NTB):
        sl = (tb % 2) * 1024
        pa = ps[:, sl:sl + TB]
        pb = ps[:, sl + TB:sl + 2 * TB]
        if tb >= 2:
            p.wait("tensor", lap, b_lap + (tb - 1) * NCH)
        p.op("tensor", (lambda pa, tb: lambda e: e.matmul(
            pa, consts.onesrow_f[0:1, :],
            rows[0:1, 2 * TC + tb * TB: 2 * TC + tb * TB + TB],
            start=True, stop=True))(pa, tb))
        p.op("tensor", (lambda pb, tb: lambda e: e.matmul(
            pb, consts.onesrow_f[0:1, :],
            rows[0:1, tb * TB: tb * TB + TB],
            start=True, stop=True))(pb, tb), inc=(lbc, 1))
        p.wait("vector", lbc, b_lbc + tb + 1)
        for c in range(NCH):
            ssl = s_sb[:, c * TC + tb * TB: c * TC + tb * TB + TB]
            osl = out_sb[:, c * TC + tb * TB: c * TC + tb * TB + TB]
            p.op("vector", (lambda o, s, pa: lambda e: e.tensor_tensor(
                o, s, pa, ALU.mult))(osl, ssl, pa))
            p.op("vector", (lambda o, pb: lambda e: e.tensor_tensor(
                o, o, pb, ALU.add))(osl, pb), inc=lap)


def emit_absmax_cell(p, consts, src_sb, col0, n_chunks, n_tok, scr, psc, out_cell):
    for c in range(n_chunks):
        p.op("vector", (lambda o, i: lambda e: e.tensor_reduce(
            o, i, AX.X, ALU.max, apply_absolute_value=True))(
            scr[:, c:c + 1], src_sb[:, col0 + c * n_tok: col0 + (c + 1) * n_tok]))
    p.barrier()
    p.op("vector", lambda e: e.tensor_reduce(
        scr[:, n_chunks:n_chunks + 1], scr[:, 0:n_chunks], AX.X, ALU.max))
    p.barrier()
    p.op("tensor", lambda e: e.transpose(
        psc[0:1, 0:128], scr[:, n_chunks:n_chunks + 1], consts.id128[:, :]))
    p.barrier()
    p.op("vector", lambda e: e.tensor_reduce(
        out_cell, psc[0:1, 0:128], AX.X, ALU.max))
    p.barrier()


def emit_flash(p, consts, *, q_sb, kg_ap, vg_ap, negb_sb, scale, attn_sb,
               fl, n_blocks=4, kg_ready=None, vg_ready=None):
    TK = n_blocks * 2048
    NKT = TK // 128
    kbuf, vbuf, pT, rec, otmp = fl["kbuf"], fl["vbuf"], fl["pT"], fl["rec"], fl["otmp"]
    S = (fl["S0"], fl["S1"])
    O = fl["O"]
    kv = [p.ctr("kv0"), p.ctr("kv1")]
    mm = p.ctr("fmm")
    ex = p.ctr("fex")
    pv = p.ctr("fpv")
    rc = p.ctr("frc")
    bc = p.ctr("fbc")
    oc = p.ctr("foc")
    ev = p.ctr("fev")
    b_kv = [kv[0].n, kv[1].n]
    b_mm, b_ex, b_pv = mm.n, ex.n, pv.n
    b_rc, b_bc, b_oc, b_ev = rc.n, bc.n, oc.n, ev.n

    def issue_loads(hp):
        b = hp % 2
        if kg_ready is not None:
            p.wait("sync", kg_ready[0], kg_ready[1] + hp + 1)
        if vg_ready is not None:
            p.wait("sync", vg_ready[0], vg_ready[1] + hp + 1)
        for r in range(n_blocks):
            src = kg_ap[hp][r * 128:(r + 1) * 128, :]
            dst = kbuf[:, b * TK + r * 2048: b * TK + r * 2048 + 2048]
            p.op("sync", (lambda dst, src: lambda e: e.dma_start(
                out=dst, in_=src))(dst, src), inc=kv[b], dma=True)
        vsrc = vg_ap[hp].rearrange("(k p) c -> p k c", p=128)
        vdst = vbuf[:, b * NKT * 130:(b + 1) * NKT * 130].rearrange(
            "p (k c) -> p k c", c=130)
        p.op("sync", (lambda vdst, vsrc: lambda e: e.dma_start(
            out=vdst, in_=vsrc))(vdst, vsrc), inc=kv[b], dma=True)

    issue_loads(0)
    deferred = [None]

    def flush_evac(u):
        o1, o2 = O[2 * (u % 2)], O[2 * (u % 2) + 1]
        ru = rec[0:1, (u % 2) * 1024:(u % 2) * 1024 + 1024]
        hp, qb = u // 4, u % 4
        p.wait("tensor", rc, b_rc + 2 * (u + 1))
        p.op("tensor", (lambda o1, ru: lambda e: e.matmul(
            o1[64:128, :], consts.onesrow_b[0:1, 0:64], ru[0:1, 0:512],
            start=True, stop=True))(o1, ru))
        p.op("tensor", (lambda o2, ru: lambda e: e.matmul(
            o2[64:128, :], consts.onesrow_b[0:1, 0:64], ru[0:1, 512:1024],
            start=True, stop=True))(o2, ru), inc=(bc, 2))
        ot = otmp[:, (u % 2) * 512:(u % 2) * 512 + 512]
        p.wait("scalar", bc, b_bc + 2 * (u + 1))
        p.op("scalar", (lambda ot, o1: lambda e: e.activation(
            ot[0:64, :], o1[0:64, :], AF.Copy))(ot, o1))
        p.op("scalar", (lambda ot, o2: lambda e: e.activation(
            ot[64:128, :], o2[0:64, :], AF.Copy))(ot, o2), inc=(oc, 2))
        a0 = attn_sb[:, hp * TC + qb * 512: hp * TC + qb * 512 + 512]
        p.wait("vector", oc, b_oc + 2 * (u + 1))
        p.op("vector", (lambda a0, ot, o1: lambda e: e.tensor_tensor(
            a0[0:64, :], ot[0:64, :], o1[64:128, :], ALU.mult))(a0, ot, o1))
        p.op("vector", (lambda a0, ot, o2: lambda e: e.tensor_tensor(
            a0[64:128, :], ot[64:128, :], o2[64:128, :], ALU.mult))(a0, ot, o2),
            inc=ev)

    for hp in range(8):
        b = hp % 2
        if hp + 1 < 8:
            p.wait("sync", pv, b_pv + 256 * hp)
            issue_loads(hp + 1)
        for qb in range(4):
            u = hp * 4 + qb
            o1, o2 = O[2 * (u % 2)], O[2 * (u % 2) + 1]
            qsl = q_sb[:, hp * TC + qb * 512: hp * TC + qb * 512 + 512]
            vb = vbuf[:, b * NKT * 130:(b + 1) * NKT * 130]
            kb = kbuf[:, b * TK:(b + 1) * TK]

            def emit_pv(j, o1=o1, o2=o2, u=u, vb=vb):
                gj = u * NKT + j
                ptj = pT[:, (gj % 2) * 1024:(gj % 2) * 1024 + 1024]
                p.wait("tensor", ex, b_ex + gj + 1)
                if j == 0 and u >= 2:
                    p.wait("tensor", ev, b_ev + u - 1)
                p.op("tensor", (lambda o1, vbs, pt, first, last: lambda e:
                     e.matmul(o1[0:65, :], vbs[:, 0:65], pt[:, 0:512],
                              start=first, stop=last))(
                    o1, vb[:, j * 130:j * 130 + 130], ptj, j == 0,
                    j == NKT - 1))
                p.op("tensor", (lambda o2, vbs, pt, first, last: lambda e:
                     e.matmul(o2[0:65, :], vbs[:, 65:130], pt[:, 512:1024],
                              start=first, stop=last))(
                    o2, vb[:, j * 130:j * 130 + 130], ptj, j == 0,
                    j == NKT - 1), inc=pv)

            for kt in range(NKT):
                g = u * NKT + kt
                pr = S[g % 2]
                if deferred[0] is not None and kt == 2:
                    flush_evac(deferred[0])
                    deferred[0] = None
                if kt == 0:
                    p.wait("tensor", kv[b], b_kv[b] + 80 * (hp // 2 + 1))
                if g >= 2:
                    p.wait("tensor", ex, b_ex + g - 1)
                for (tp, prow, pcol, osl) in (
                        ((0, 0), 0, 0, pr[0:64, 0:512]),
                        ((0, 64), 0, 64, pr[64:128, 0:512]),
                        ((64, 0), 64, 0, pr[0:64, 512:1024]),
                        ((64, 64), 64, 64, pr[64:128, 512:1024])):
                    lhs = kb[prow:prow + 64, kt * 128 + pcol: kt * 128 + pcol + 64]
                    rhs = qsl[prow:prow + 64, :]
                    last = tp == (64, 64)
                    p.op("tensor", (lambda o, l, r, tp: lambda e: e.matmul(
                        o, l, r, start=True, stop=True, tile_position=tp))(
                        osl, lhs, rhs, tp), inc=mm if last else None)
                p.wait("scalar", mm, b_mm + g + 1)
                pt = pT[:, (g % 2) * 1024:(g % 2) * 1024 + 1024]
                p.op("scalar", (lambda pt, pr: lambda e: e.activation(
                    pt, pr[:, 0:1024], AF.Exp, bias=negb_sb[:, 0:1],
                    scale=scale))(pt, pr), inc=ex)
                if kt >= 1:
                    emit_pv(kt - 1)
            emit_pv(NKT - 1)
            p.wait("vector", pv, b_pv + (u + 1) * NKT)
            ru = rec[0:1, (u % 2) * 1024:(u % 2) * 1024 + 1024]
            recf = fl["recf"]
            frc2 = p.ctr("frc2")
            p.op("vector", (lambda recf, o1: lambda e: e.reciprocal(
                recf[0:1, 0:512], o1[64:65, :]))(recf, o1))
            p.op("vector", (lambda recf, o2: lambda e: e.reciprocal(
                recf[0:1, 512:1024], o2[64:65, :]))(recf, o2), inc=frc2)
            p.wait("vector", frc2)
            p.op("vector", (lambda ru, recf: lambda e: e.tensor_copy(
                ru[0:1, 0:1024], recf[0:1, 0:1024]))(ru, recf), inc=(rc, 2))
            deferred[0] = u
    flush_evac(deferred[0])
    deferred[0] = None
    p.barrier()


def alloc_flash(p, scope, kbuf_bf16, big_f32):
    return dict(
        kbuf=kbuf_bf16,
        vbuf=big_f32[:, 0:8320].bitcast(BF16),
        pT=big_f32[:, 8320:9344].bitcast(BF16),
        otmp=big_f32[:, 9344:10368],
        rec=big_f32[0:1, 10368:11392].bitcast(BF16),
        recf=big_f32[0:1, 11392:12416],
        S0=p.psum("fS0", [128, 1024], F32, scope),
        S1=p.psum("fS1", [128, 1024], F32, scope),
        O=[p.psum(f"fO{i}", [128, 512], F32, scope) for i in range(4)],
    )


def emit_quant2(p, src, dst, nblk, qs, src_ready=None, done=None):
    y, ta, tb_, tc_, sgn = qs["y"], qs["ta"], qs["tb_"], qs["tc_"], qs["sgn"]
    bm, sv, rv = qs["bm"], qs["sv"], qs["rv"]
    n = 16 * nblk
    P = src.shape[0]
    sg = p.ctr("qsg")
    if src_ready is not None:
        p.wait("scalar", src_ready[0], src_ready[1])
        p.wait("vector", src_ready[0], src_ready[1])
    if done is not None:
        p.wait("scalar", done, done.n)
    p.op("scalar", (lambda o, s: lambda e: e.activation(o, s, AF.Sign))(
        sgn[:P, 0:n], src), inc=sg)
    qrc = p.ctr("qrc")
    p.op("vector", (lambda o, s: lambda e: e.tensor_reduce(
        o, s, AX.X, ALU.max, apply_absolute_value=True))(
        bm[:P, 0:nblk], src.rearrange("p (b k) -> p b k", k=16)), inc=qrc)
    p.wait("vector", qrc)
    p.op("vector", (lambda o, i: lambda e: e.tensor_scalar(
        o, i, 1.0 / 6.0, 1e-12, ALU.mult, ALU.max))(sv[:P, 0:nblk], bm[:P, 0:nblk]))
    p.op("vector", (lambda o, i: lambda e: e.reciprocal(o, i))(
        rv[:P, 0:nblk], sv[:P, 0:nblk]), inc=qrc)
    p.op("vector", (lambda o, i: lambda e: e.tensor_single_scalar(
        o, i, 0x7FFFFFFF, ALU.bitwise_and))(
        y[:P, 0:n].bitcast(I32), src.bitcast(I32)))
    p.wait("vector", qrc)
    p.op("vector", (lambda o, i, r: lambda e: e.tensor_tensor(
        o, i, r, ALU.mult))(
        y[:P, 0:n].rearrange("p (b k) -> p b k", k=16),
        y[:P, 0:n].rearrange("p (b k) -> p b k", k=16),
        rv[:P, 0:nblk].rearrange("p b -> p b ()").broadcast_to([P, nblk, 16])))
    p.op("vector", (lambda o, i: lambda e: e.tensor_scalar(
        o, i, 2.0, 2.0, ALU.min, ALU.mult))(ta[:P, 0:n], y[:P, 0:n]))
    p.op("vector", (lambda o, i: lambda e: e.tensor_scalar(
        o, i, MAGIC, MAGIC, ALU.add, ALU.subtract))(ta[:P, 0:n], ta[:P, 0:n]))
    p.op("vector", (lambda o, i: lambda e: e.tensor_scalar(
        o, i, 2.0, MAGIC, ALU.max, ALU.add))(tb_[:P, 0:n], y[:P, 0:n]))
    p.op("vector", (lambda o, i: lambda e: e.tensor_scalar(
        o, i, MAGIC, 4.0, ALU.subtract, ALU.min))(tb_[:P, 0:n], tb_[:P, 0:n]))
    p.op("vector", (lambda o, i: lambda e: e.tensor_scalar(
        o, i, 0.5, MAGIC, ALU.mult, ALU.add))(tc_[:P, 0:n], y[:P, 0:n]))
    p.op("vector", (lambda o, i: lambda e: e.tensor_scalar(
        o, i, MAGIC, 2.0, ALU.subtract, ALU.mult))(tc_[:P, 0:n], tc_[:P, 0:n]))
    p.op("vector", (lambda o, i: lambda e: e.tensor_scalar(
        o, i, 4.0, -6.0, ALU.max, ALU.add))(tc_[:P, 0:n], tc_[:P, 0:n]))
    p.op("vector", (lambda o, a, b: lambda e: e.scalar_tensor_tensor(
        o, a, 0.5, b, ALU.mult, ALU.add))(ta[:P, 0:n], ta[:P, 0:n], tb_[:P, 0:n]))
    p.op("vector", (lambda o, a, c: lambda e: e.tensor_tensor(
        o, a, c, ALU.add))(ta[:P, 0:n], ta[:P, 0:n], tc_[:P, 0:n]))
    p.op("vector", (lambda o, a, s: lambda e: e.tensor_tensor(
        o, a, s, ALU.mult))(
        ta[:P, 0:n].rearrange("p (b k) -> p b k", k=16),
        ta[:P, 0:n].rearrange("p (b k) -> p b k", k=16),
        sv[:P, 0:nblk].rearrange("p b -> p b ()").broadcast_to([P, nblk, 16])))
    p.wait("vector", sg)
    p.op("vector", (lambda o, a, s: lambda e: e.tensor_tensor(
        o, a, s, ALU.mult))(dst, ta[:P, 0:n], sgn[:P, 0:n]), inc=done)


def alloc_quant_scratch(p, scope, n, nblk):
    return dict(
        y=p.sbuf("q_y", [128, n], F32, scope),
        ta=p.sbuf("q_ta", [128, n], F32, scope),
        tb_=p.sbuf("q_tb", [128, n], F32, scope),
        tc_=p.sbuf("q_tc", [128, n], F32, scope),
        sgn=p.sbuf("q_sgn", [128, n], F32, scope),
        bm=p.sbuf("q_bm", [128, nblk], F32, scope),
        sv=p.sbuf("q_sv", [128, nblk], F32, scope),
        rv=p.sbuf("q_rv", [128, nblk], F32, scope),
    )


def quant_scratch_views(reg, n, nblk):
    d = {}
    o = 0
    for nm in ("y", "ta", "tb_", "tc_", "sgn"):
        d[nm] = reg[:, o:o + n]
        o += n
    for nm in ("bm", "sv", "rv"):
        d[nm] = reg[:, o:o + nblk]
        o += nblk
    return d


def emit_v_stage(p, consts, v_sb, vg_i8, scope):
    psv = [p.psum("vps0", [128, 1024], BF16, scope),
           p.psum("vps1", [128, 1024], BF16, scope)]
    vstage = p.sbuf("vstage", [128, 2 * 1040], BF16, scope)
    tr = p.ctr("vtr")
    cpy = p.ctr("vcp")
    st = [p.ctr("vsta"), p.ctr("vstb")]
    b_tr, b_cpy = tr.n, cpy.n
    b_st = [st[0].n, st[1].n]
    for b in range(2):
        p.op("vector", (lambda b: lambda e: e.memset(
            vstage[:, b * 1040:(b + 1) * 1040].rearrange(
                "p (h c) -> p h c", c=65)[:, :, 64:65], 1.0))(b))
    for tt in range(16):
        b = tt % 2
        ps = psv[b]
        if tt >= 2:
            p.wait("tensor", cpy, b_cpy + tt - 1)
        for c in range(NCH):
            in_ = v_sb[:, c * TC + tt * 128: c * TC + tt * 128 + 128]
            p.op("tensor", (lambda ps, in_, c: lambda e: e.transpose(
                ps[:, c * 128:(c + 1) * 128], in_, consts.id128b[:, :]))(
                ps, in_, c), inc=tr if c == NCH - 1 else None)
        p.wait("vector", tr, b_tr + tt + 1)
        if tt >= 2:
            p.wait("vector", st[b], b_st[b] + 128 * ((tt - 2) // 2 + 1))
        p.op("vector", (lambda b, ps: lambda e: e.tensor_copy(
            vstage[:, b * 1040:(b + 1) * 1040].rearrange(
                "p (h c) -> p h c", c=65)[:, :, 0:64],
            ps[:, :].rearrange("p (h c) -> p h c", c=64)))(b, ps), inc=cpy)
        p.wait("sync", cpy, b_cpy + tt + 1)
        for g in range(8):
            p.op("sync", (lambda b, tt, g: lambda e: e.dma_start(
                out=vg_i8[g][tt * 128:(tt + 1) * 128, :],
                in_=vstage[:, b * 1040 + g * 130:
                           b * 1040 + (g + 1) * 130]))(b, tt, g),
                inc=st[b], dma=True)
    return st


def emit_proj_resid(p, *, w_sb, w_col0, cout, cin_chunks, rhs_sb, bias_sb,
                    x_dram, s_sb, xstr, pp):
    xld = [p.ctr("xlda"), p.ctr("xldb")]
    b_xld = [c.n for c in xld]
    g_of = {}
    g = 0
    for f in range(NCH):
        for tb in range(NTB):
            g_of[(f, tb)] = g
            g += 1
    done_ev = [None, 0]

    def x_dma(f, tb, gg):
        src = x_dram[f * 128:(f + 1) * 128, tb * TB:(tb + 1) * TB]
        dst = xstr[:, (gg % 2) * TB:(gg % 2) * TB + TB]
        if gg >= 2:
            p.wait("sync", done_ev[0], done_ev[1] + gg - 1)
        p.op("sync", (lambda dst, src: lambda e: e.dma_start(
            out=dst, in_=src))(dst, src), inc=xld[gg % 2], dma=True)

    def cb(p_, bank, f, t0, nt, mm, ev):
        gg = g_of[(f, t0 // TB)]
        if done_ev[0] is None:
            done_ev[0] = ev
            done_ev[1] = ev.n
            for g0 in range(2):
                x_dma(*divmod(g0, NTB), g0)
        if gg + 2 < NCH * NTB:
            f2, tb2 = divmod(gg + 2, NTB)
            x_dma(f2, tb2, gg + 2)
        osl = s_sb[:, f * TC + t0: f * TC + t0 + nt]
        xsl = xstr[:, (gg % 2) * TB:(gg % 2) * TB + TB]
        p_.wait("vector", mm)
        p_.wait("vector", xld[gg % 2], b_xld[gg % 2] + 16 * (gg // 2 + 1))
        if bias_sb is not None:
            bsl = bias_sb[:, f:f + 1]
            p_.op("vector", (lambda o, b, bs, x: lambda e: e.scalar_tensor_tensor(
                o, b, bs, x, ALU.add, ALU.add))(osl, bank, bsl, xsl), inc=ev)
        else:
            p_.op("vector", (lambda o, b, x: lambda e: e.tensor_tensor(
                o, b, x, ALU.add))(osl, bank, xsl), inc=ev)

    return emit_proj(p, w_sb=w_sb, w_col0=w_col0, cout=cout,
                     cin_chunks=cin_chunks, cout_chunks=NCH, rhs_sb=rhs_sb,
                     rhs_col0=0, out_cb=cb, pp=pp)


def emit_negb(p, consts, st_out, srow, negb_sb, scale, psB):
    ld = p.ctr("nld")
    p.op("sync", lambda e: e.dma_start(
        out=srow[0:1, 0:16],
        in_=st_out.rearrange("r c -> (r c)").unsqueeze(0)), inc=ld, dma=True)
    p.barrier()
    nrd = p.ctr("nrd")
    p.op("vector", lambda e: e.tensor_reduce(
        srow[0:1, 16:17],
        srow[0:1, 0:16].rearrange("p (r c) -> p c r", c=2)[:, 0:1, :].squeeze(1),
        AX.X, ALU.max))
    p.op("vector", lambda e: e.tensor_reduce(
        srow[0:1, 17:18],
        srow[0:1, 0:16].rearrange("p (r c) -> p c r", c=2)[:, 1:2, :].squeeze(1),
        AX.X, ALU.max), inc=nrd)
    p.wait("vector", nrd)
    p.op("vector", lambda e: e.tensor_tensor(
        srow[0:1, 16:17], srow[0:1, 16:17], srow[0:1, 17:18], ALU.mult))
    p.op("vector", lambda e: e.tensor_single_scalar(
        srow[0:1, 16:17], srow[0:1, 16:17], -64.0 * scale, ALU.mult))
    p.barrier()
    p.op("tensor", lambda e: e.matmul(
        psB[:, 0:1], consts.onesrow_f[0:1, :], srow[0:1, 16:17],
        start=True, stop=True))
    p.barrier()
    p.op("vector", lambda e: e.tensor_copy(negb_sb[:, 0:1], psB[:, 0:1]))
    p.barrier()


def emit_a0(p, consts, io, phases=8.0, pre_flash_cb=None):
    def bail(buf):
        od = p.ctr("od")
        p.op("vector", (lambda buf: lambda e: e.memset(buf, 0.5))(buf))
        p.barrier()
        p.op("sync", (lambda buf: lambda e: e.dma_start(
            out=io["x_out"].rearrange("(c p) t -> p c t", p=128),
            in_=buf.rearrange("p (c t) -> p c t", c=NCH)))(buf), inc=od, dma=True)
        p.barrier()
        return od
    nc = p.nc
    wld = p.ctr("wld")
    vld = p.ctr("vld")
    wbuf = p.sbuf("wbuf", [128, 2 * NCH * D], BF16)
    qsb = p.sbuf("qsb", [128, NCH * TC], BF16)
    buA = p.sbuf("buA", [128, NCH * TC], BF16)
    buB = p.sbuf("buB", [128, NCH * TC], BF16)
    big0 = p.sbuf("big0", [128, NCH * TC], F32)
    xbf = big0[:, 0:NCH * TC // 2].bitcast(BF16)
    s_sb = big0[:, :]
    bias_sb = p.sbuf("bias_sb", [128, 40], F32)
    srow = p.sbuf("srow", [1, 64], F32)
    cells = p.sbuf("cells", [1, 8], F32)
    statscr = p.sbuf("statscr", [128, 16], F32)
    negb_sb = p.sbuf("negb", [128, 1], F32)

    p.op("gpsimd", lambda e: e.dma_start(
        out=xbf.rearrange("p (c t) -> p c t", c=NCH),
        in_=io["x_in"].rearrange("(c p) t -> p c t", p=128)), inc=wld, dma=True)
    load_weight_cast(p, io["wq"], wbuf, NCH, D, wld, col0=0)
    load_weight_cast(p, io["wk"], wbuf, NCH, D, wld, col0=NCH * D)
    load_vec(p, io["bias_all"], bias_sb, vld)
    p.barrier()
    if phases <= 1:
        return bail(s_sb)

    sc_pp = p.scope()
    pp = p.psum("pp0", [128, 1024], F32, sc_pp)
    mm_q, _ = emit_proj(
        p, w_sb=wbuf, w_col0=0, cout=D, cin_chunks=NCH, cout_chunks=NCH,
        rhs_sb=xbf, rhs_col0=0, pp=pp,
        out_cb=evac_act(qsb, 0, bias_sb=bias_sb[:, 0:8]))
    n_mm_q = mm_q.n
    mm_kk, _ = emit_proj(
        p, w_sb=wbuf, w_col0=NCH * D, cout=D, cin_chunks=NCH,
        cout_chunks=NCH, rhs_sb=xbf, rhs_col0=0, pp=pp,
        out_cb=evac_act(buA, 0, bias_sb=bias_sb[:, 8:16]))
    if phases <= 2:
        p.barrier()
        return bail(s_sb)

    p.wait("gpsimd", mm_q, n_mm_q)
    load_weight_cast(p, io["wl"], wbuf, NCH, D, wld, col0=0)
    p.wait("gpsimd", mm_kk)
    load_weight_cast(p, io["wv"], wbuf, NCH, D, wld, col0=NCH * D)
    p.wait("tensor", wld)
    emit_proj(p, w_sb=wbuf, w_col0=0, cout=D, cin_chunks=NCH, cout_chunks=NCH,
              rhs_sb=buA, rhs_col0=0, pp=pp,
              out_cb=evac_act(buB, 0, bias_sb=bias_sb[:, 24:32]))
    p.barrier()
    if phases <= 2.3:
        return bail(s_sb)
    emit_absmax_cell(p, consts, qsb, 0, NCH, TC, statscr, pp, cells[0:1, 0:1])
    emit_absmax_cell(p, consts, buB, 0, NCH, TC, statscr, pp, cells[0:1, 1:2])
    sd = p.ctr("sd")
    p.op("sync", lambda e: e.dma_start(out=io["st_in"], in_=cells[0:1, 0:2]),
         inc=sd, dma=True)
    ccs = p.ctr("ccs", unique=True)
    p.wait("gpsimd", sd, background=True)
    p.op("gpsimd", lambda e: e.collective_compute(
        "AllGather", ALU.bypass, replica_groups=[[0, 1, 2, 3, 4, 5, 6, 7]],
        ins=[io["st_in"].opt()], outs=[io["st_out"].opt()]), inc=(ccs, 1),
        background=True)
    kd = p.ctr("kd")
    p.op("sync", lambda e: e.dma_start(out=io["kg_in"][:, :].rearrange(
        "(c p) t -> p c t", p=128), in_=buB[:, :].rearrange(
        "p (c t) -> p c t", c=NCH)), inc=kd, dma=True)
    cck8 = p.ctr("cck8")
    b_cck = cck8.n
    p.wait("gpsimd", kd, background=True)
    for g in range(8):
        p.op("gpsimd", (lambda g: lambda e: e.collective_compute(
            "AllGather", ALU.bypass,
            replica_groups=[[0, 1, 2, 3], [4, 5, 6, 7]],
            ins=[io["kg_in"][g * 128:(g + 1) * 128, :].opt()],
            outs=[io["kg_o8"][g].opt()]))(g), inc=(cck8, 1), background=True)
    if phases <= 2.6:
        p.barrier()
        return bail(s_sb)
    emit_proj(p, w_sb=wbuf, w_col0=NCH * D, cout=D, cin_chunks=NCH,
              cout_chunks=NCH, rhs_sb=xbf, rhs_col0=0, pp=pp,
              out_cb=evac_act(buA, 0, bias_sb=bias_sb[:, 16:24]))
    p.barrier()
    emit_proj(p, w_sb=wbuf, w_col0=0, cout=D, cin_chunks=NCH, cout_chunks=NCH,
              rhs_sb=buA, rhs_col0=0, pp=pp,
              out_cb=evac_act(buB, 0, bias_sb=bias_sb[:, 24:32]))
    p.barrier()
    if phases <= 2.8:
        return bail(s_sb)
    scv = p.scope()
    vst = emit_v_stage(p, consts, buB, io["vg_i8"], scv)
    p.barrier()
    scv.close()
    ccv8 = p.ctr("ccv8")
    b_ccv = ccv8.n
    for c_ in vst:
        p.wait("gpsimd", c_, background=True)
    for g in range(8):
        p.op("gpsimd", (lambda g: lambda e: e.collective_compute(
            "AllGather", ALU.bypass,
            replica_groups=[[0, 1, 2, 3], [4, 5, 6, 7]],
            ins=[io["vg_i8"][g].opt()], outs=[io["vg_o8"][g].opt()]))(g),
            inc=(ccv8, 1), background=True)
    if phases <= 3:
        return bail(s_sb)

    for e in ENGINES:
        p.wait(e, ccs, 1)
    if phases <= 4:
        return bail(s_sb)

    emit_negb(p, consts, io["st_out"], srow, negb_sb, 0.125, pp)
    load_weight_cast(p, io["wo"], wbuf, NCH, D, wld, col0=0)
    sc_pp.close()
    p.barrier()
    if phases <= 5:
        return bail(s_sb)

    if pre_flash_cb is not None:
        pre_flash_cb()
    scf = p.scope()
    fl = alloc_flash(p, scf, buB[:, :], big0)
    emit_flash(p, consts, q_sb=qsb, kg_ap=io["kg_o8"], vg_ap=io["vg_o8"],
               negb_sb=negb_sb, scale=0.125, attn_sb=buA, fl=fl,
               kg_ready=(cck8, b_cck), vg_ready=(ccv8, b_ccv))
    scf.close()
    if phases <= 6:
        return bail(s_sb)

    scw = p.scope()
    xstr = p.sbuf("xstr", [128, 4 * TB], F32, scw)
    ppw = p.psum("ppw", [128, 1024], F32, scw)
    emit_proj_resid(p, w_sb=wbuf, w_col0=0, cout=D, cin_chunks=NCH,
                    rhs_sb=buA, bias_sb=bias_sb[:, 32:40], x_dram=io["x_in"],
                    s_sb=s_sb, xstr=xstr, pp=ppw)
    p.barrier()
    scw.close()
    if phases <= 7:
        return bail(s_sb)

    scl = p.scope()
    lnsc = dict(
        bf_a=buB[:, :], bf_b=qsb[:, :],
        rows=wbuf[0:1, 0:12288].bitcast(F32),
        ps=p.psum("lnps", [128, 2048], F32, scl),
        ps_row=p.psum("lnpr", [1, 2048], F32, scl),
    )
    emit_ln(p, consts, s_sb=s_sb, out_sb=s_sb, sc=lnsc)
    od = p.ctr("od")
    p.op("sync", lambda e: e.dma_start(
        out=io["x_out"].rearrange("(c p) t -> p c t", p=128),
        in_=s_sb.rearrange("p (c t) -> p c t", c=NCH)), inc=od, dma=True)
    p.barrier()
    scl.close()
    return od


def _attn_gather_tensors(nc, sfx=""):
    return dict(
        kg_in=nc.dram_tensor("kg_in" + sfx, [1024, 2048], BF16).ap(),
        kg_o8=[nc.dram_tensor("kg_o%d" % g + sfx, [512, 2048], BF16).ap()
               for g in range(8)],
        vg_i8=[nc.dram_tensor("vg_i%d" % g + sfx, [2048, 130], BF16).ap()
               for g in range(8)],
        vg_o8=[nc.dram_tensor("vg_o%d" % g + sfx, [8192, 130], BF16).ap()
               for g in range(8)],
        st_in=nc.dram_tensor("st_in" + sfx, [1, 2], F32).ap(),
        st_out=nc.dram_tensor("st_out" + sfx, [8, 2], F32).ap(),
    )


def build_a0_launch(phases=8):
    p = Prog()
    nc = p.nc
    io = {}
    io["x_in"] = nc.dram_tensor("x", [1024, 2048], F32, kind="ExternalInput").ap()
    io["x_out"] = nc.dram_tensor("xo", [1024, 2048], F32,
                                 kind="ExternalOutput").ap()
    for w in ("wq", "wk", "wv", "wl", "wo"):
        io[w] = nc.dram_tensor(w, [1024, 1024], F32, kind="ExternalInput").ap()
    io["bias_all"] = nc.dram_tensor("bias_all", [128, 40], F32,
                                    kind="ExternalInput").ap()
    io.update(_attn_gather_tensors(nc))
    consts = Consts(p)
    od = emit_a0(p, consts, io, phases=phases)
    return p.finalize(final_waits=[(od, od.n)])


def pack_vec(v, n_chunks=NCH):
    v = np.asarray(v, np.float32).reshape(n_chunks, 128)
    return np.ascontiguousarray(v.T)


def shard_x(x):
    out = []
    for c in range(NCORES):
        sl = np.asarray(x[c // 4, (c % 4) * TC:(c % 4 + 1) * TC, :], np.float32)
        out.append(np.ascontiguousarray(sl.T))
    return out


def unshard_x(parts):
    x = np.zeros((2, 4 * TC, D), np.float32)
    for c in range(NCORES):
        x[c // 4, (c % 4) * TC:(c % 4 + 1) * TC, :] = np.asarray(
            parts[c], np.float32).T
    return x


def stage_a0(inputs, xs):
    consts = {("c_" + k): v for k, v in make_const_arrays().items()}
    shared = dict(
        wq=np.ascontiguousarray(inputs["L0_attn_Wq"], np.float32),
        wk=np.ascontiguousarray(inputs["L0_attn_Wk"], np.float32),
        wv=np.ascontiguousarray(inputs["L0_attn_Wv"], np.float32),
        wl=np.ascontiguousarray(inputs["L0_attn_Wl"], np.float32),
        wo=np.ascontiguousarray(inputs["L0_attn_Wo"], np.float32),
        bias_all=np.concatenate(
            [pack_vec(inputs["L0_attn_b" + t]) for t in "qkvlo"], axis=1),
        **consts,
    )
    return [dict(shared, x=xs[c]) for c in range(NCORES)]


def run_spmd(nc, in_maps):
    res = run_bass_kernel_spmd(nc, in_maps, core_ids=list(range(NCORES)))
    return res.results


def kernel(**inputs):
    xs = shard_x(inputs["L0_attn_x"])
    res = run_spmd(build_fused_launch(), stage_fused(inputs, xs))
    return unshard_x([np.asarray(r["xo"], np.float32) for r in res])


NSL = 11
SLC = 2


def emit_f0_staging(p, io, wtmp, *, background=False):
    bg = background
    tl = [p.ctr("wptla"), p.ctr("wptlb")]
    ts_ = [p.ctr("wptsa"), p.ctr("wptsb")]
    b_tl = [tl[0].n, tl[1].n]
    b_ts = [ts_[0].n, ts_[1].n]
    jobs = []
    for fc in range(8):
        for rh in range(4):
            jobs.append(("wgq_g", "wgq_fm", fc, rh * 704, 704))
            jobs.append(("wuq_g", "wuq_fm", fc, rh * 704, 704))
    for ic in range(22):
        jobs.append(("wdq_g", "wdq_im", ic, 0, 704))
        jobs.append(("wdq_g", "wdq_im", ic, 704, 320))
    for j, (a, b2, blk, r0, ncols) in enumerate(jobs):
        par = j % 2
        if j >= 2:
            p.wait("sync", ts_[par], b_ts[par] + 16 * (j // 2), background=bg)
        p.op("sync", (lambda a, blk, r0, ncols, par: lambda e:
             e.dma_start_transpose(
            wtmp[:, par * 704: par * 704 + ncols],
            io[a][r0:r0 + ncols, blk * 128:(blk + 1) * 128]))(
            a, blk, r0, ncols, par), inc=tl[par], dma=True, background=bg)
        p.wait("gpsimd", tl[par], b_tl[par] + 16 * (j // 2 + 1), background=bg)
        p.op("gpsimd", (lambda b2, blk, r0, ncols, par: lambda e: e.dma_start(
            out=io[b2][blk * 128:(blk + 1) * 128, r0:r0 + ncols],
            in_=wtmp[:, par * 704: par * 704 + ncols]))(
            b2, blk, r0, ncols, par), inc=ts_[par], dma=True, background=bg)
    return dict(ts0=(ts_[0], ts_[0].n), ts1=(ts_[1], ts_[1].n))


def emit_f0(p, consts, io, phases=8.0, wprep=None):
    nc = p.nc
    hq_sb = p.sbuf("hq_sb", [128, ICH * TC], BF16)
    wdq_sb = p.sbuf("wdq_sb", [128, ICH * D], BF16)
    bigf = p.sbuf("bigf", [128, 16384], F32)
    xq = bigf[:, 0:8192].bitcast(BF16)
    s_sb = bigf[:, :]

    def bail(buf):
        od = p.ctr("od")
        p.op("vector", (lambda buf: lambda e: e.memset(buf, 0.5))(buf))
        p.barrier()
        p.op("sync", (lambda buf: lambda e: e.dma_start(
            out=io["x_out"].rearrange("(c p) t -> p c t", p=128),
            in_=buf.rearrange("p (c t) -> p c t", c=NCH)))(buf), inc=od, dma=True)
        p.barrier()
        return od

    if wprep is None:
        scw = p.scope()
        wprep = emit_f0_wprep(p, io, background=False, scope=scw,
                              bufs=hq_sb[:, 0:24576].bitcast(F32))
        p.barrier()
        scw.close()
    for e in ENGINES:
        for k in ("cc0", "cc1", "cc2"):
            p.wait(e, wprep[k][0], wprep[k][1])
    p.barrier()
    stg = emit_f0_staging(p, io, hq_sb[:, 20480:21888], background=True)
    if phases <= 1:
        return bail(s_sb)

    scx = p.scope()
    xtb = hq_sb[:, 0:8192].bitcast(F32)
    xt = hq_sb[:, 8192:16384].bitcast(F32)
    xqt = hq_sb[:, 16384:20480]
    qs2 = quant_scratch_views(wdq_sb[:, 0:22528].bitcast(F32), 1024, 64)
    pst = p.psum("pst", [128, 1024], F32, scx)
    pstb = p.psum("pstb", [128, 1024], BF16, scx)
    xld = p.ctr("xld")
    xtr = p.ctr("xtr")
    xcp = p.ctr("xcp")
    xqd = p.ctr("xqd")
    xbt = p.ctr("xbt")
    xhc = p.ctr("xhc")
    for tbb in range(NTB):
        if tbb > 0:
            p.wait("gpsimd", xtr, xtr.n)
        p.op("gpsimd", (lambda tbb: lambda e: e.dma_start(
            out=xtb[:, :].rearrange("p (c t) -> p c t", c=NCH),
            in_=io["x_in"][:, tbb * TB:(tbb + 1) * TB].rearrange(
                "(c p) t -> p c t", p=128)))(tbb), inc=xld, dma=True)
        for tt in range(4):
            p.wait("tensor", xld, xld.n)
            if not (tbb == 0 and tt == 0):
                p.wait("tensor", xcp, xcp.n)
            for c in range(NCH):
                half = (c % 4) * 128
                p.op("tensor", (lambda tt, c, half: lambda e: e.transpose(
                    pst[:, (c // 4) * 512 + half: (c // 4) * 512 + half + 128],
                    xtb[:, c * TB + tt * 128: c * TB + tt * 128 + 128],
                    consts.id128[:, :]))(tt, c, half),
                    inc=xtr if c == NCH - 1 else None)
            p.wait("vector", xtr, xtr.n)
            p.op("vector", (lambda tt: lambda e: e.tensor_copy(
                xt[:, tt * 1024:(tt + 1) * 1024], pst[:, :]))(tt), inc=xcp)
        if tbb > 0:
            p.wait("vector", xbt, xbt.n)
        for tt in range(4):
            emit_quant2(p, xt[:, tt * 1024:(tt + 1) * 1024],
                        xqt[:, tt * 1024:(tt + 1) * 1024], 64, qs2,
                        src_ready=(xcp, xcp.n), done=xqd)
        for tt in range(4):
            p.wait("tensor", xqd, xqd.n)
            if not (tbb == 0 and tt == 0):
                p.wait("tensor", xhc, xhc.n)
            for c in range(NCH):
                p.op("tensor", (lambda tt, c: lambda e: e.transpose(
                    pstb[:, c * 128:(c + 1) * 128],
                    xqt[:, tt * 1024 + c * 128: tt * 1024 + c * 128 + 128],
                    consts.id128b[:, :]))(tt, c),
                    inc=xbt if c == NCH - 1 else None)
            p.wait("vector", xbt, xbt.n)
            p.op("vector", (lambda tbb, tt: lambda e: e.tensor_copy(
                xq.rearrange("p (c t) -> p c t", c=NCH)[
                    :, :, tbb * TB + tt * 128: tbb * TB + tt * 128 + 128],
                pstb[:, :].rearrange("p (c t) -> p c t", c=NCH)))(tbb, tt),
                inc=xhc)
    p.barrier()
    scx.close()
    if phases <= 2:
        return bail(s_sb)

    for e in ENGINES:
        p.wait(e, stg["ts0"][0], stg["ts0"][1])
        p.wait(e, stg["ts1"][0], stg["ts1"][1])
    p.barrier()
    scm = p.scope()
    wsl = wdq_sb[:, 0:4 * NCH * 256]
    hg = bigf[:, 8192:9216]
    hh = hg
    ht = bigf[:, 9216:10240]
    hqt = bigf[:, 10240:10752].bitcast(BF16)
    qs3 = quant_scratch_views(wdq_sb[:, 8192:22528].bitcast(F32), 1024, 64)
    ppg = p.psum("ppg", [128, 1024], F32, scm)
    ppu = p.psum("ppu", [128, 1024], F32, scm)
    pst2 = p.psum("pst2", [128, 512], F32, scm)
    pstb2 = p.psum("pstb2", [128, 512], BF16, scm)
    wld2 = [p.ctr("fwla"), p.ctr("fwlb")]
    b_wld2 = [wld2[0].n, wld2[1].n]
    mmg = p.ctr("mmg")
    b_mmg = mmg.n
    hdone = p.ctr("mhd")
    b_hd = hdone.n
    ftr = p.ctr("mftr")
    fcp = p.ctr("mfcp")
    mqd = p.ctr("mqd")
    mbt = p.ctr("mbt")
    mhc = p.ctr("mhc")

    def back_trans(it):
        isl_, tbb_ = divmod(it, NTB)
        p.wait("tensor", mqd, mqd.n)
        for tt in range(4):
            if not (it == 0 and tt == 0):
                p.wait("tensor", mhc, mhc.n)
            for sc_ in range(SLC):
                p.op("tensor", (lambda tt, sc_: lambda e: e.transpose(
                    pstb2[:, sc_ * 128:(sc_ + 1) * 128],
                    hqt[:, tt * 256 + sc_ * 128: tt * 256 + sc_ * 128 + 128],
                    consts.id128b[:, :]))(tt, sc_),
                    inc=mbt if sc_ == SLC - 1 else None)
            p.wait("vector", mbt, mbt.n)
            for sc_ in range(SLC):
                ic = isl_ * SLC + sc_
                p.op("vector", (lambda ic, tbb_, tt, sc_: lambda e:
                     e.tensor_copy(
                    hq_sb[:, ic * TC + tbb_ * TB + tt * 128:
                          ic * TC + tbb_ * TB + tt * 128 + 128],
                    pstb2[:, sc_ * 128:(sc_ + 1) * 128]))(ic, tbb_, tt, sc_),
                    inc=mhc if sc_ == SLC - 1 else None)

    for isl in range(NSL):
        par = isl % 2
        if isl >= 2:
            p.wait("sync", mmg, b_mmg + 16 * (isl - 1))
        for wi, wfm in enumerate(("wgq_fm", "wuq_fm")):
            dst = wsl[:, (par * 2 + wi) * NCH * 256:
                      (par * 2 + wi + 1) * NCH * 256].rearrange(
                "p (c i) -> p c i", c=NCH)
            srcw = io[wfm].rearrange("(c p) i -> p c i", p=128)[
                :, :, isl * 256:(isl + 1) * 256]
            p.op("sync", (lambda dst, srcw: lambda e: e.dma_start(
                out=dst, in_=srcw))(dst, srcw), inc=wld2[par], dma=True)
        p.wait("tensor", wld2[par], b_wld2[par] + 32 * (isl // 2 + 1))
        for tbb in range(NTB):
            it = isl * NTB + tbb
            if it >= 1:
                p.wait("tensor", hdone, b_hd + it)
            for wi, ppx in ((0, ppg), (1, ppu)):
                for sc_ in range(SLC):
                    bank = ppx[:, sc_ * 512:(sc_ + 1) * 512]
                    for ci in range(NCH):
                        wslc = wsl[:, (par * 2 + wi) * NCH * 256 + ci * 256
                                   + sc_ * 128: (par * 2 + wi) * NCH * 256
                                   + ci * 256 + sc_ * 128 + 128]
                        rsl = xq[:, ci * TC + tbb * TB: ci * TC + tbb * TB + TB]
                        p.op("tensor", (lambda bank, wslc, rsl, first, last:
                                        lambda e: e.matmul(
                            bank, wslc, rsl, start=first, stop=last))(
                            bank, wslc, rsl, ci == 0, ci == NCH - 1),
                            inc=mmg if ci == NCH - 1 else None)
            if it >= 1:
                back_trans(it - 1)
            p.wait("scalar", mmg, b_mmg + 4 * it + 2)
            if it >= 1:
                p.wait("scalar", hdone, b_hd + it)
            msil = p.ctr("msil")
            p.op("scalar", lambda e: e.activation(
                hg[:, 0:1024], ppg[:, 0:1024], AF.Silu), inc=msil)
            p.wait("vector", mmg, b_mmg + 4 * it + 4)
            p.wait("vector", msil, msil.n)
            p.op("vector", lambda e: e.tensor_tensor(
                hh[:, 0:1024], hg[:, 0:1024], ppu[:, 0:1024], ALU.mult),
                inc=hdone)
            for tt in range(4):
                p.wait("tensor", hdone, b_hd + it + 1)
                if not (it == 0 and tt == 0):
                    p.wait("tensor", fcp, fcp.n)
                for sc_ in range(SLC):
                    p.op("tensor", (lambda tt, sc_: lambda e: e.transpose(
                        pst2[:, sc_ * 128:(sc_ + 1) * 128],
                        hh[:, sc_ * 512 + tt * 128:
                           sc_ * 512 + tt * 128 + 128],
                        consts.id128[:, :]))(tt, sc_),
                        inc=ftr if sc_ == SLC - 1 else None)
                p.wait("vector", ftr, ftr.n)
                p.op("vector", (lambda tt: lambda e: e.tensor_copy(
                    ht[:, tt * 256:(tt + 1) * 256], pst2[:, 0:256]))(tt),
                    inc=fcp)
            if it >= 1:
                p.wait("vector", mbt, mbt.n)
            emit_quant2(p, ht[:, :], hqt[:, :], 64, qs3,
                        src_ready=(fcp, fcp.n), done=mqd)
    back_trans(NSL * NTB - 1)
    p.barrier()
    scm.close()
    if phases <= 3:
        return bail(s_sb)

    scd = p.scope()
    dld = p.ctr("dld")
    p.op("gpsimd", lambda e: e.dma_start(
        out=wdq_sb[:, :].rearrange("p (c f) -> p c f", c=ICH),
        in_=io["wdq_im"].rearrange("(c p) f -> p c f", p=128)), inc=dld, dma=True)
    p.barrier()
    xstr = p.sbuf("xstr", [128, 4 * TB], F32, scd)
    ppd = p.psum("ppd", [128, 1024], F32, scd)
    emit_proj_resid(p, w_sb=wdq_sb, w_col0=0, cout=D, cin_chunks=ICH,
                    rhs_sb=hq_sb, bias_sb=None, x_dram=io["x_in"],
                    s_sb=s_sb, xstr=xstr, pp=ppd)
    p.barrier()
    scd.close()
    if phases <= 4:
        return bail(s_sb)

    scl = p.scope()
    lnsc = dict(
        bf_a=hq_sb[:, 0:NCH * TC],
        bf_b=hq_sb[:, NCH * TC:2 * NCH * TC],
        rows=wdq_sb[0:1, 0:12288].bitcast(F32),
        ps=p.psum("lnps", [128, 2048], F32, scl),
        ps_row=p.psum("lnpr", [1, 2048], F32, scl),
    )
    emit_ln(p, consts, s_sb=s_sb, out_sb=s_sb, sc=lnsc)
    od = p.ctr("od")
    p.op("sync", lambda e: e.dma_start(
        out=io["x_out"].rearrange("(c p) t -> p c t", p=128),
        in_=s_sb.rearrange("p (c t) -> p c t", c=NCH)), inc=od, dma=True)
    p.barrier()
    scl.close()
    return od


def build_f0_launch(phases=8.0):
    p = Prog()
    nc = p.nc
    io = {}
    io["x_in"] = nc.dram_tensor("x", [1024, 2048], F32, kind="ExternalInput").ap()
    io["x_out"] = nc.dram_tensor("xo", [1024, 2048], F32,
                                 kind="ExternalOutput").ap()
    io["wg_shard"] = nc.dram_tensor("wg_shard", [352, 1024], F32,
                                    kind="ExternalInput").ap()
    io["wu_shard"] = nc.dram_tensor("wu_shard", [352, 1024], F32,
                                    kind="ExternalInput").ap()
    io["wd_shard"] = nc.dram_tensor("wd_shard", [128, 2816], F32,
                                    kind="ExternalInput").ap()
    for nm, shp in (("wgq_in", [352, 1024]), ("wgq_g", [2816, 1024]),
                    ("wuq_in", [352, 1024]), ("wuq_g", [2816, 1024]),
                    ("wdq_in", [128, 2816]), ("wdq_g", [1024, 2816]),
                    ("wgq_fm", [1024, 2816]), ("wuq_fm", [1024, 2816]),
                    ("wdq_im", [2816, 1024])):
        io[nm] = nc.dram_tensor(nm, shp, BF16).ap()
    consts = Consts(p)
    od = emit_f0(p, consts, io, phases=phases)
    return p.finalize(final_waits=[(od, od.n)])


def stage_f0(inputs, xs):
    consts = {("c_" + k): v for k, v in make_const_arrays().items()}
    wg = np.ascontiguousarray(inputs["L0_ffn_w_gate"], np.float32)
    wu = np.ascontiguousarray(inputs["L0_ffn_w_up"], np.float32)
    wd = np.ascontiguousarray(inputs["L0_ffn_w_down"], np.float32)
    maps = []
    for c in range(NCORES):
        maps.append(dict(
            x=xs[c],
            wg_shard=np.ascontiguousarray(wg[c * 352:(c + 1) * 352]),
            wu_shard=np.ascontiguousarray(wu[c * 352:(c + 1) * 352]),
            wd_shard=np.ascontiguousarray(wd[c * 128:(c + 1) * 128]),
            **consts,
        ))
    return maps


def emit_rope_path(p, consts, *, w_sb, w_col0, cin_chunks, bias_sl, rhs_sb,
                   cs_sb, out_sb, krtb, ktok, rtmp, pp, pst, add_out=True):
    for tb in range(NTB):
        emit_proj(p, w_sb=w_sb, w_col0=w_col0, cout=D, cin_chunks=cin_chunks,
                  cout_chunks=NCH, rhs_sb=rhs_sb, rhs_col0=tb * TB, pp=pp,
                  n_tok=TB, tbs=TB, rhs_tok=TC,
                  out_cb=evac_act(krtb, 0, bias_sb=bias_sl, chunk_stride=TB))
        p.barrier()
        for tt in range(4):
            for c in range(NCH):
                half = (c % 4) * 128
                p.op("tensor", (lambda tt, c, half: lambda e: e.transpose(
                    pst[:, (c // 4) * 512 + half:(c // 4) * 512 + half + 128],
                    krtb[:, c * TB + tt * 128: c * TB + tt * 128 + 128],
                    consts.id128[:, :]))(tt, c, half))
            p.barrier()
            p.op("vector", (lambda tt: lambda e: e.tensor_copy(
                ktok[:, tt * 1024:(tt + 1) * 1024], pst[:, :]))(tt))
            p.barrier()
        for tt in range(4):
            ttg = tb * 4 + tt
            base = tt * 1024
            x0 = ktok[:, base:base + 1024].rearrange(
                "p (i two) -> p i two", two=2)[:, :, 0:1].squeeze(-1)
            x1 = ktok[:, base:base + 1024].rearrange(
                "p (i two) -> p i two", two=2)[:, :, 1:2].squeeze(-1)
            cosv = cs_sb[:, ttg * 512:(ttg + 1) * 512]
            sinv = cs_sb[:, 8192 + ttg * 512: 8192 + (ttg + 1) * 512]
            e0 = rtmp[:, 0:512]
            e1 = rtmp[:, 512:1024]
            p.op("vector", (lambda e0, x0, cosv: lambda e: e.tensor_tensor(
                e0, x0, cosv, ALU.mult))(e0, x0, cosv))
            p.op("vector", (lambda e1, x1, sinv: lambda e: e.tensor_tensor(
                e1, x1, sinv, ALU.mult))(e1, x1, sinv))
            p.op("vector", (lambda e0, e1: lambda e: e.tensor_tensor(
                e0, e0, e1, ALU.subtract))(e0, e1))
            p.op("vector", (lambda e1, x1, cosv: lambda e: e.tensor_tensor(
                e1, x1, cosv, ALU.mult))(e1, x1, cosv))
            p.op("vector", (lambda x1, x0, sinv: lambda e: e.tensor_tensor(
                x1, x0, sinv, ALU.mult))(x1, x0, sinv))
            p.op("vector", (lambda x1, e1: lambda e: e.tensor_tensor(
                x1, e1, x1, ALU.add))(x1, e1))
            p.op("vector", (lambda x0, e0: lambda e: e.tensor_copy(
                x0, e0))(x0, e0))
            p.barrier()
        for tt in range(4):
            for c in range(NCH):
                half = (c % 4) * 128
                p.op("tensor", (lambda tt, c, half: lambda e: e.transpose(
                    pst[:, (c // 4) * 512 + half:(c // 4) * 512 + half + 128],
                    ktok[:, tt * 1024 + c * 128: tt * 1024 + c * 128 + 128],
                    consts.id128[:, :]))(tt, c, half))
            p.barrier()
            osl = out_sb.rearrange("p (c t) -> p c t", c=NCH)[
                :, :, tb * TB + tt * 128: tb * TB + tt * 128 + 128]
            psv = pst[:, :].rearrange("p (c t) -> p c t", c=NCH)
            if add_out:
                p.op("vector", (lambda osl, psv: lambda e: e.tensor_tensor(
                    osl, psv, osl, ALU.add))(osl, psv))
            else:
                p.op("vector", (lambda osl, psv: lambda e: e.tensor_copy(
                    osl, psv))(osl, psv))
            p.barrier()


def emit_a1(p, consts, io, phases=8.0):
    nc = p.nc
    wld = p.ctr("wld")
    vld = p.ctr("vld")
    wbuf = p.sbuf("wbuf1", [128, 2 * NCH * D], BF16)
    qsb = p.sbuf("qsb1", [128, NCH * TC], BF16)
    buA = p.sbuf("buA1", [128, NCH * TC], BF16)
    cs_sb = p.sbuf("cs1", [128, NCH * TC], BF16)
    big0 = p.sbuf("big1", [128, NCH * TC], F32)
    xbf = big0[:, 0:8192].bitcast(BF16)
    valb = xbf
    ctq = big0[:, 8192:12288].bitcast(BF16)
    ctkv = big0[:, 12288:16384].bitcast(BF16)
    s_sb = big0[:, :]
    bias_sb = p.sbuf("bias1", [128, 56], F32)
    srow = p.sbuf("srow1", [1, 64], F32)
    cells = p.sbuf("cells1", [1, 8], F32)
    statscr = p.sbuf("statscr1", [128, 16], F32)
    negb_sb = p.sbuf("negb1", [128, 1], F32)
    SC1 = float(1.0 / np.sqrt(np.float32(16 + 512 + 512)))

    def bail(buf):
        od = p.ctr("od")
        p.op("vector", (lambda buf: lambda e: e.memset(buf, 0.5))(buf))
        p.barrier()
        p.op("sync", (lambda buf: lambda e: e.dma_start(
            out=io["x_out"].rearrange("(c p) t -> p c t", p=128),
            in_=buf.rearrange("p (c t) -> p c t", c=NCH)))(buf), inc=od, dma=True)
        p.barrier()
        return od

    BD = dict(dq=(0, 4), uq=(4, 12), qr=(12, 20), kr=(20, 28), dkv=(28, 32),
              uk=(32, 40), uv=(40, 48), o=(48, 56))

    def bsl(key):
        a, b = BD[key]
        return bias_sb[:, a:b]

    p.op("gpsimd", lambda e: e.dma_start(
        out=xbf.rearrange("p (c t) -> p c t", c=NCH),
        in_=io["x_in"].rearrange("(c p) t -> p c t", p=128)), inc=wld, dma=True)
    p.op("gpsimd", lambda e: e.dma_start(
        out=cs_sb[:, :], in_=io["cs"]), inc=wld, dma=True)
    load_weight_cast(p, io["wdq_w"], wbuf, NCH, 512, wld, col0=0)
    load_weight_cast(p, io["wdkv_w"], wbuf, NCH, 512, wld, col0=4096)
    load_weight_cast(p, io["wuk_w"], wbuf, 4, D, wld, col0=8192)
    load_vec(p, io["bias_all"], bias_sb, vld)
    p.barrier()

    sc_pp = p.scope()
    pp = p.psum("pp1", [128, 1024], F32, sc_pp)
    pst = p.psum("pstA", [128, 1024], F32, sc_pp)
    emit_proj(p, w_sb=wbuf, w_col0=0, cout=512, cin_chunks=NCH, cout_chunks=4,
              rhs_sb=xbf, rhs_col0=0, pp=pp,
              out_cb=evac_act(ctq, 0, bias_sb=bsl("dq")))
    p.barrier()
    emit_proj(p, w_sb=wbuf, w_col0=4096, cout=512, cin_chunks=NCH,
              cout_chunks=4, rhs_sb=xbf, rhs_col0=0, pp=pp,
              out_cb=evac_act(ctkv, 0, bias_sb=bsl("dkv")))
    p.barrier()
    if phases <= 1:
        sc_pp.close()
        return bail(s_sb)

    emit_proj(p, w_sb=wbuf, w_col0=8192, cout=D, cin_chunks=4, cout_chunks=NCH,
              rhs_sb=ctkv, rhs_col0=0, pp=pp,
              out_cb=evac_act(buA, 0, bias_sb=bsl("uk")))
    load_weight_cast(p, io["wkr_w"], wbuf, NCH, D, wld, col0=0)
    p.barrier()
    krtb = qsb[:, 0:8192].bitcast(F32)
    ktok = qsb[:, 8192:16384].bitcast(F32)
    rtmp = p.sbuf("rtmp1", [128, 1024], F32)
    emit_rope_path(p, consts, w_sb=wbuf, w_col0=0, cin_chunks=NCH,
                   bias_sl=bsl("kr"), rhs_sb=xbf, cs_sb=cs_sb, out_sb=buA,
                   krtb=krtb, ktok=ktok, rtmp=rtmp, pp=pp, pst=pst)
    if phases <= 2:
        sc_pp.close()
        return bail(s_sb)

    emit_absmax_cell(p, consts, buA, 0, NCH, TC, statscr, pp, cells[0:1, 1:2])
    kd = p.ctr("kd")
    p.op("sync", lambda e: e.dma_start(out=io["kg_in"][:, :].rearrange(
        "(c p) t -> p c t", p=128), in_=buA[:, :].rearrange(
        "p (c t) -> p c t", c=NCH)), inc=kd, dma=True)
    cck8 = p.ctr("cck8")
    b_cck = cck8.n
    p.wait("gpsimd", kd, background=True)
    for g in range(8):
        p.op("gpsimd", (lambda g: lambda e: e.collective_compute(
            "AllGather", ALU.bypass,
            replica_groups=[[0, 1, 2, 3], [4, 5, 6, 7]],
            ins=[io["kg_in"][g * 128:(g + 1) * 128, :].opt()],
            outs=[io["kg_o8"][g].opt()]))(g), inc=(cck8, 1), background=True)
    load_weight_cast(p, io["wuv_w"], wbuf, 4, D, wld, col0=8192)
    p.barrier()
    emit_proj(p, w_sb=wbuf, w_col0=8192, cout=D, cin_chunks=4, cout_chunks=NCH,
              rhs_sb=ctkv, rhs_col0=0, pp=pp,
              out_cb=evac_act(valb, 0, bias_sb=bsl("uv")))
    p.barrier()
    scv = p.scope()
    vst = emit_v_stage(p, consts, valb, io["vg_i8"], scv)
    p.barrier()
    scv.close()
    ccv8 = p.ctr("ccv8")
    b_ccv = ccv8.n
    for c_ in vst:
        p.wait("gpsimd", c_, background=True)
    for g in range(8):
        p.op("gpsimd", (lambda g: lambda e: e.collective_compute(
            "AllGather", ALU.bypass,
            replica_groups=[[0, 1, 2, 3], [4, 5, 6, 7]],
            ins=[io["vg_i8"][g].opt()], outs=[io["vg_o8"][g].opt()]))(g),
            inc=(ccv8, 1), background=True)
    if phases <= 3:
        sc_pp.close()
        return bail(s_sb)

    load_weight_cast(p, io["wuq_w"], wbuf, 4, D, wld, col0=0)
    load_weight_cast(p, io["wqr_w"], wbuf, 4, D, wld, col0=4096)
    p.barrier()
    emit_proj(p, w_sb=wbuf, w_col0=0, cout=D, cin_chunks=4, cout_chunks=NCH,
              rhs_sb=ctq, rhs_col0=0, pp=pp,
              out_cb=evac_act(qsb, 0, bias_sb=bsl("uq")))
    p.barrier()
    krtb2 = valb[0:128, 0:8192].bitcast(F32)
    ktok2 = valb[0:128, 8192:16384].bitcast(F32)
    emit_rope_path(p, consts, w_sb=wbuf, w_col0=4096, cin_chunks=4,
                   bias_sl=bsl("qr"), rhs_sb=ctq, cs_sb=cs_sb, out_sb=qsb,
                   krtb=krtb2, ktok=ktok2, rtmp=rtmp, pp=pp, pst=pst)
    if phases <= 4:
        sc_pp.close()
        return bail(s_sb)

    emit_absmax_cell(p, consts, qsb, 0, NCH, TC, statscr, pp, cells[0:1, 0:1])
    sd = p.ctr("sd")
    p.op("sync", lambda e: e.dma_start(out=io["st_in"], in_=cells[0:1, 0:2]),
         inc=sd, dma=True)
    ccs = p.ctr("ccs1", unique=True)
    p.wait("gpsimd", sd, background=True)
    p.op("gpsimd", lambda e: e.collective_compute(
        "AllGather", ALU.bypass, replica_groups=[[0, 1, 2, 3, 4, 5, 6, 7]],
        ins=[io["st_in"].opt()], outs=[io["st_out"].opt()]), inc=(ccs, 1),
        background=True)
    for e in ENGINES:
        p.wait(e, ccs, 1)
    emit_negb(p, consts, io["st_out"], srow, negb_sb, SC1, pp)
    load_weight_cast(p, io["wo_w"], wbuf, NCH, D, wld, col0=0)
    sc_pp.close()
    p.barrier()
    if phases <= 5:
        return bail(s_sb)

    scf = p.scope()
    fl = alloc_flash(p, scf, cs_sb[:, :], big0)
    emit_flash(p, consts, q_sb=qsb, kg_ap=io["kg_o8"], vg_ap=io["vg_o8"],
               negb_sb=negb_sb, scale=SC1, attn_sb=buA, fl=fl,
               kg_ready=(cck8, b_cck), vg_ready=(ccv8, b_ccv))
    scf.close()
    if phases <= 6:
        return bail(s_sb)

    scw = p.scope()
    xstr = p.sbuf("xstr1", [128, 4 * TB], F32, scw)
    ppw = p.psum("ppw1", [128, 1024], F32, scw)
    emit_proj_resid(p, w_sb=wbuf, w_col0=0, cout=D, cin_chunks=NCH,
                    rhs_sb=buA, bias_sb=bsl("o"), x_dram=io["x_in"],
                    s_sb=s_sb, xstr=xstr, pp=ppw)
    p.barrier()
    scw.close()

    scl = p.scope()
    lnsc = dict(
        bf_a=buA[:, :], bf_b=qsb[:, :],
        rows=wbuf[0:1, 0:12288].bitcast(F32),
        ps=p.psum("lnps1", [128, 2048], F32, scl),
        ps_row=p.psum("lnpr1", [1, 2048], F32, scl),
    )
    emit_ln(p, consts, s_sb=s_sb, out_sb=s_sb, sc=lnsc)
    od = p.ctr("od")
    p.op("sync", lambda e: e.dma_start(
        out=io["x_out"].rearrange("(c p) t -> p c t", p=128),
        in_=s_sb.rearrange("p (c t) -> p c t", c=NCH)), inc=od, dma=True)
    p.barrier()
    scl.close()
    return od


def build_a1_launch(phases=8.0):
    p = Prog()
    nc = p.nc
    io = {}
    io["x_in"] = nc.dram_tensor("x", [1024, 2048], F32, kind="ExternalInput").ap()
    io["x_out"] = nc.dram_tensor("xo", [1024, 2048], F32,
                                 kind="ExternalOutput").ap()
    for nm, shp in (("wdq_w", [1024, 512]), ("wuq_w", [512, 1024]),
                    ("wqr_w", [512, 1024]), ("wkr_w", [1024, 1024]),
                    ("wdkv_w", [1024, 512]), ("wuk_w", [512, 1024]),
                    ("wuv_w", [512, 1024]), ("wo_w", [1024, 1024]),
                    ("bias_all", [128, 56]), ("cs", [128, 16384])):
        io[nm] = nc.dram_tensor(nm, shp, F32, kind="ExternalInput").ap()
    io.update(_attn_gather_tensors(nc, "1"))
    consts = Consts(p)
    od = emit_a1(p, consts, io, phases=phases)
    return p.finalize(final_waits=[(od, od.n)])


def host_cos_sin():
    ids = np.arange(512, dtype=np.float32)
    theta = (np.float32(10000.0) ** (np.float32(-2.0) * ids)) / np.float32(512)
    out = []
    for c in range(NCORES):
        t0 = (c % 4) * TC
        pos = np.arange(t0, t0 + TC, dtype=np.float32)[:, None]
        r = (pos * theta[None, :]).astype(np.float32)
        cos = np.cos(r).astype(np.float32)
        sin = np.sin(r).astype(np.float32)
        arr = np.zeros((128, 16384), np.float32)
        for tt in range(16):
            arr[:, tt * 512:(tt + 1) * 512] = cos[tt * 128:(tt + 1) * 128]
            arr[:, 8192 + tt * 512: 8192 + (tt + 1) * 512] = \
                sin[tt * 128:(tt + 1) * 128]
        out.append(arr)
    return out


def stage_a1(inputs, xs):
    consts = {("c_" + k): v for k, v in make_const_arrays().items()}
    cs = host_cos_sin()
    bias = np.concatenate([
        pack_vec(inputs["L1_attn_Wd_q_b"], 4),
        pack_vec(inputs["L1_attn_Wu_q_b"], 8),
        pack_vec(inputs["L1_attn_Wq_r_b"], 8),
        pack_vec(inputs["L1_attn_Wk_r_b"], 8),
        pack_vec(inputs["L1_attn_Wd_kv_b"], 4),
        pack_vec(inputs["L1_attn_Wu_k_b"], 8),
        pack_vec(inputs["L1_attn_Wu_v_b"], 8),
        pack_vec(inputs["L1_attn_Wo_b"], 8),
    ], axis=1)
    shared = dict(
        wdq_w=np.ascontiguousarray(inputs["L1_attn_Wd_q_w"], np.float32),
        wuq_w=np.ascontiguousarray(inputs["L1_attn_Wu_q_w"], np.float32),
        wqr_w=np.ascontiguousarray(inputs["L1_attn_Wq_r_w"], np.float32),
        wkr_w=np.ascontiguousarray(inputs["L1_attn_Wk_r_w"], np.float32),
        wdkv_w=np.ascontiguousarray(inputs["L1_attn_Wd_kv_w"], np.float32),
        wuk_w=np.ascontiguousarray(inputs["L1_attn_Wu_k_w"], np.float32),
        wuv_w=np.ascontiguousarray(inputs["L1_attn_Wu_v_w"], np.float32),
        wo_w=np.ascontiguousarray(inputs["L1_attn_Wo_w"], np.float32),
        bias_all=bias,
        **consts,
    )
    return [dict(shared, x=xs[c], cs=cs[c]) for c in range(NCORES)]


def emit_f1(p, consts, io, phases=8.0):
    nc = p.nc
    wld = p.ctr("wld")
    vld = p.ctr("vld")
    wbuf = p.sbuf("wbufF", [128, 3 * NCH * D], BF16)
    s_sb = p.sbuf("bigF", [128, NCH * TC], F32)
    lnb = p.sbuf("lnbF", [128, NCH * TC], BF16)
    x_tb = p.sbuf("x_tbF", [128, NCH * 513], F32)
    xr_tb = p.sbuf("xrF", [128, NCH * TB], BF16)
    xk_tb = p.sbuf("xkF", [128, NCH * TB], BF16)
    r_tb = p.sbuf("rF", [128, NCH * TB], F32)
    ka_tb = p.sbuf("kaF", [128, NCH * TB], BF16)
    t1 = p.sbuf("t1F", [128, TB], F32)
    mus = p.sbuf("musF", [128, 32], F32)
    msk = p.sbuf("mskF", [128, 64], F32)
    g64 = p.sbuf("g64F", [128, 64], F32)
    bcol = p.sbuf("bcolF", [128, 8], F32)

    def bail(buf):
        od = p.ctr("od")
        p.op("vector", (lambda buf: lambda e: e.memset(buf, 0.5))(buf))
        p.barrier()
        p.op("sync", (lambda buf: lambda e: e.dma_start(
            out=io["x_out"].rearrange("(c p) t -> p c t", p=128),
            in_=buf.rearrange("p (c t) -> p c t", c=NCH)))(buf), inc=od, dma=True)
        p.barrier()
        return od

    load_weight_cast(p, io["wr"], wbuf, NCH, D, wld, col0=0)
    load_weight_cast(p, io["wk"], wbuf, NCH, D, wld, col0=8192)
    load_weight_cast(p, io["wv"], wbuf, NCH, D, wld, col0=16384)
    p.op("sync", lambda e: e.dma_start(out=mus[:, 0:16], in_=io["mus"]),
         inc=vld, dma=True)
    p.op("sync", lambda e: e.dma_start(out=msk[:, :], in_=io["msk"]),
         inc=vld, dma=True)
    p.op("sync", lambda e: e.dma_start(
        out=bcol[:, :].rearrange("p c -> p c ()"),
        in_=io["x_in"].rearrange("(c p) t -> p c t", p=128)[:, :, 2047:2048],
        allow_slow_non_contiguous=True), inc=vld, dma=True)
    p.barrier()
    cc = p.ctr("ccF", unique=True)
    p.op("sync", lambda e: e.dma_start(out=io["bc_in"], in_=bcol[:, :]),
         inc=vld, dma=True)
    p.barrier()
    p.op("gpsimd", lambda e: e.collective_compute(
        "AllGather", ALU.bypass, replica_groups=[[0, 1, 2, 3, 4, 5, 6, 7]],
        ins=[io["bc_in"].opt()], outs=[io["bc_g"].opt()]), inc=(cc, 1))
    for e in ENGINES:
        p.wait(e, cc, 1)
    p.op("sync", lambda e: e.dma_start(
        out=g64[:, :].rearrange("p (r c) -> p r c", r=8),
        in_=io["bc_g"].rearrange("(r p) c -> p r c", p=128)), inc=vld, dma=True)
    p.barrier()
    p.op("vector", lambda e: e.tensor_tensor(
        g64[:, :].rearrange("p (r c) -> p c r", r=8),
        g64[:, :].rearrange("p (r c) -> p c r", r=8),
        msk[:, :].rearrange("p (r c) -> p c r", r=8), ALU.mult))
    p.barrier()
    p.op("vector", lambda e: e.tensor_reduce(
        bcol[:, :], g64[:, :].rearrange("p (r c) -> p c r", r=8), AX.X, ALU.add))
    p.op("vector", lambda e: e.tensor_scalar(
        mus[:, 16:32], mus[:, 0:16], -1.0, 1.0, ALU.mult, ALU.add))
    p.barrier()
    if phases <= 1:
        return bail(s_sb)

    sc_pp = p.scope()
    pp = p.psum("ppF", [128, 1024], F32, sc_pp)
    xldF = p.ctr("xldF")
    for tbb in range(NTB):
        if tbb == 0:
            p.op("sync", lambda e: e.dma_start(
                out=x_tb[:, :].rearrange("p (c t) -> p c t", c=NCH)[:, :, 1:513],
                in_=io["x_in"].rearrange("(c p) t -> p c t", p=128)[
                    :, :, 0:512]), inc=xldF, dma=True)
            p.barrier()
            p.op("vector", lambda e: e.tensor_copy(
                x_tb[:, :].rearrange("p (c t) -> p c t", c=NCH)[:, :, 0:1],
                bcol[:, :].rearrange("p c -> p c ()")))
        else:
            p.op("sync", (lambda tbb: lambda e: e.dma_start(
                out=x_tb[:, :].rearrange("p (c t) -> p c t", c=NCH),
                in_=io["x_in"].rearrange("(c p) t -> p c t", p=128)[
                    :, :, tbb * TB - 1: tbb * TB + 512]))(tbb), inc=xldF,
                dma=True)
        p.barrier()
        for which, xo, mu0, om0 in ((0, xr_tb, 0, 16), (1, xk_tb, 8, 24)):
            for c in range(NCH):
                xc = x_tb[:, c * 513 + 1: c * 513 + 513]
                xp = x_tb[:, c * 513: c * 513 + 512]
                p.op("vector", (lambda xc, mu: lambda e: e.tensor_scalar(
                    t1[:, :], xc, mu, None, ALU.mult, ...))(
                    xc, None) if False else
                    (lambda xc, c, mu0: lambda e: e.tensor_single_scalar(
                        t1[:, :], xc, mus[:, mu0 + c: mu0 + c + 1], ALU.mult))(
                        xc, c, mu0))
                p.op("vector", (lambda xo, xp, c, om0: lambda e:
                     e.scalar_tensor_tensor(
                    xo[:, c * TB:(c + 1) * TB], xp,
                    mus[:, om0 + c: om0 + c + 1], t1[:, :],
                    ALU.mult, ALU.add))(xo, xp, c, om0))
        p.barrier()
        emit_proj(p, w_sb=wbuf, w_col0=0, cout=D, cin_chunks=NCH,
                  cout_chunks=NCH, rhs_sb=xr_tb, rhs_col0=0, pp=pp,
                  n_tok=TB, tbs=TB, rhs_tok=TB,
                  out_cb=evac_act(r_tb, 0, func=AF.Sigmoid, chunk_stride=TB))
        p.barrier()

        def cb_k(p_, bank, f, t0, nt, mm, ev):
            p_.wait("vector", mm)
            p_.op("vector", (lambda bank: lambda e: e.tensor_scalar_max(
                t1[:, :], bank, 0.0))(bank))
            p_.op("vector", (lambda f: lambda e: e.tensor_tensor(
                ka_tb[:, f * TB: f * TB + TB], t1[:, :], t1[:, :],
                ALU.mult))(f), inc=ev)
        emit_proj(p, w_sb=wbuf, w_col0=8192, cout=D, cin_chunks=NCH,
                  cout_chunks=NCH, rhs_sb=xk_tb, rhs_col0=0, pp=pp,
                  n_tok=TB, tbs=TB, rhs_tok=TB, out_cb=cb_k)
        p.barrier()

        def cb_v(p_, bank, f, t0, nt, mm, ev):
            p_.wait("vector", mm)
            rsl = r_tb[:, f * TB: f * TB + nt]
            ssl = s_sb[:, f * TC + tbb * TB: f * TC + tbb * TB + nt]
            xsl = x_tb[:, f * 513 + 1: f * 513 + 513]
            p_.op("vector", (lambda ssl, bank, rsl: lambda e: e.tensor_tensor(
                ssl, bank, rsl, ALU.mult))(ssl, bank, rsl))
            p_.op("vector", (lambda ssl, xsl: lambda e: e.tensor_tensor(
                ssl, ssl, xsl, ALU.add))(ssl, xsl), inc=ev)
        emit_proj(p, w_sb=wbuf, w_col0=16384, cout=D, cin_chunks=NCH,
                  cout_chunks=NCH, rhs_sb=ka_tb, rhs_col0=0, pp=pp,
                  n_tok=TB, tbs=TB, rhs_tok=TB, out_cb=cb_v)
        p.barrier()
    sc_pp.close()
    if phases <= 2:
        return bail(s_sb)

    scl = p.scope()
    lnsc = dict(
        bf_a=wbuf[:, 8192:24576], bf_b=lnb[:, :],
        rows=wbuf[0:1, 0:12288].bitcast(F32),
        ps=p.psum("lnpsF", [128, 2048], F32, scl),
        ps_row=p.psum("lnprF", [1, 2048], F32, scl),
    )
    emit_ln(p, consts, s_sb=s_sb, out_sb=s_sb, sc=lnsc)
    od = p.ctr("od")
    p.op("sync", lambda e: e.dma_start(
        out=io["x_out"].rearrange("(c p) t -> p c t", p=128),
        in_=s_sb.rearrange("p (c t) -> p c t", c=NCH)), inc=od, dma=True)
    p.barrier()
    scl.close()
    return od


def build_f1_launch(phases=8.0):
    p = Prog()
    nc = p.nc
    io = {}
    io["x_in"] = nc.dram_tensor("x", [1024, 2048], F32, kind="ExternalInput").ap()
    io["x_out"] = nc.dram_tensor("xo", [1024, 2048], F32,
                                 kind="ExternalOutput").ap()
    for nm, shp in (("wr", [1024, 1024]), ("wk", [1024, 1024]),
                    ("wv", [1024, 1024]), ("mus", [128, 16]),
                    ("msk", [128, 64])):
        io[nm] = nc.dram_tensor(nm, shp, F32, kind="ExternalInput").ap()
    io["bc_in"] = nc.dram_tensor("bc_in", [128, 8], F32).ap()
    io["bc_g"] = nc.dram_tensor("bc_g", [1024, 8], F32).ap()
    consts = Consts(p)
    od = emit_f1(p, consts, io, phases=phases)
    return p.finalize(final_waits=[(od, od.n)])


def stage_f1(inputs, xs):
    consts = {("c_" + k): v for k, v in make_const_arrays().items()}
    mus = np.concatenate([pack_vec(inputs["L1_ffn_mu_r"], 8),
                          pack_vec(inputs["L1_ffn_mu_k"], 8)], axis=1)
    shared = dict(
        wr=np.ascontiguousarray(np.asarray(inputs["L1_ffn_W_r"],
                                           np.float32).T),
        wk=np.ascontiguousarray(np.asarray(inputs["L1_ffn_W_k"],
                                           np.float32).T),
        wv=np.ascontiguousarray(np.asarray(inputs["L1_ffn_W_v"],
                                           np.float32).T),
        mus=mus,
        **consts,
    )
    maps = []
    for c in range(NCORES):
        m = np.zeros((128, 8, 8), np.float32)
        if c % 4 != 0:
            m[:, c - 1, :] = 1.0
        maps.append(dict(shared, x=xs[c], msk=m.reshape(128, 64)))
    return maps


def build_fused_launch():
    p = Prog()
    nc = p.nc

    def din(nm, shp):
        return nc.dram_tensor(nm, shp, F32, kind="ExternalInput").ap()

    x0 = din("x", [1024, 2048])
    xo = nc.dram_tensor("xo", [1024, 2048], F32, kind="ExternalOutput").ap()
    x1 = nc.dram_tensor("x1", [1024, 2048], F32).ap()
    x2 = nc.dram_tensor("x2", [1024, 2048], F32).ap()
    x3 = nc.dram_tensor("x3", [1024, 2048], F32).ap()

    io_a0 = dict(x_in=x0, x_out=x1)
    for w in ("wq", "wk", "wv", "wl", "wo"):
        io_a0[w] = din("a0_" + w, [1024, 1024])
    io_a0["bias_all"] = din("a0_bias", [128, 40])
    io_a0.update(_attn_gather_tensors(nc, "0"))

    io_f0 = dict(x_in=x1, x_out=x2,
                 wg_shard=din("wg_shard", [352, 1024]),
                 wu_shard=din("wu_shard", [352, 1024]),
                 wd_shard=din("wd_shard", [128, 2816]))
    for nm, shp in (("wgq_in", [352, 1024]), ("wgq_g", [2816, 1024]),
                    ("wuq_in", [352, 1024]), ("wuq_g", [2816, 1024]),
                    ("wdq_in", [128, 2816]), ("wdq_g", [1024, 2816]),
                    ("wgq_fm", [1024, 2816]), ("wuq_fm", [1024, 2816]),
                    ("wdq_im", [2816, 1024])):
        io_f0[nm] = nc.dram_tensor(nm, shp, BF16).ap()

    io_a1 = dict(x_in=x2, x_out=x3)
    for nm, shp in (("wdq_w", [1024, 512]), ("wuq_w", [512, 1024]),
                    ("wqr_w", [512, 1024]), ("wkr_w", [1024, 1024]),
                    ("wdkv_w", [1024, 512]), ("wuk_w", [512, 1024]),
                    ("wuv_w", [512, 1024]), ("wo_w", [1024, 1024])):
        io_a1[nm] = din("a1_" + nm, shp)
    io_a1["bias_all"] = din("a1_bias", [128, 56])
    io_a1["cs"] = din("a1_cs", [128, 16384])
    io_a1.update(_attn_gather_tensors(nc, "1"))

    io_f1 = dict(x_in=x3, x_out=xo,
                 wr=din("f1_wr", [1024, 1024]), wk=din("f1_wk", [1024, 1024]),
                 wv=din("f1_wv", [1024, 1024]), mus=din("f1_mus", [128, 16]),
                 msk=din("f1_msk", [128, 64]))
    io_f1["bc_in"] = nc.dram_tensor("bc_in", [128, 8], F32).ap()
    io_f1["bc_g"] = nc.dram_tensor("bc_g", [1024, 8], F32).ap()

    consts = Consts(p)
    od = None
    for emitter, io in ((emit_a0, io_a0), (emit_f0, io_f0),
                        (emit_a1, io_a1), (emit_f1, io_f1)):
        ds = p.scope()
        p._default_scope = ds
        od = emitter(p, consts, io)
        p._default_scope = None
        p.barrier()
        ds.close()
    return p.finalize(final_waits=[(od, od.n)])


def stage_fused(inputs, xs):
    m_a0 = stage_a0(inputs, xs)
    m_f0 = stage_f0(inputs, xs)
    m_a1 = stage_a1(inputs, xs)
    m_f1 = stage_f1(inputs, xs)
    maps = []
    for c in range(NCORES):
        d = {"x": xs[c]}
        for k, v in m_a0[c].items():
            if k in ("wq", "wk", "wv", "wl", "wo"):
                d["a0_" + k] = v
            elif k == "bias_all":
                d["a0_bias"] = v
            elif k.startswith("c_"):
                d[k] = v
        for k, v in m_f0[c].items():
            if k in ("wg_shard", "wu_shard", "wd_shard"):
                d[k] = v
        for k, v in m_a1[c].items():
            if k in ("wdq_w", "wuq_w", "wqr_w", "wkr_w", "wdkv_w",
                     "wuk_w", "wuv_w", "wo_w"):
                d["a1_" + k] = v
            elif k == "bias_all":
                d["a1_bias"] = v
            elif k == "cs":
                d["a1_cs"] = v
        for k, v in m_f1[c].items():
            if k in ("wr", "wk", "wv", "mus", "msk"):
                d["f1_" + k] = v
        maps.append(d)
    return maps


def kernel_fused(**inputs):
    xs = shard_x(inputs["L0_attn_x"])
    res = run_spmd(build_fused_launch(), stage_fused(inputs, xs))
    return unshard_x([np.asarray(r["xo"], np.float32) for r in res])


QMIDS = (0.25, 0.75, 1.25, 1.75, 2.5, 3.5, 5.0)
QSTEPS = (0.5, 0.5, 0.5, 0.5, 1.0, 1.0, 2.0)


def emit_quant_pool(p, src, dst, nblk, qs, done=None, eng="gpsimd",
                    background=False, src_ready=None):
    absx, y, ta, tb, tc, sgn = (qs["absx"], qs["y"], qs["ta"], qs["tb"],
                                qs["tc"], qs["sgn"])
    s, rv = qs["s"], qs["rv"]
    n = 16 * nblk
    P = src.shape[0]
    bg = background
    qvr = p.ctr("qvr")
    qvs = p.ctr("qvs")

    def dv(emit, inc=None):
        p.op("vector", emit, inc=inc, background=bg)

    def pl(emit, inc=None):
        p.op(eng, emit, inc=inc, background=bg)

    if done is not None:
        p.wait("vector", done, done.n, background=bg)
    if src_ready is not None:
        p.wait("vector", src_ready[0], src_ready[1], background=bg)
    dv((lambda o, i: lambda e: e.tensor_single_scalar(
        o, i, 0x7FFFFFFF, ALU.bitwise_and))(
        absx[:P, 0:n].bitcast(I32), src.bitcast(I32)))
    dv((lambda o, i: lambda e: e.tensor_reduce(
        o, i, AX.X, ALU.max))(
        s[:P, 0:nblk], absx[:P, 0:n].rearrange("p (b k) -> p b k", k=16)),
        inc=qvs)
    p.wait("vector", qvs, qvs.n, background=bg)
    dv((lambda o, i: lambda e: e.tensor_scalar(
        o, i, 1.0 / 6.0, 1e-12, ALU.mult, ALU.max))(s[:P, 0:nblk],
                                                    s[:P, 0:nblk]))
    dv((lambda o, i: lambda e: e.tensor_single_scalar(
        o, i, 0.0, ALU.is_gt))(sgn[:P, 0:n], src))
    dv((lambda o, i: lambda e: e.tensor_single_scalar(
        o, i, 0.0, ALU.is_lt))(y[:P, 0:n], src))
    dv((lambda o, a, b2: lambda e: e.tensor_tensor(
        o, a, b2, ALU.subtract))(sgn[:P, 0:n], sgn[:P, 0:n], y[:P, 0:n]),
        inc=qvs)
    dv((lambda o, i: lambda e: e.reciprocal(o, i))(rv[:P, 0:nblk],
                                                   s[:P, 0:nblk]), inc=qvr)
    p.wait("vector", qvr, qvr.n, background=bg)
    p.wait(eng, qvr, qvr.n, background=bg)
    pl((lambda o, i, r: lambda e: e.tensor_tensor(
        o, i, r, ALU.mult))(
        y[:P, 0:n].rearrange("p (b k) -> p b k", k=16),
        absx[:P, 0:n].rearrange("p (b k) -> p b k", k=16),
        rv[:P, 0:nblk].rearrange("p b -> p b ()").broadcast_to([P, nblk, 16])))
    pl((lambda o, i: lambda e: e.tensor_scalar(
        o, i, 2.0, 2.0, ALU.min, ALU.mult))(ta[:P, 0:n], y[:P, 0:n]))
    pl((lambda o, i: lambda e: e.tensor_scalar(
        o, i, MAGIC, MAGIC, ALU.add, ALU.subtract))(ta[:P, 0:n], ta[:P, 0:n]))
    pl((lambda o, i: lambda e: e.tensor_scalar(
        o, i, 2.0, MAGIC, ALU.max, ALU.add))(tb[:P, 0:n], y[:P, 0:n]))
    pl((lambda o, i: lambda e: e.tensor_scalar(
        o, i, MAGIC, 4.0, ALU.subtract, ALU.min))(tb[:P, 0:n], tb[:P, 0:n]))
    pl((lambda o, i: lambda e: e.tensor_scalar(
        o, i, 0.5, MAGIC, ALU.mult, ALU.add))(tc[:P, 0:n], y[:P, 0:n]))
    pl((lambda o, i: lambda e: e.tensor_scalar(
        o, i, MAGIC, 2.0, ALU.subtract, ALU.mult))(tc[:P, 0:n], tc[:P, 0:n]))
    pl((lambda o, i: lambda e: e.tensor_scalar(
        o, i, 4.0, -6.0, ALU.max, ALU.add))(tc[:P, 0:n], tc[:P, 0:n]))
    pl((lambda o, i: lambda e: e.tensor_single_scalar(
        o, i, 0.5, ALU.mult))(ta[:P, 0:n], ta[:P, 0:n]))
    pl((lambda o, a, b2: lambda e: e.tensor_tensor(
        o, a, b2, ALU.add))(ta[:P, 0:n], ta[:P, 0:n], tb[:P, 0:n]))
    pl((lambda o, a, b2: lambda e: e.tensor_tensor(
        o, a, b2, ALU.add))(ta[:P, 0:n], ta[:P, 0:n], tc[:P, 0:n]))
    pl((lambda o, a, sb: lambda e: e.tensor_tensor(
        o, a, sb, ALU.mult))(
        ta[:P, 0:n].rearrange("p (b k) -> p b k", k=16),
        ta[:P, 0:n].rearrange("p (b k) -> p b k", k=16),
        s[:P, 0:nblk].rearrange("p b -> p b ()").broadcast_to([P, nblk, 16])))
    p.wait(eng, qvs, qvs.n, background=bg)
    pl((lambda o, a, sg: lambda e: e.tensor_tensor(
        o, a, sg, ALU.mult))(dst, ta[:P, 0:n], sgn[:P, 0:n]), inc=done)


def emit_f0_wprep(p, io, *, background=False, scope=None, bufs=None):
    ws = bufs[:, 0:3072]
    wqo = bufs[:, 3072:4608].bitcast(BF16)
    qs = quant_scratch_views(bufs[:, 4608:11912], 1408, 88)
    ld = p.ctr("wpld")
    st = p.ctr("wpst")
    for name_in, name_q in (("wg_shard", "wgq_in"), ("wu_shard", "wuq_in")):
        rows = [(0, 128), (128, 128), (256, 96)]
        for i, (r0, nr) in enumerate(rows):
            p.op("sync", (lambda name_in, r0, nr, i: lambda e: e.dma_start(
                out=ws[0:nr, i * 1024:(i + 1) * 1024],
                in_=io[name_in][r0:r0 + nr, :]))(name_in, r0, nr, i),
                inc=ld, dma=True)
        p.barrier()
        qd = p.ctr("wpqd", unique=True)
        for i, (r0, nr) in enumerate(rows):
            emit_quant2(p, ws[0:nr, i * 1024:(i + 1) * 1024],
                        wqo[0:nr, i * 1024:(i + 1) * 1024], 64, qs, done=qd)
        p.barrier()
        for i, (r0, nr) in enumerate(rows):
            p.op("sync", (lambda name_q, r0, nr, i: lambda e: e.dma_start(
                out=io[name_q][r0:r0 + nr, :],
                in_=wqo[0:nr, i * 1024:(i + 1) * 1024]))(name_q, r0, nr, i),
                inc=st, dma=True)
        p.barrier()
    p.op("sync", lambda e: e.dma_start(
        out=ws[:, 0:2816], in_=io["wd_shard"]), inc=ld, dma=True)
    p.barrier()
    qd = p.ctr("wpqd", unique=True)
    emit_quant2(p, ws[:, 0:1408], wqo[:, 0:1408], 88, qs, done=qd)
    emit_quant2(p, ws[:, 1408:2816], wqo[:, 1408:2816], 88, qs, done=qd)
    p.barrier()
    p.op("sync", lambda e: e.dma_start(
        out=io["wdq_in"], in_=wqo[:, 0:2816]), inc=st, dma=True)
    p.barrier()
    grp8 = [[0, 1, 2, 3, 4, 5, 6, 7]]
    ccs = [p.ctr("wpcc", unique=True) for _ in range(3)]
    for i, (a, b2) in enumerate((("wgq_in", "wgq_g"), ("wuq_in", "wuq_g"),
                                 ("wdq_in", "wdq_g"))):
        p.op("gpsimd", (lambda a, b2: lambda e: e.collective_compute(
            "AllGather", ALU.bypass, replica_groups=grp8,
            ins=[io[a].opt()], outs=[io[b2].opt()]))(a, b2),
            inc=(ccs[i], 1))
    return dict(cc0=(ccs[0], 1), cc1=(ccs[1], 1), cc2=(ccs[2], 1))

